# version 16 (final)
# baseline (speedup 1.0000x reference)
import contextlib
import numpy as np
import ml_dtypes
import concourse.bass as bass
import concourse.mybir as mybir
from concourse.bass_utils import run_bass_kernel_spmd

BF = ml_dtypes.bfloat16
F32 = mybir.dt.float32
BF16 = mybir.dt.bfloat16
AF = mybir.ActivationFunctionType
OP = mybir.AluOpType

NCORE = 8
Bb, S, D = 2, 8192, 4096
TOK = 2048
HALO = 2560
NCH = 32
H, DH = 32, 128
SCALE = float(1.0 / np.sqrt(128.0))
NW = 9
NQS = 4
EPS = 1e-5
MASKVAL = -30000.0


class Sem:
    def __init__(self, h, step):
        self.h = h
        self.c = 0
        self.step = step

    def now(self):
        return (self, self.c)


class Prog:
    ENG = ("sync", "gpsimd", "tensor", "scalar", "vector")

    def __init__(self):
        self.ops = {k: [] for k in self.ENG}

    def emit(self, eng, build=None, waits=(), inc=None):
        if inc is not None:
            inc.c += inc.step
        self.ops[eng].append((build, tuple(waits), inc))

    def replay(self, eng_name, e):
        auto_drain = eng_name in ("vector", "scalar")
        prev_real = False
        for build, waits, inc in self.ops[eng_name]:
            for s, v in waits:
                if v > 0:
                    e.wait_ge(s.h, v)
            if build is not None and auto_drain and prev_real:
                e.drain()
            ins = build(e) if build is not None else None
            if build is not None:
                prev_real = True
            if inc is not None:
                if ins is None:
                    ins = e.drain()
                ins.then_inc(inc.h, inc.step)


class DmaQ:

    def __init__(self, prog, eng, nc, es, name, n=8):
        self.p = prog
        self.eng = eng
        self.sems = [Sem(es.enter_context(nc.semaphore(f"{name}{i}")), 16)
                     for i in range(n)]
        self.i = 0

    def dma(self, build, waits=()):
        s = self.sems[self.i]
        self.i = (self.i + 1) % len(self.sems)
        w = list(waits) + ([(s, s.c)] if s.c else [])
        self.p.emit(self.eng, build, waits=w, inc=s)
        return (s, s.c)

    def all(self):
        return [(s, s.c) for s in self.sems if s.c]


def build_nc(dbg=False):
    nc = bass.Bass("TRN2", target_bir_lowering=False, dynamic_dma_scratch_size=16384)

    def din(name, shape, dt):
        return nc.dram_tensor(name, list(shape), dt, kind="ExternalInput")

    def dscr(name, shape, dt):
        return nc.dram_tensor(name, list(shape), dt,
                              kind="ExternalOutput" if dbg else "Internal")

    xin = din("xin", (NCH, 128, HALO), F32)
    wq = din("wq", (NCH, 128, D), F32)
    wo = din("wo", (NCH, 128, D), F32)
    wb = din("wb", (NCH, 128, D + 128), F32)
    wk = din("wk", (128, D), F32)
    wv = din("wv", (128, D), F32)
    wg9 = din("wg9", (128, NCH * 33), F32)
    acat = din("acat", (128, D), F32)
    bbase = din("bbase", (128, 32), F32)
    bthr = din("bthr", (1, 1), F32)
    ones_in = din("ones_in", (128, 128), BF16)
    e16_in = din("e16_in", (8, 128), BF16)
    cos_in = din("cos_in", (128, HALO), BF16)
    ss_in = din("ss_in", (128, HALO), BF16)
    alA_in = din("alA_in", (128, 256), BF16)
    alB_in = din("alB_in", (128, 256), BF16)
    alC_in = din("alC_in", (128, 256), BF16)
    alD_in = din("alD_in", (128, 256), BF16)
    mask_in = din("mask_in", (128, 20), F32)

    s_xn = dscr("s_xn", (NCH, 128, HALO), BF16)
    s_q = dscr("s_q", (H, 128, TOK), BF16)
    s_at = dscr("s_at", (NCH, 128, TOK), BF16)
    s_y1 = dscr("s_y1", (NCH, 128, TOK), BF16)
    s_ff = dscr("s_ff", (NCH, 128, TOK), BF16)
    s_xb = dscr("s_xb", (NCH, 128, TOK), BF16)
    y2 = nc.dram_tensor("y2", [NCH, 128, TOK], F32, kind="ExternalOutput")

    p = Prog()
    E = p.emit

    with contextlib.ExitStack() as es:
        def sb(name, shape, dt):
            return es.enter_context(nc.sbuf_tensor(name, list(shape), dt))

        BIG = sb("BIG", (128, 65536), BF16)
        wcol = [sb(f"wcol{i}", (128, D + 128), BF16) for i in range(2)]
        ones_sb = sb("ones_sb", (128, 128), BF16)
        e16_sb = sb("e16_sb", (8, 128), BF16)
        cos_sb = sb("cos_sb", (128, HALO), BF16)
        ss_sb = sb("ss_sb", (128, HALO), BF16)
        alA = sb("alA", (128, 256), BF16)
        alB = sb("alB", (128, 256), BF16)
        alC = sb("alC", (128, 256), BF16)
        alD = sb("alD", (128, 256), BF16)
        mask_sb = sb("mask_sb", (128, 20), F32)
        bbase_sb = sb("bbase_sb", (128, 32), F32)
        bthr_sb = sb("bthr_sb", (1, 1), F32)
        bthr2 = sb("bthr2", (33, 1), F32)
        eps_sb = sb("eps_sb", (128, 1), F32)
        wg9_sb = sb("wg9_sb", (128, NCH * 33), BF16)
        acat_sb = sb("acat_sb", (128, D), BF16)
        Kr = sb("Kr", (128, HALO), BF16)
        Vt = sb("Vt", (128, HALO), BF16)
        qh = sb("qh", (128, TOK), BF16)
        estage = [sb(f"estage{i}", (128, 512), BF16) for i in range(2)]
        P_sb = [sb(f"P{i}", (128, 2048), BF16) for i in range(2)]
        obuf = [sb(f"obuf{i}", (128, 512), BF16) for i in range(2)]
        T1 = sb("T1", (128, 512), F32)
        T2 = sb("T2", (128, 512), F32)
        T3 = sb("T3", (128, 512), F32)
        rsA = sb("rsA", (128, 512), BF16)
        nmA = sb("nmA", (128, 512), BF16)
        sqr = [sb(f"sqr{i}", (128, 512), BF16) for i in range(2)]
        ar = [sb(f"ar{i}", (128, 512), BF16) for i in range(2)]
        xr = [sb(f"xr{i}", (128, 512), BF16) for i in range(2)]
        cos_f = cos_sb.ap().bitcast(F32)
        ss_f = ss_sb.ap().bitcast(F32)
        kr_f = Kr.ap().bitcast(F32)
        T2b = T2.ap().bitcast(BF16)
        gvA = cos_f[:, 0:512]
        gvB = cos_f[:, 512:1024]
        gvC = ss_f[:, 0:512]
        WN = ss_sb[0:8, 2048:2560]
        ebf = P_sb[0][0:8, 512:1024]
        tbf = P_sb[0][0:1, 1024:1536]
        webf = Vt[0:8, 0:512]
        wful = P_sb[0][:, 0:512]
        hw_all = P_sb[1][:]
        ystage = [kr_f[:, 0:512], kr_f[:, 512:1024]]
        rsB = Vt[:, 2048:2560]
        nmB = cos_sb[:, 2048:2560]

        psA = es.enter_context(nc.psum_tensor("psA", [128, 2048], F32))
        psB = es.enter_context(nc.psum_tensor("psB", [128, 512], F32))
        psC = es.enter_context(nc.psum_tensor("psC", [128, 512], F32))
        psD = es.enter_context(nc.psum_tensor("psD", [128, 512], F32))
        psE = es.enter_context(nc.psum_tensor("psE", [128, 512], F32))

        sPE = Sem(es.enter_context(nc.semaphore("sPE")), 1)
        sAC = Sem(es.enter_context(nc.semaphore("sAC")), 1)
        sDV = Sem(es.enter_context(nc.semaphore("sDV")), 1)
        syq = DmaQ(p, "sync", nc, es, "sy", 8)
        gpq = DmaQ(p, "gpsimd", nc, es, "gp", 8)

        state = {"psbc_ac": 0, "psde_ac": 0, "stats_g": [], "g_ln2": [0, 0]}
        ar_g = [[], []]
        xr_g = [[], []]
        sq_g = [[], []]
        ys_g = [[], []]

        for dst, src in ((ones_sb, ones_in), (e16_sb, e16_in), (cos_sb, cos_in),
                         (ss_sb, ss_in), (alA, alA_in), (alB, alB_in),
                         (alC, alC_in), (alD, alD_in), (mask_sb, mask_in),
                         (bbase_sb, bbase), (bthr_sb, bthr)):
            syq.dma(lambda e, d=dst, s=src: e.dma_start(out=d[:], in_=s.ap()))
        consts_sy = syq.all()
        gpq.dma(lambda e: e.dma_start(out=wg9_sb[:], in_=wg9.ap()))
        wk_sb = BIG[:, 49152:53248]
        wv_sb = BIG[:, 53248:57344]
        gpq.dma(lambda e: e.dma_start(out=wk_sb, in_=wk.ap()))
        gpq.dma(lambda e: e.dma_start(out=wv_sb, in_=wv.ap()))
        consts_gp = gpq.all()
        E("scalar", lambda e: e.mul(bthr2[0:1, :], bthr_sb[:], 0.5),
          waits=consts_sy, inc=sAC)
        E("vector", lambda e: e.tensor_copy(bthr2[32:33, :], bthr2[0:1, :]),
          waits=[sAC.now()], inc=sDV)
        E("vector", lambda e: e.memset(eps_sb[:], EPS), inc=sDV)

        def stats_finalize(rstd_bf, nmu_bf, mm_wait, pb=None, pc=None, gk2=None):
            pb = psB if pb is None else pb
            pc = psC if pc is None else pc
            E("scalar", lambda e: e.mul(T1[:], pb[:], -1.0 / D),
              waits=[mm_wait] + state["stats_g"], inc=sAC)
            E("scalar", lambda e: e.activation(T2[:], pb[:], AF.Square, scale=1.0 / D), inc=sAC)
            E("scalar", lambda e: e.mul(T3[:], pc[:], 1.0 / D), inc=sAC)
            if gk2 is not None:
                state["g_ln2"][gk2] = sAC.c
            else:
                state["psbc_ac" if pb is psB else "psde_ac"] = sAC.c
            aw = sAC.now()
            E("vector", lambda e: e.tensor_sub(T3[:], T3[:], T2[:]), waits=[aw], inc=sDV)
            vw = sDV.now()
            E("scalar", lambda e: e.activation(T2[:], T3[:], AF.Sqrt, bias=eps_sb.ap()),
              waits=[vw], inc=sAC)
            aw2 = sAC.now()
            E("vector", lambda e: e.reciprocal(T3[:], T2[:]), waits=[aw2])
            E("vector", lambda e, r=rstd_bf: e.tensor_copy(r[:], T3[:]))
            E("vector", lambda e, m=nmu_bf: e.tensor_tensor(m[:], T1[:], T3[:], op=OP.mult), inc=sDV)
            state["stats_g"] = [sDV.now()]
            return sDV.now()

        def stats_finalize32(rstd_bf, nmu_bf, pbrow, pcrow, waits0):
            E("scalar", lambda e: e.mul(T1[0:32, :], pbrow, -1.0 / D),
              waits=list(waits0) + state["stats_g"], inc=sAC)
            E("scalar", lambda e: e.activation(T2[0:32, :], pbrow, AF.Square, scale=1.0 / D), inc=sAC)
            E("scalar", lambda e: e.mul(T3[0:32, :], pcrow, 1.0 / D), inc=sAC)
            state["fin_read"] = sAC.c
            aw = sAC.now()
            E("vector", lambda e: e.tensor_sub(T3[0:32, :], T3[0:32, :], T2[0:32, :]),
              waits=[aw], inc=sDV)
            vw = sDV.now()
            E("scalar", lambda e: e.activation(T2[0:32, :], T3[0:32, :], AF.Sqrt,
                                               bias=eps_sb[0:32, :]), waits=[vw], inc=sAC)
            aw2 = sAC.now()
            E("vector", lambda e: e.reciprocal(T3[0:32, :], T2[0:32, :]), waits=[aw2])
            E("vector", lambda e, r=rstd_bf: e.tensor_copy(r[0:32, :], T3[0:32, :]))
            E("vector", lambda e, m=nmu_bf: e.tensor_tensor(m[0:32, :], T1[0:32, :],
                                                            T3[0:32, :], op=OP.mult))
            for b in (32, 64, 96):
                E("vector", lambda e, r=rstd_bf, b=b: e.tensor_copy(r[b:b + 32, :], r[0:32, :]))
                E("vector", lambda e, m=nmu_bf, b=b: e.tensor_copy(m[b:b + 32, :], m[0:32, :]))
            E("vector", None, inc=sDV)
            state["stats_g"] = [sDV.now()]
            return sDV.now()

        def rope(dst_slice, src_tile, col0, tmp):
            E("vector", lambda e: e.tensor_copy(tmp[0:64, :], src_tile[64:128, :]))
            E("vector", lambda e: e.tensor_copy(tmp[64:128, :], src_tile[0:64, :]))
            E("vector", lambda e: e.tensor_tensor(tmp[:], tmp[:], ss_sb[:, col0:col0 + 512], op=OP.mult))
            E("vector", lambda e: e.tensor_tensor(dst_slice, src_tile[:], cos_sb[:, col0:col0 + 512], op=OP.mult))
            E("vector", lambda e: e.tensor_tensor(dst_slice, dst_slice, tmp[:], op=OP.add))

        xa_view = [BIG[:, 0:16384], BIG[:, 16384:32768]]
        xnbuf = BIG[:, 32768:49152]
        xa_free = [0, 0]
        xn_free_pe = 0
        xn_wr_tok = []
        xb_wr = [[], []]
        for s in range(5):
            t0 = 512 * s
            par = s % 2
            lw = []
            for g in range(4):
                lw.append(gpq.dma(
                    lambda e, s=s, g=g, par=par: e.dma_start(
                        out=xa_view[par].rearrange("p (c q) -> p c q", c=NCH)[:, g * 8:(g + 1) * 8, :],
                        in_=xin.ap()[g * 8:(g + 1) * 8, :, 512 * s:512 * s + 512].transpose([1, 0, 2])),
                    waits=(consts_gp if s == 0 else [])
                          + ([(sPE, xa_free[par])] if s >= 2 else [])
                          + xb_wr[par]))
            xb_wr[par] = []
            off0 = max(0, 512 * s - 256)
            t0o = max(0, 256 - 512 * s)
            ln = min(512 * s + 512, 2304) - max(512 * s, 256)
            for g in range(4):
                xb_wr[par].append(syq.dma(
                    lambda e, g=g, par=par, t0o=t0o, off0=off0, ln=ln: e.dma_start(
                        out=s_xb.ap()[g * 8:(g + 1) * 8, :, off0:off0 + ln].transpose([1, 0, 2]),
                        in_=xa_view[par].rearrange("p (c q) -> p c q", c=NCH)[:, g * 8:(g + 1) * 8, t0o:t0o + ln]),
                    waits=list(lw)))
            for c in range(NCH):
                sq = sqr[c % 2]
                E("vector", lambda e, c=c, par=par, q=sq: e.tensor_tensor(
                    q[:], xa_view[par][:, c * 512:(c + 1) * 512],
                    xa_view[par][:, c * 512:(c + 1) * 512], op=OP.mult),
                  waits=list(lw) + sq_g[c % 2], inc=sDV)
                sqw = sDV.now()
                E("tensor", lambda e, c=c, par=par: e.matmul(
                    psB[:], ones_sb[:], xa_view[par][:, c * 512:(c + 1) * 512],
                    start=(c == 0), stop=(c == NCH - 1)),
                  waits=list(lw) + (consts_sy + [(sAC, state["psbc_ac"])] if c == 0 else []))
                E("tensor", lambda e, q=sq, c=c: e.matmul(
                    psC[:], ones_sb[:], q[:], start=(c == 0), stop=(c == NCH - 1)),
                  waits=[sqw], inc=sPE)
                sq_g[c % 2] = [sPE.now()]
            stw = stats_finalize(rsA, nmA, sPE.now())
            for c in range(NCH):
                E("vector", lambda e, c=c, par=par: e.tensor_tensor(
                    xnbuf[:, c * 512:(c + 1) * 512], xa_view[par][:, c * 512:(c + 1) * 512],
                    rsA[:], op=OP.mult),
                  waits=([stw, (sPE, xn_free_pe)] + xn_wr_tok if c == 0 else []))
                E("vector", lambda e, c=c: e.tensor_tensor(
                    xnbuf[:, c * 512:(c + 1) * 512], xnbuf[:, c * 512:(c + 1) * 512],
                    nmA[:], op=OP.add), inc=sDV)
                xnw = sDV.now()
                E("tensor", lambda e, c=c: e.matmul(
                    psD[:], wk_sb[:, c * 128:(c + 1) * 128], xnbuf[:, c * 512:(c + 1) * 512],
                    start=(c == 0), stop=(c == NCH - 1)),
                  waits=[xnw] + (consts_gp if (s == 0 and c == 0) else []))
            for t4 in range(4):
                for c in range(NCH):
                    E("tensor", lambda e, t4=t4, c=c: e.matmul(
                        psE[:, t4 * 128:(t4 + 1) * 128],
                        xnbuf[:, c * 512 + t4 * 128:c * 512 + t4 * 128 + 128],
                        wv_sb[:, c * 128:(c + 1) * 128],
                        start=(c == 0), stop=(c == NCH - 1)))
            E("tensor", None, inc=sPE)
            kvw = sPE.now()
            xn_free_pe = sPE.c
            xa_free[par] = sPE.c
            E("scalar", lambda e: e.copy(sqr[0][:], psD[:]),
              waits=[kvw] + sq_g[0] + sq_g[1], inc=sAC)
            E("vector", None, waits=[sAC.now()])
            rope(Kr[:, t0:t0 + 512], sqr[0], t0, sqr[1])
            for t4 in range(4):
                E("vector", lambda e, t4=t4, s=s: e.tensor_copy(
                    Vt[:, (4 * s + t4) * 128:(4 * s + t4) * 128 + 128],
                    psE[:, t4 * 128:(t4 + 1) * 128]),
                  inc=(sDV if t4 == 3 else None))
            kv_done = sDV.now()
            sq_g[0] = sq_g[1] = [sDV.now()]
            xn_wr_tok = []
            for g in range(4):
                xn_wr_tok.append(syq.dma(
                    lambda e, s=s, g=g: e.dma_start(
                        out=s_xn.ap()[g * 8:(g + 1) * 8, :, 512 * s:512 * s + 512].transpose([1, 0, 2]),
                        in_=xnbuf.rearrange("p (c q) -> p c q", c=NCH)[:, g * 8:(g + 1) * 8, :]),
                    waits=[sDV.now()]))
        kv_all = kv_done
        a_pe_done = sPE.now()

        xn_ld = []
        for g in range(4):
            xn_ld.append(syq.dma(
                lambda e, g=g: e.dma_start(
                    out=BIG.rearrange("p (c q) -> p c q", c=NCH)[:, g * 8:(g + 1) * 8, :],
                    in_=s_xn.ap()[g * 8:(g + 1) * 8, :, 256:256 + TOK].transpose([1, 0, 2])),
                waits=syq.all() + [kv_all, a_pe_done]))
        XN = BIG

        o_sbv = T1.ap().bitcast(BF16)
        rd_bfv = T3.ap().bitcast(BF16)
        o_sb = [o_sbv[:, 0:512], o_sbv[:, 512:1024]]
        rd_bf = [rd_bfv[:, 0:512], rd_bfv[:, 512:1024]]
        wcol_free = [0, 0]
        est_free = [None, None]
        psb_evac = [0, 0]
        q_written = []
        qhs = [qh.ap(), acat_sb.ap()[:, 0:TOK]]
        wl_pre = {}
        qw_pre = {}
        b2s = {"sig_slot": [0, 0], "smm_tok": [None, None], "rc_free": [0, 0],
               "oev_free": [0, 0], "pp_free": [0, 0], "qh_free": [0, 0],
               "bst_g": [[], []]}
        blend_toks = [[] for _ in range(H)]

        def emit_b1_ot(ot):
            par = ot % 2
            wl = wl_pre.pop(ot, None)
            if wl is None:
                wl = gpq.dma(lambda e, ot=ot, par=par: e.dma_start(
                    out=wcol[par][:, 0:D], in_=wq.ap()[ot]),
                    waits=[(sPE, wcol_free[par])] if ot >= 2 else [])
            qtoks = []
            for qs in range(NQS):
                yield
                bank = psB if qs % 2 == 0 else psC
                for c in range(NCH):
                    E("tensor", lambda e, c=c, qs=qs, par=par, bank=bank: e.matmul(
                        bank[:],
                        wcol[par][:, c * 128:(c + 1) * 128],
                        XN[:, c * 2048 + qs * 512: c * 2048 + qs * 512 + 512],
                        start=(c == 0), stop=(c == NCH - 1)),
                      waits=([wl] + xn_ld if c == 0 and qs == 0 else [])
                            + ([(sAC, psb_evac[qs % 2])] if c == 0 and (ot > 0 or qs >= 2) else []))
                E("tensor", None, inc=sPE)
                mmw = sPE.now()
                if qs == NQS - 1:
                    wcol_free[par] = sPE.c
                ei = (ot * NQS + qs) % 2
                E("scalar", lambda e, ei=ei, bank=bank: e.copy(estage[ei][:], bank[:]),
                  waits=[mmw] + ([est_free[ei]] if est_free[ei] else []),
                  inc=sAC)
                psb_evac[qs % 2] = sAC.c
                E("vector", None, waits=[sAC.now()] + sq_g[0])
                rope(estage[ei][:], estage[ei], 256 + qs * 512, sqr[0])
                E("vector", None, inc=sDV)
                sq_g[0] = [sDV.now()]
                tok = syq.dma(lambda e, ot=ot, qs=qs, ei=ei: e.dma_start(
                    out=s_q.ap()[ot, :, qs * 512:qs * 512 + 512], in_=estage[ei][:]),
                    waits=[sDV.now()])
                est_free[ei] = tok
                qtoks.append(tok)
                if qs == 1 and ot + 1 < H:
                    npar = (ot + 1) % 2
                    wl_pre[ot + 1] = gpq.dma(lambda e, o1=ot + 1, npar=npar: e.dma_start(
                        out=wcol[npar][:, 0:D], in_=wq.ap()[o1]),
                        waits=[(sPE, wcol_free[npar])])
            q_written.append(qtoks)

        def emit_b2_head(h):
            yield
            par = h % 2
            myq = qhs[par]
            qw = qw_pre.pop(h, None)
            if qw is None:
                qw = syq.dma(lambda e, par=par, h=h: e.dma_start(
                    out=qhs[par], in_=s_q.ap()[h]),
                    waits=list(q_written[h]) + ([(sPE, b2s["qh_free"][par])] if h >= 2 else []))
            sigs = {}
            dn = [psA[:, 1024:1536], psA[:, 1536:2048]]

            def s_mms(w, cpair):
                qlo = max(0, 256 * w - 256)
                qhi = min(TOK, 256 * w + 256)
                wlen = qhi - qlo
                boff = qlo - (256 * w - 256)
                for c in (cpair, cpair + 1):
                    kt = 2 * w + c
                    E("tensor", lambda e, c=c, kt=kt, qlo=qlo, wlen=wlen, mq=myq: e.matmul(
                        psA[:, (c % 2) * 512: (c % 2) * 512 + wlen],
                        Kr[:, kt * 128:kt * 128 + 128],
                        mq[:, qlo:qlo + wlen],
                        start=True, stop=True),
                      waits=([qw, kv_all] if c == cpair else [])
                            + [(sAC, b2s["sig_slot"][c % 2])],
                      inc=sPE)
                    b2s["smm_tok"][c % 2] = sPE.now()
                return boff, wlen

            def sig_half(w, cpair, boff, wlen, pp):
                for c in (cpair, cpair + 1):
                    kt = 2 * w + c
                    masked = (w == 0 or w == NW - 1)
                    E("scalar", lambda e, c=c, kt=kt, boff=boff, wlen=wlen, pp=pp,
                                m=masked: e.activation(
                        P_sb[pp][:, c * 512 + boff: c * 512 + boff + wlen],
                        psA[:, (c % 2) * 512: (c % 2) * 512 + wlen],
                        AF.Sigmoid, scale=SCALE,
                        **(dict(bias=mask_sb[:, kt:kt + 1]) if m else {})),
                      waits=[b2s["smm_tok"][c % 2]]
                            + ([(sPE, b2s["pp_free"][pp])] if c == cpair else []),
                      inc=sAC)
                    b2s["sig_slot"][c % 2] = sAC.c

            for w in range(NW + 1):
                yield
                if w >= 1:
                    wp = w - 1
                    sw, pp_p, boff_p, wlen_p = sigs[wp]
                    ds = wp % 2
                    ob = [psD, psE][ds]
                if w <= NW - 1:
                    pp = (h * NW + w) % 2
                    boff, wlen = s_mms(w, 0)
                    sig_half(w, 0, boff, wlen, pp)
                if w >= 1:
                    for c in range(4):
                        E("tensor", lambda e, c=c, pp=pp_p, boff=boff_p, wlen=wlen_p, ds=ds: e.matmul(
                            dn[ds][:, boff:boff + wlen], ones_sb[:],
                            P_sb[pp][:, c * 512 + boff: c * 512 + boff + wlen],
                            start=(c == 0), stop=(c == 3)),
                          waits=[sw, (sDV, b2s["rc_free"][ds])] if c == 0 else [],
                          inc=(sPE if c == 3 else None))
                    dn_tok = sPE.now()
                if w <= NW - 1:
                    boff, wlen = s_mms(w, 2)
                    if w == NW - 1:
                        b2s["qh_free"][par] = sPE.c
                    sig_half(w, 2, boff, wlen, pp)
                    sigs[w] = (sAC.now(), pp, boff, wlen)
                    if w == 2 and h + 1 < H:
                        hn = h + 1
                        npar = hn % 2
                        qw_pre[hn] = syq.dma(lambda e, h1=hn, npar=npar: e.dma_start(
                            out=qhs[npar], in_=s_q.ap()[h1]),
                            waits=list(q_written[hn]) + [(sPE, b2s["qh_free"][npar])])
                if w >= 1:
                    for c in range(4):
                        kt = 2 * wp + c
                        E("tensor", lambda e, c=c, kt=kt, pp=pp_p, boff=boff_p, wlen=wlen_p, ob=ob: e.matmul(
                            ob[:, boff:boff + wlen],
                            Vt[:, kt * 128:kt * 128 + 128],
                            P_sb[pp][:, c * 512 + boff: c * 512 + boff + wlen],
                            start=(c == 0), stop=(c == 3)),
                          waits=([(sAC, b2s["oev_free"][ds])] if c == 0 else []),
                          inc=(sPE if c == 3 else None))
                    b2s["pp_free"][pp_p] = sPE.c
                    dnw = sPE.now()
                    E("scalar", lambda e, boff=boff_p, wlen=wlen_p, ob=ob, ds=ds: e.copy(
                        o_sb[ds][:, boff:boff + wlen], ob[:, boff:boff + wlen]),
                      waits=[dnw, (sDV, b2s["rc_free"][ds])], inc=sAC)
                    b2s["oev_free"][ds] = sAC.c
                    oev = sAC.now()

                    def _recip(e, boff=boff_p, wlen=wlen_p, ds=ds):
                        with nc.allow_low_precision("bf16 rounding of 1/denom; product is bf16 anyway"):
                            return e.reciprocal(rd_bf[ds][:, boff:boff + wlen], dn[ds][:, boff:boff + wlen])
                    E("vector", _recip, waits=[dn_tok])
                    E("vector", lambda e, wp=wp, boff=boff_p, wlen=wlen_p, ds=ds: e.tensor_tensor(
                        obuf[wp % 2][:, boff:boff + wlen], o_sb[ds][:, boff:boff + wlen],
                        rd_bf[ds][:, boff:boff + wlen], op=OP.mult), waits=[oev], inc=sDV)
                    b2s["rc_free"][ds] = sDV.c
                    if wp >= 1:
                        m = wp - 1
                        bi = m % 2
                        a_new, a_old = (alA, alB) if m == 0 else (alC, alD)
                        E("vector", lambda e, wp=wp, a_old=a_old: e.tensor_tensor(
                            T2b[:, 0:256], obuf[(wp - 1) % 2][:, 256:512], a_old[:], op=OP.mult))
                        E("vector", lambda e, wp=wp, a_new=a_new: e.tensor_tensor(
                            T2b[:, 256:512], obuf[wp % 2][:, 0:256], a_new[:], op=OP.mult))
                        E("vector", lambda e, bi=bi: e.tensor_tensor(
                            T2b[:, 512 + 256 * bi: 768 + 256 * bi],
                            T2b[:, 0:256], T2b[:, 256:512], op=OP.add),
                          waits=b2s["bst_g"][bi], inc=sDV)
                        b2s["bst_g"][bi] = [syq.dma(
                            lambda e, h=h, m=m, bi=bi: e.dma_start(
                                out=s_q.ap()[h, :, 256 * m:256 * m + 256],
                                in_=T2b[:, 512 + 256 * bi: 768 + 256 * bi]),
                            waits=[sDV.now(), qw])]
                        blend_toks[h].append(b2s["bst_g"][bi][0])

        LAG = 2
        acat_ld = []
        b1_pe_guard = [0]
        gens = []

        def _drive(g):
            try:
                next(g)
                return True
            except StopIteration:
                return False

        active_b1 = None
        active_b2 = None
        for k in range(H + LAG):
            if k < H:
                active_b1 = emit_b1_ot(k)
                next(active_b1)
            if k >= LAG:
                active_b2 = emit_b2_head(k - LAG)
                next(active_b2)
            sched = [0, 1, 1, 1, 0, 1, 1, 0, 1, 1, 1, 0, 1, 1, 1, 1]
            for pick in sched:
                if pick == 0:
                    if active_b1 is not None and not _drive(active_b1):
                        active_b1 = None
                else:
                    if active_b2 is not None and not _drive(active_b2):
                        active_b2 = None
            while active_b1 is not None and _drive(active_b1):
                pass
            if k == H - 1:
                b1_pe_guard[0] = sPE.c
                for g in range(3):
                    toks = [t for hh in range(8 * g, 8 * g + 8) for t in blend_toks[hh]]
                    acat_ld.append(syq.dma(
                        lambda e, g=g: e.dma_start(
                            out=BIG.rearrange("p (c q) -> p c q", c=NCH)[:, g * 8:(g + 1) * 8, :],
                            in_=s_q.ap()[g * 8:(g + 1) * 8, :, :].transpose([1, 0, 2])),
                        waits=toks + [(sPE, b1_pe_guard[0])]))
            while active_b2 is not None and _drive(active_b2):
                pass
        toks = [t for hh in range(24, 32) for t in blend_toks[hh]]
        acat_ld.append(syq.dma(
            lambda e: e.dma_start(
                out=BIG.rearrange("p (c q) -> p c q", c=NCH)[:, 24:32, :],
                in_=s_q.ap()[24:32, :, :].transpose([1, 0, 2])),
            waits=toks + [(sPE, b1_pe_guard[0])]))
        acat_w = gpq.dma(lambda e: e.dma_start(out=acat_sb[:], in_=acat.ap()),
                         waits=[(sPE, b2s["qh_free"][1])])
        state["psbc_ac"] = sAC.c
        state["psde_ac"] = sAC.c

        E("vector", lambda e: e.memset(psB[:], 0.0),
          waits=[(sAC, psb_evac[0]), (sAC, psb_evac[1])])
        E("vector", lambda e: e.memset(psC[:], 0.0), inc=sDV)
        b3ms = sDV.now()
        wcol_free = [sPE.c, sPE.c]
        est_free = [None, None]
        psa_evac = [sAC.c] * 4
        at_written = [[None] * NQS for _ in range(NCH)]
        pend3 = [None]
        b3_half_pe = [None, None]

        arB4 = ([Kr[:, i * 512:(i + 1) * 512] for i in range(5)]
                + [Vt[:, 512 + i * 512:1024 + i * 512] for i in range(3)])
        arB4_g = [[] for _ in range(8)]
        xrR = [qh[:, i * 512:(i + 1) * 512] for i in range(4)]
        xrR_g = [[] for _ in range(4)]
        sqb = [obuf[0][:], obuf[1][:]]
        sqb_g = [[], []]
        y1_written = [None] * NQS
        y1n_q = [None] * NQS
        fin32_read = [None] * NQS
        fin2_read = [None] * NQS
        b4st = {"yw": {}, "ms": {}}

        def b4_partA(qs):
            qo = qs * 512
            stw = stats_finalize32(rsA, nmA, psB[32 * qs:32 * qs + 32, :],
                                   psC[32 * qs:32 * qs + 32, :],
                                   [b3_half_pe[qs // 2]])
            fin32_read[qs] = state["fin_read"]
            E("vector", lambda e, qs=qs: e.memset(psB[32 * qs:32 * qs + 32, :], 0.0),
              waits=[(sAC, fin32_read[qs])])
            E("vector", lambda e, qs=qs: e.memset(psC[32 * qs:32 * qs + 32, :], 0.0),
              inc=sDV)
            b4st["ms"][qs] = sDV.now()
            aws, xws, yw = {}, {}, {}
            b4st["yw"] = yw

            def _lda(c, qs=qs, qo=qo):
                slot = c % 8
                aws[c] = syq.dma(lambda e, c=c, slot=slot, qs=qs, qo=qo: e.dma_start(
                    out=arB4[slot], in_=s_at.ap()[c, :, qo:qo + 512]),
                    waits=[at_written[c][qs]] + arB4_g[slot])
                arB4_g[slot] = []

            def _ldx(c, qs=qs, qo=qo):
                xslot = c % 4
                xws[c] = syq.dma(lambda e, c=c, xslot=xslot, qo=qo: e.dma_start(
                    out=xrR[xslot], in_=s_xb.ap()[c, :, qo:qo + 512]),
                    waits=xrR_g[xslot]
                          + (xn_ld + [b3_half_pe[0]] if qs == 0 and c < 4 else []))
                xrR_g[xslot] = []

            for c in range(8):
                _lda(c)
            for c in range(4):
                _ldx(c)
            for c in range(NCH):
                slot, xslot = c % 8, c % 4
                aw, xw = aws[c], xws[c]
                dst = XN[:, c * 2048 + qo: c * 2048 + qo + 512]
                E("vector", lambda e, slot=slot, d=dst: e.tensor_tensor(
                    d, arB4[slot], rsA[:], op=OP.mult),
                  waits=[aw, stw] + ([b3_half_pe[qs // 2]] if c == 0 else []))
                E("vector", lambda e, d=dst: e.tensor_tensor(d, d, nmA[:], op=OP.add))
                E("vector", lambda e, xslot=xslot, d=dst: e.tensor_tensor(
                    d, d, xrR[xslot], op=OP.add), waits=[xw], inc=sDV)
                arB4_g[slot].append(sDV.now())
                xrR_g[xslot].append(sDV.now())
                yw[c] = sDV.now()
                if c + 8 < NCH:
                    _lda(c + 8)
                if c + 4 < NCH:
                    _ldx(c + 4)
                syq.dma(lambda e, c=c, qo=qo, d=dst: e.dma_start(
                    out=s_y1.ap()[c, :, qo:qo + 512], in_=d), waits=[yw[c]])

        def b4_batch(qs, c0):
            qo = qs * 512
            for c in range(c0, c0 + 8):
                dst = XN[:, c * 2048 + qo: c * 2048 + qo + 512]
                sq = sqb[c % 2]
                E("scalar", lambda e, q=sq, d=dst: e.activation(q, d, AF.Square),
                  waits=[b4st["yw"][c]] + sqb_g[c % 2], inc=sAC)
                sqw = sAC.now()
                E("tensor", lambda e, d=dst, qs=qs: e.matmul(
                    psB[32 * qs:32 * qs + 32, :], ones_sb[:, 0:32], d,
                    start=False, stop=False, skip_group_check=True,
                    tile_position=(0, 32 * qs)),
                  waits=[b4st["yw"][c]] + ([b4st["ms"][qs]] if c == c0 else []))
                E("tensor", lambda e, q=sq, qs=qs: e.matmul(
                    psC[32 * qs:32 * qs + 32, :], ones_sb[:, 0:32], q,
                    start=False, stop=False, skip_group_check=True,
                    tile_position=(0, 32 * qs)),
                  waits=[sqw], inc=sPE)
                sqb_g[c % 2] = [sPE.now()]

        def b4_partB(qs):
            qo = qs * 512
            stw2 = stats_finalize32(rsB, nmB, psB[32 * qs:32 * qs + 32, :],
                                    psC[32 * qs:32 * qs + 32, :], [sPE.now()])
            fin2_read[qs] = state["fin_read"]
            wr_done = syq.all()
            for c in range(NCH):
                dst = XN[:, c * 2048 + qo: c * 2048 + qo + 512]
                E("vector", lambda e, d=dst: e.tensor_tensor(d, d, rsB[:], op=OP.mult),
                  waits=([stw2] + wr_done) if c == 0 else [])
                E("vector", lambda e, d=dst: e.tensor_tensor(d, d, nmB[:], op=OP.add),
                  inc=(sDV if c == NCH - 1 else None))
            y1n_q[qs] = sDV.now()
            y1_written[qs] = wr_done

        def b4_sched3(ot):
            if ot == 0:
                b4_partA(0)
            elif 2 <= ot <= 5:
                b4_batch(0, 8 * (ot - 2))
            elif ot == 6:
                b4_partB(0)
                b4_partA(1)
            elif 8 <= ot <= 11:
                b4_batch(1, 8 * (ot - 8))
            elif ot == 12:
                b4_partB(1)

        for half in (0, 1):
            for ot in range(NCH):
                par = ot % 2
                wl = gpq.dma(lambda e, ot=ot, par=par: e.dma_start(
                    out=wcol[par][:, 0:D], in_=wo.ap()[ot]),
                    waits=[(sPE, wcol_free[par])])
                for qs in (2 * half, 2 * half + 1):
                    for c in range(NCH):
                        E("tensor", lambda e, c=c, qs=qs, par=par: e.matmul(
                            psA[:, qs * 512:qs * 512 + 512],
                            wcol[par][:, c * 128:(c + 1) * 128],
                            XN[:, c * 2048 + qs * 512: c * 2048 + qs * 512 + 512],
                            start=(c == 0), stop=(c == NCH - 1)),
                          waits=(([wl] + (acat_ld if half == 0 and ot == 0 and qs == 0 else []))
                                 if c == 0 and qs == 2 * half else [])
                                + ([(sAC, psa_evac[qs])] if c == 0 and ot >= 1 else []))
                    if pend3[0] is not None:
                        pend3[0]()
                    E("tensor", None, inc=sPE)
                    mmw = sPE.now()
                    if qs == 2 * half + 1:
                        wcol_free[par] = sPE.c
                    ei = qs % 2
                    E("scalar", lambda e, ei=ei, qs=qs: e.copy(
                        estage[ei][:], psA[:, qs * 512:qs * 512 + 512]),
                      waits=[mmw] + (est_free[ei] if est_free[ei] else []),
                      inc=sAC)
                    psa_evac[qs] = sAC.c
                    evt = sAC.now()
                    tok = syq.dma(lambda e, ot=ot, qs=qs, ei=ei: e.dma_start(
                        out=s_at.ap()[ot, :, qs * 512:qs * 512 + 512], in_=estage[ei][:]),
                        waits=[evt])
                    sq = sqr[ei]
                    E("scalar", lambda e, ei=ei, q=sq: e.activation(q[:], estage[ei][:], AF.Square),
                      waits=sq_g[ei], inc=sAC)
                    sqw = sAC.now()

                    def _stats3(ei=ei, qs=qs, ot=ot, evt=evt, sqw=sqw, sq=sq, tok=tok):
                        E("tensor", lambda e, ei=ei, qs=qs: e.matmul(
                            psB[32 * qs:32 * qs + 32, :], ones_sb[:, 0:32], estage[ei][:],
                            start=False, stop=False, skip_group_check=True,
                            tile_position=(0, 32 * qs)),
                          waits=[evt] + ([b3ms] if ot == 0 else []))
                        E("tensor", lambda e, q=sq, qs=qs: e.matmul(
                            psC[32 * qs:32 * qs + 32, :], ones_sb[:, 0:32], q[:],
                            start=False, stop=False, skip_group_check=True,
                            tile_position=(0, 32 * qs)),
                          waits=[sqw], inc=sPE)
                        sq_g[ei] = [sPE.now()]
                        est_free[ei] = [tok, sPE.now()]
                    pend3[0] = _stats3
                    at_written[ot][qs] = tok
                if half == 1:
                    b4_sched3(ot)
            pend3[0]()
            pend3[0] = None
            b3_half_pe[half] = sPE.now()


        GA = psA[:, 1024:1536]
        HB = psA[:, 1536:2048]
        hw_rdy = [None] * 4
        gate_psde_ac = [0]
        g5 = {"hw_dv": 0, "ga_free": sAC.c}

        def gate_gen(qs):
            qo = qs * 512
            for c in range(NCH):
                E("tensor", lambda e, c=c, qo=qo: e.matmul(
                    psA[0:33, 1024:1536], wg9_sb[:, c * 33:(c + 1) * 33],
                    XN[:, c * 2048 + qo: c * 2048 + qo + 512],
                    start=(c == 0), stop=(c == NCH - 1)),
                  waits=(([y1n_q[qs], (sAC, g5["ga_free"]),
                           (sAC, psa_evac[2]), (sAC, psa_evac[3]),
                           (sDV, g5["hw_dv"])] + consts_gp)
                         if c == 0 else []))
            yield
            for c in range(NCH):
                E("tensor", lambda e, c=c, qo=qo: e.matmul(
                    HB, acat_sb[:, c * 128:(c + 1) * 128],
                    XN[:, c * 2048 + qo: c * 2048 + qo + 512],
                    start=(c == 0), stop=(c == NCH - 1)),
                  waits=([acat_w] if c == 0 else []),
                  inc=(sPE if c == NCH - 1 else None))
            gh_mm = sPE.now()
            yield
            E("scalar", lambda e: e.activation(gvA[0:8, :], psA[0:8, 1024:1536], AF.Exp),
              waits=[gh_mm], inc=sAC)
            E("scalar", lambda e: e.activation(gvA[32:33, :], psA[32:33, 1024:1536], AF.Tanh,
                                               scale=0.5, bias=bthr2[32:33, :]), inc=sAC)
            g5["ga_free"] = sAC.c
            gw = sAC.now()
            E("vector", lambda e: e.tensor_scalar(gvA[32:33, :], gvA[32:33, :], 0.0625, 0.0625,
                                                  op0=OP.mult, op1=OP.add), waits=[gw])
            E("vector", lambda e: e.tensor_copy(ebf, gvA[0:8, :]))
            E("vector", lambda e: e.tensor_copy(tbf, gvA[32:33, :]), inc=sDV)
            bfw = sDV.now()
            yield
            E("tensor", lambda e: e.matmul(psA[0:8, 1024:1536], ones_sb[0:8, 0:8], ebf,
                                           start=True, stop=True),
              waits=[bfw, (sAC, g5["ga_free"])])
            E("tensor", lambda e: e.matmul(psD[0:8, :], ones_sb[0:1, 0:8], tbf,
                                           start=True, stop=True,
                                           skip_group_check=True),
              waits=[(sAC, gate_psde_ac[0])], inc=sPE)
            sbw = sPE.now()
            yield
            E("scalar", lambda e: e.copy(gvB[0:8, :], psD[0:8, :]), waits=[sbw], inc=sAC)
            gate_psde_ac[0] = sAC.c
            E("vector", lambda e: e.tensor_tensor(gvC[0:8, :], psA[0:8, 1024:1536], gvB[0:8, :],
                                                  op=OP.mult), waits=[sAC.now()])
            E("vector", lambda e: e.tensor_sub(gvA[0:8, :], gvA[0:8, :], gvC[0:8, :]))
            E("vector", lambda e: e.tensor_scalar_max(gvA[0:8, :], gvA[0:8, :], 0.0))
            E("vector", lambda e: e.tensor_copy(webf, gvA[0:8, :]), inc=sDV)
            wew = sDV.now()
            yield
            E("tensor", lambda e: e.matmul(psA[0:8, 1024:1536], ones_sb[0:8, 0:8], webf,
                                           start=True, stop=True), waits=[wew], inc=sPE)
            wsw = sPE.now()
            yield
            E("scalar", lambda e: e.copy(gvB[0:8, :], psA[0:8, 1024:1536]), waits=[wsw], inc=sAC)
            g5["ga_free"] = sAC.c
            E("vector", lambda e: e.tensor_scalar(gvC[0:8, :], gvB[0:8, :], 0.0, None,
                                                  op0=OP.is_equal), waits=[sAC.now()])
            E("vector", lambda e: e.tensor_add(gvB[0:8, :], gvB[0:8, :], gvC[0:8, :]))
            E("vector", lambda e: e.reciprocal(gvB[0:8, :], gvB[0:8, :]))
            E("vector", lambda e: e.tensor_mul(WN[:], gvA[0:8, :], gvB[0:8, :]), inc=sDV)
            wnw = sDV.now()
            yield
            E("tensor", lambda e: e.matmul(psD[:], e16_sb[:], WN[:], start=True, stop=True),
              waits=[wnw, sAC.now()] + consts_sy, inc=sPE)
            exw = sPE.now()
            yield
            E("scalar", lambda e: e.copy(wful[:], psD[:]), waits=[exw], inc=sAC)
            gate_psde_ac[0] = sAC.c
            E("vector", lambda e, qo=qo, qs=qs: e.tensor_tensor(
                hw_all[:, qo:qo + 512], HB, wful[:], op=OP.mult),
              waits=[sAC.now()], inc=sDV)
            g5["hw_dv"] = sDV.c
            hw_rdy[qs] = sDV.now()

        g = gate_gen(0)
        for _ in g:
            pass
        gate_q = {1: gate_gen(1), 2: gate_gen(2), 3: gate_gen(3)}

        ms5 = [None] * NQS

        def b5_memset(qs):
            E("vector", lambda e, qs=qs: e.memset(psB[32 * qs:32 * qs + 32, :], 0.0),
              waits=[(sAC, fin2_read[qs])])
            E("vector", lambda e, qs=qs: e.memset(psC[32 * qs:32 * qs + 32, :], 0.0),
              inc=sDV)
            ms5[qs] = sDV.now()

        b5_memset(0)
        b5_memset(1)

        def b4_sched5(ot):
            if ot == 0:
                b4_partA(2)
            elif 2 <= ot <= 5:
                b4_batch(2, 8 * (ot - 2))
            elif ot == 6:
                b4_partB(2)
                b5_memset(2)
                b4_partA(3)
            elif ot == 7:
                for _ in gate_q.pop(2):
                    pass
            elif 8 <= ot <= 11:
                b4_batch(3, 8 * (ot - 8))
            elif ot == 12:
                b4_partB(3)
                b5_memset(3)
            elif ot == 13:
                for _ in gate_q.pop(3):
                    pass

        wcol_free = [sPE.c, sPE.c]
        psb_evac5 = [sAC.c, sAC.c]
        ff_written = [[None] * NQS for _ in range(NCH)]
        pend5 = [None]
        b5_half_pe = [None, None]
        for half in (0, 1):
            for ot in range(NCH):
                par = ot % 2
                wl = gpq.dma(lambda e, ot=ot, par=par: e.dma_start(
                    out=wcol[par][:], in_=wb.ap()[ot]),
                    waits=[(sPE, wcol_free[par])])
                for qs in (2 * half, 2 * half + 1):
                    if half == 0 and ot == 0 and qs == 1:
                        for _ in gate_q.pop(1):
                            pass
                    bank = psA[:, 0:512] if qs % 2 == 0 else psA[:, 512:1024]
                    for c in range(NCH):
                        E("tensor", lambda e, c=c, qs=qs, par=par, bank=bank: e.matmul(
                            bank, wcol[par][:, c * 128:(c + 1) * 128],
                            XN[:, c * 2048 + qs * 512: c * 2048 + qs * 512 + 512],
                            start=(c == 0), stop=False),
                          waits=(([wl] if c == 0 and qs == 2 * half else [])
                                 + (([y1n_q[qs]]
                                     + ([(sAC, psa_evac[qs % 2])] if half == 0 else []))
                                    if c == 0 and ot == 0 else [])
                                + ([(sAC, psb_evac5[qs % 2])] if c == 0 and (ot > 0 or half == 1) else [])))
                    E("tensor", lambda e, qs=qs, par=par, bank=bank: e.matmul(
                        bank, wcol[par][:, D:D + 128],
                        hw_all[:, qs * 512:qs * 512 + 512],
                        start=False, stop=True),
                      waits=[hw_rdy[qs]] if ot == 0 else [])
                    if pend5[0] is not None:
                        pend5[0]()
                    E("tensor", None, inc=sPE)
                    mmw = sPE.now()
                    if qs == 2 * half + 1:
                        wcol_free[par] = sPE.c
                    ei = qs % 2
                    E("scalar", lambda e, ei=ei, bank=bank, ot=ot: e.add(
                        estage[ei][:], bank, bbase_sb[:, ot:ot + 1]),
                      waits=[mmw] + (est_free[ei] if est_free[ei] else []),
                      inc=sAC)
                    psb_evac5[qs % 2] = sAC.c
                    evt = sAC.now()
                    tok = syq.dma(lambda e, ot=ot, qs=qs, ei=ei: e.dma_start(
                        out=s_ff.ap()[ot, :, qs * 512:qs * 512 + 512], in_=estage[ei][:]),
                        waits=[evt])
                    sq = sqr[ei]
                    E("scalar", lambda e, ei=ei, q=sq: e.activation(q[:], estage[ei][:], AF.Square),
                      waits=sq_g[ei], inc=sAC)
                    sqw = sAC.now()

                    def _stats5(ei=ei, qs=qs, ot=ot, evt=evt, sqw=sqw, sq=sq, tok=tok):
                        E("tensor", lambda e, ei=ei, qs=qs: e.matmul(
                            psB[32 * qs:32 * qs + 32, :], ones_sb[:, 0:32], estage[ei][:],
                            start=False, stop=False, skip_group_check=True,
                            tile_position=(0, 32 * qs)),
                          waits=[evt] + ([ms5[qs]] if ot == 0 else []))
                        E("tensor", lambda e, q=sq, qs=qs: e.matmul(
                            psC[32 * qs:32 * qs + 32, :], ones_sb[:, 0:32], q[:],
                            start=False, stop=False, skip_group_check=True,
                            tile_position=(0, 32 * qs)),
                          waits=[sqw], inc=sPE)
                        sq_g[ei] = [sPE.now()]
                        est_free[ei] = [tok, sPE.now()]
                    pend5[0] = _stats5
                    ff_written[ot][qs] = tok
                if half == 0:
                    b4_sched5(ot)
            pend5[0]()
            pend5[0] = None
            b5_half_pe[half] = sPE.now()
        b5_pe = b5_half_pe[1]

        vt_f = Vt.ap().bitcast(F32)
        ysR = [kr_f[:, 0:512], kr_f[:, 512:1024], vt_f[:, 0:512], vt_f[:, 512:1024],
               cos_f[:, 0:512], cos_f[:, 512:1024], ss_f[:, 0:512], ss_f[:, 512:1024]]
        ysR_g = [[] for _ in range(8)]
        arR6 = ([P_sb[0][:, i * 512:(i + 1) * 512] for i in range(4)]
                + [acat_sb[:, i * 512:(i + 1) * 512] for i in range(4)])
        arR6_g = [[hw_rdy[3]] for _ in range(4)] + [[acat_w] for _ in range(4)]
        xrR6 = xrR + [obuf[1][:]]
        xrR_g6 = [list(xrR_g[i]) for i in range(4)] + [list(sqb_g[1])]
        for qs in range(NQS):
            qo = qs * 512
            hf = qs // 2
            stw = stats_finalize32(rsA, nmA, psB[32 * qs:32 * qs + 32, :],
                                   psC[32 * qs:32 * qs + 32, :], [b5_half_pe[hf]])
            aws, xws = {}, {}

            def _lda6(c, qs=qs, qo=qo, hf=hf):
                slot = c % 8
                aws[c] = syq.dma(lambda e, c=c, slot=slot, qo=qo: e.dma_start(
                    out=arR6[slot], in_=s_ff.ap()[c, :, qo:qo + 512]),
                    waits=[ff_written[c][qs], b5_half_pe[hf]] + arR6_g[slot])
                arR6_g[slot] = []

            def _ldy6(c, qs=qs, qo=qo):
                xslot = c % 5
                xws[c] = gpq.dma(lambda e, c=c, xslot=xslot, qo=qo: e.dma_start(
                    out=xrR6[xslot], in_=s_y1.ap()[c, :, qo:qo + 512]),
                    waits=list(y1_written[qs]) + xrR_g6[xslot])
                xrR_g6[xslot] = []

            for c in range(8):
                _lda6(c)
            for c in range(5):
                _ldy6(c)
            for c in range(NCH):
                slot = c % 8
                xslot = c % 5
                aw = aws[c]
                yw = xws[c]
                ypar = c % 8
                E("vector", lambda e, slot=slot: e.tensor_tensor(
                    obuf[0][:], arR6[slot], rsA[:], op=OP.mult), waits=[aw, stw])
                E("vector", lambda e: e.tensor_tensor(obuf[0][:], obuf[0][:], nmA[:], op=OP.add))
                E("vector", lambda e, xslot=xslot, ypar=ypar: e.tensor_tensor(
                    ysR[ypar], obuf[0][:], xrR6[xslot], op=OP.add),
                  waits=[yw] + ysR_g[ypar], inc=sDV)
                arR6_g[slot].append(sDV.now())
                xrR_g6[xslot].append(sDV.now())
                if c + 8 < NCH:
                    _lda6(c + 8)
                if c + 5 < NCH:
                    _ldy6(c + 5)
                ysR_g[ypar] = [syq.dma(lambda e, c=c, qo=qo, ypar=ypar: e.dma_start(
                    out=y2.ap()[c, :, qo:qo + 512], in_=ysR[ypar]),
                    waits=[sDV.now()])]
        E("sync", None, waits=syq.all())

        with nc.Block() as block:
            @block.sync
            def _(e):
                p.replay("sync", e)

            @block.gpsimd
            def _(e):
                p.replay("gpsimd", e)

            @block.tensor
            def _(e):
                p.replay("tensor", e)

            @block.scalar
            def _(e):
                p.replay("scalar", e)

            @block.vector
            def _(e):
                p.replay("vector", e)

    return nc


def _rope_tables():
    inv_freq = 1.0 / 10000.0 ** (np.arange(0, DH, 2, dtype=np.float32) / DH)
    t = np.arange(S, dtype=np.float32)
    f = np.outer(t, inv_freq)
    cos = np.zeros((S, DH), dtype=np.float32)
    sin = np.zeros((S, DH), dtype=np.float32)
    cos[:, 0::2] = np.cos(f)
    cos[:, 1::2] = np.cos(f)
    sin[:, 0::2] = np.sin(f)
    sin[:, 1::2] = np.sin(f)
    return cos, sin


def host_inputs(L0_attn_x, L0_attn_Wq, L0_attn_Wk, L0_attn_Wv, L0_attn_Wo,
                L0_ffn_W_base, L0_ffn_b_base, L0_ffn_W_gate, L0_ffn_W_thr,
                L0_ffn_b_thr, L0_ffn_A_w, L0_ffn_B_w):
    x = np.asarray(L0_attn_x, dtype=np.float32)
    Wq = np.asarray(L0_attn_Wq, dtype=np.float32)
    Wk = np.asarray(L0_attn_Wk, dtype=np.float32)
    Wv = np.asarray(L0_attn_Wv, dtype=np.float32)
    Wo = np.asarray(L0_attn_Wo, dtype=np.float32)
    Wb = np.asarray(L0_ffn_W_base, dtype=np.float32)
    bb = np.asarray(L0_ffn_b_base, dtype=np.float32)
    Wg = np.asarray(L0_ffn_W_gate, dtype=np.float32)
    Wt = np.asarray(L0_ffn_W_thr, dtype=np.float32)
    bt = np.asarray(L0_ffn_b_thr, dtype=np.float32)
    Aw = np.asarray(L0_ffn_A_w, dtype=np.float32)
    Bw = np.asarray(L0_ffn_B_w, dtype=np.float32)

    def slab(W):
        return np.ascontiguousarray(W.reshape(32, 128, 32, 128).transpose(0, 3, 2, 1)
                                    .reshape(32, 128, D))

    wq_s = slab(Wq)
    wo_s = slab(Wo)
    wb_s = slab(Wb)
    B2 = Bw.transpose(0, 2, 1).reshape(128, D)
    b2col = np.ascontiguousarray(B2.reshape(128, 32, 128).transpose(1, 0, 2))
    wb_s = np.ascontiguousarray(np.concatenate([wb_s, b2col], axis=2))

    def kvlay(W):
        return np.ascontiguousarray(
            W.T.reshape(32, 128, 128).transpose(1, 0, 2).reshape(128, D))
    wk_l = kvlay(Wk)
    wv_l = kvlay(Wv)

    Wg33 = np.zeros((33, D), np.float32)
    Wg33[0:8] = Wg
    Wg33[32] = Wt
    wg9_l = np.ascontiguousarray(
        Wg33.T.reshape(32, 128, 33).transpose(1, 0, 2).reshape(128, 32 * 33))
    Acat = Aw.reshape(128, D)
    acat_l = np.ascontiguousarray(
        Acat.T.reshape(32, 128, 128).transpose(1, 0, 2).reshape(128, D))
    bbase_l = np.ascontiguousarray(bb.reshape(32, 128).T)
    bthr_l = np.ascontiguousarray(bt.reshape(1, 1))

    ones_c = np.ones((128, 128), dtype=BF)
    e16_c = np.zeros((8, 128), dtype=np.float32)
    for e in range(8):
        e16_c[e, e * 16:(e + 1) * 16] = 2.0
    e16_c = e16_c.astype(BF)
    cosT, sinT = _rope_tables()

    al = np.linspace(0.0, 1.0, 256, dtype=np.float32)
    alC_c = np.ascontiguousarray(np.broadcast_to(al, (128, 256))).astype(BF)
    alD_c = np.ascontiguousarray(np.broadcast_to(1.0 - al, (128, 256))).astype(BF)
    al1 = np.ones((128, 256), dtype=BF)
    al0 = np.zeros((128, 256), dtype=BF)

    in_maps = []
    for core in range(NCORE):
        seq = core // 4
        c0 = (core % 4) * TOK
        lo, hi = c0 - 256, c0 + TOK + 256
        lo_c, hi_c = max(lo, 0), min(hi, S)
        xs = x[seq, lo_c:hi_c]
        xs = np.pad(xs, ((lo_c - lo, hi - hi_c), (0, 0)))
        xin_l = np.ascontiguousarray(xs.T.reshape(32, 128, HALO))
        pos = np.clip(np.arange(lo, hi), 0, S - 1)
        cos_l = np.ascontiguousarray(cosT[pos].T).astype(BF)
        ssv = sinT[pos].T.copy()
        ssv[0:64] *= -1.0
        ss_l = np.ascontiguousarray(ssv).astype(BF)
        maskv = np.zeros(HALO, dtype=np.float32)
        if lo < 0:
            maskv[: -lo] = MASKVAL
        if hi > S:
            maskv[-(hi - S):] = MASKVAL
        mask_l = np.ascontiguousarray(
            np.broadcast_to(maskv.reshape(20, 128).T, (128, 20)))
        first = (core % 4 == 0)
        in_maps.append(dict(
            xin=xin_l, wq=wq_s, wo=wo_s, wb=wb_s, wk=wk_l, wv=wv_l,
            wg9=wg9_l, acat=acat_l, bbase=bbase_l, bthr=bthr_l,
            ones_in=ones_c, e16_in=e16_c, cos_in=cos_l, ss_in=ss_l,
            alA_in=(al1 if first else alC_c), alB_in=(al0 if first else alD_c),
            alC_in=alC_c, alD_in=alD_c, mask_in=mask_l))
    return in_maps


def kernel(**inputs):
    nc = build_nc()
    in_maps = host_inputs(**inputs)
    res = run_bass_kernel_spmd(nc, in_maps, core_ids=list(range(NCORE)))
    out = np.empty((Bb, S, D), dtype=np.float32)
    for core in range(NCORE):
        y = np.asarray(res.results[core]["y2"])
        seq = core // 4
        c0 = (core % 4) * TOK
        out[seq, c0:c0 + TOK] = y.reshape(D, TOK).T
    return out


# version 17
# speedup vs baseline: 1.1062x; 1.1062x over previous
import contextlib
import numpy as np
import ml_dtypes
import concourse.bass as bass
import concourse.mybir as mybir
from concourse.bass_utils import run_bass_kernel_spmd

BF = ml_dtypes.bfloat16
F32 = mybir.dt.float32
BF16 = mybir.dt.bfloat16
AF = mybir.ActivationFunctionType
OP = mybir.AluOpType

NCORE = 8
Bb, S, D = 2, 8192, 4096
TOK = 2048
HALO = 2560
NCH = 32
H, DH = 32, 128
SCALE = float(1.0 / np.sqrt(128.0))
NW = 9
NQS = 4
EPS = 1e-5
MASKVAL = -30000.0


class Sem:
    def __init__(self, h, step):
        self.h = h
        self.c = 0
        self.step = step

    def now(self):
        return (self, self.c)


class Prog:
    ENG = ("sync", "gpsimd", "tensor", "scalar", "vector")

    def __init__(self):
        self.ops = {k: [] for k in self.ENG}

    def emit(self, eng, build=None, waits=(), inc=None):
        if inc is not None:
            inc.c += inc.step
        self.ops[eng].append((build, tuple(waits), inc))

    def replay(self, eng_name, e):
        auto_drain = eng_name in ("vector", "scalar")
        prev_real = False
        for build, waits, inc in self.ops[eng_name]:
            for s, v in waits:
                if v > 0:
                    e.wait_ge(s.h, v)
            if build is not None and auto_drain and prev_real:
                e.drain()
            ins = build(e) if build is not None else None
            if build is not None:
                prev_real = True
            if inc is not None:
                if ins is None:
                    ins = e.drain()
                ins.then_inc(inc.h, inc.step)


class DmaQ:

    def __init__(self, prog, eng, nc, es, name, n=8):
        self.p = prog
        self.eng = eng
        self.sems = [Sem(es.enter_context(nc.semaphore(f"{name}{i}")), 16)
                     for i in range(n)]
        self.i = 0

    def dma(self, build, waits=()):
        s = self.sems[self.i]
        self.i = (self.i + 1) % len(self.sems)
        w = list(waits) + ([(s, s.c)] if s.c else [])
        self.p.emit(self.eng, build, waits=w, inc=s)
        return (s, s.c)

    def all(self):
        return [(s, s.c) for s in self.sems if s.c]


def build_nc(dbg=False):
    nc = bass.Bass("TRN2", target_bir_lowering=False, dynamic_dma_scratch_size=16384)

    def din(name, shape, dt):
        return nc.dram_tensor(name, list(shape), dt, kind="ExternalInput")

    def dscr(name, shape, dt):
        return nc.dram_tensor(name, list(shape), dt,
                              kind="ExternalOutput" if dbg else "Internal")

    xin = din("xin", (NCH, 128, HALO), F32)
    wq = din("wq", (NCH, 128, D), F32)
    wo = din("wo", (NCH, 128, D), F32)
    wb = din("wb", (NCH, 128, D + 128), F32)
    wk = din("wk", (128, D), F32)
    wv = din("wv", (128, D), F32)
    wg9 = din("wg9", (128, NCH * 33), F32)
    acat = din("acat", (128, D), F32)
    bbase = din("bbase", (128, 32), F32)
    bthr = din("bthr", (1, 1), F32)
    ones_in = din("ones_in", (128, 128), BF16)
    e16_in = din("e16_in", (8, 128), BF16)
    cos_in = din("cos_in", (128, HALO), BF16)
    ss_in = din("ss_in", (128, HALO), BF16)
    alA_in = din("alA_in", (128, 256), BF16)
    alB_in = din("alB_in", (128, 256), BF16)
    alC_in = din("alC_in", (128, 256), BF16)
    alD_in = din("alD_in", (128, 256), BF16)
    mask_in = din("mask_in", (128, 20), F32)

    s_xn = dscr("s_xn", (NCH, 128, HALO), BF16)
    s_q = dscr("s_q", (H, 128, TOK), BF16)
    s_at = dscr("s_at", (NCH, 128, TOK), BF16)
    s_y1 = dscr("s_y1", (NCH, 128, TOK), BF16)
    s_ff = dscr("s_ff", (NCH, 128, TOK), BF16)
    s_xb = dscr("s_xb", (NCH, 128, TOK), BF16)
    y2 = nc.dram_tensor("y2", [NCH, 128, TOK], F32, kind="ExternalOutput")

    p = Prog()
    E = p.emit

    with contextlib.ExitStack() as es:
        def sb(name, shape, dt):
            return es.enter_context(nc.sbuf_tensor(name, list(shape), dt))

        BIG = sb("BIG", (128, 65536), BF16)
        wcol = [sb(f"wcol{i}", (128, D + 128), BF16) for i in range(2)]
        ones_sb = sb("ones_sb", (128, 128), BF16)
        e16_sb = sb("e16_sb", (8, 128), BF16)
        cos_sb = sb("cos_sb", (128, HALO), BF16)
        ss_sb = sb("ss_sb", (128, HALO), BF16)
        alA = sb("alA", (128, 256), BF16)
        alB = sb("alB", (128, 256), BF16)
        alC = sb("alC", (128, 256), BF16)
        alD = sb("alD", (128, 256), BF16)
        mask_sb = sb("mask_sb", (128, 20), F32)
        bbase_sb = sb("bbase_sb", (128, 32), F32)
        bthr_sb = sb("bthr_sb", (1, 1), F32)
        bthr2 = sb("bthr2", (33, 1), F32)
        eps_sb = sb("eps_sb", (128, 1), F32)
        wg9_sb = sb("wg9_sb", (128, NCH * 33), BF16)
        acat_sb = sb("acat_sb", (128, D), BF16)
        Kr = sb("Kr", (128, HALO), BF16)
        Vt = sb("Vt", (128, HALO), BF16)
        qh = sb("qh", (128, TOK), BF16)
        estage = [sb(f"estage{i}", (128, 512), BF16) for i in range(2)]
        P_sb = [sb(f"P{i}", (128, 2048), BF16) for i in range(2)]
        obuf = [sb(f"obuf{i}", (128, 512), BF16) for i in range(2)]
        T1 = sb("T1", (128, 512), F32)
        T2 = sb("T2", (128, 512), F32)
        T3 = sb("T3", (128, 512), F32)
        rsA = sb("rsA", (128, 512), BF16)
        nmA = sb("nmA", (128, 512), BF16)
        sqr = [sb(f"sqr{i}", (128, 512), BF16) for i in range(2)]
        ar = [sb(f"ar{i}", (128, 512), BF16) for i in range(2)]
        xr = [sb(f"xr{i}", (128, 512), BF16) for i in range(2)]
        cos_f = cos_sb.ap().bitcast(F32)
        ss_f = ss_sb.ap().bitcast(F32)
        kr_f = Kr.ap().bitcast(F32)
        T2b = T2.ap().bitcast(BF16)
        gvA = cos_f[:, 0:512]
        gvB = cos_f[:, 512:1024]
        gvC = ss_f[:, 0:512]
        WN = ss_sb[0:8, 2048:2560]
        ebf = P_sb[0][0:8, 512:1024]
        tbf = P_sb[0][0:1, 1024:1536]
        webf = Vt[0:8, 0:512]
        wful = P_sb[0][:, 0:512]
        hw_all = P_sb[1][:]
        ystage = [kr_f[:, 0:512], kr_f[:, 512:1024]]
        rsB = Vt[:, 2048:2560]
        nmB = cos_sb[:, 2048:2560]

        psA = es.enter_context(nc.psum_tensor("psA", [128, 2048], F32))
        psB = es.enter_context(nc.psum_tensor("psB", [128, 512], F32))
        psC = es.enter_context(nc.psum_tensor("psC", [128, 512], F32))
        psD = es.enter_context(nc.psum_tensor("psD", [128, 512], F32))
        psE = es.enter_context(nc.psum_tensor("psE", [128, 512], F32))

        sPE = Sem(es.enter_context(nc.semaphore("sPE")), 1)
        sAC = Sem(es.enter_context(nc.semaphore("sAC")), 1)
        sDV = Sem(es.enter_context(nc.semaphore("sDV")), 1)
        syq = DmaQ(p, "sync", nc, es, "sy", 8)
        gpq = DmaQ(p, "gpsimd", nc, es, "gp", 8)

        state = {"psbc_ac": 0, "psde_ac": 0, "stats_g": [], "g_ln2": [0, 0]}
        ar_g = [[], []]
        xr_g = [[], []]
        sq_g = [[], []]
        ys_g = [[], []]

        for dst, src in ((ones_sb, ones_in), (e16_sb, e16_in), (cos_sb, cos_in),
                         (ss_sb, ss_in), (alA, alA_in), (alB, alB_in),
                         (alC, alC_in), (alD, alD_in), (mask_sb, mask_in),
                         (bbase_sb, bbase), (bthr_sb, bthr)):
            syq.dma(lambda e, d=dst, s=src: e.dma_start(out=d[:], in_=s.ap()))
        consts_sy = syq.all()
        gpq.dma(lambda e: e.dma_start(out=wg9_sb[:], in_=wg9.ap()))
        wk_sb = BIG[:, 49152:53248]
        wv_sb = BIG[:, 53248:57344]
        gpq.dma(lambda e: e.dma_start(out=wk_sb, in_=wk.ap()))
        gpq.dma(lambda e: e.dma_start(out=wv_sb, in_=wv.ap()))
        consts_gp = gpq.all()
        E("scalar", lambda e: e.mul(bthr2[0:1, :], bthr_sb[:], 0.5),
          waits=consts_sy, inc=sAC)
        E("vector", lambda e: e.tensor_copy(bthr2[32:33, :], bthr2[0:1, :]),
          waits=[sAC.now()], inc=sDV)
        E("vector", lambda e: e.memset(eps_sb[:], EPS), inc=sDV)

        def stats_finalize(rstd_bf, nmu_bf, mm_wait, pb=None, pc=None, gk2=None):
            pb = psB if pb is None else pb
            pc = psC if pc is None else pc
            E("scalar", lambda e: e.mul(T1[:], pb[:], -1.0 / D),
              waits=[mm_wait] + state["stats_g"], inc=sAC)
            E("scalar", lambda e: e.activation(T2[:], pb[:], AF.Square, scale=1.0 / D), inc=sAC)
            E("scalar", lambda e: e.mul(T3[:], pc[:], 1.0 / D), inc=sAC)
            if gk2 is not None:
                state["g_ln2"][gk2] = sAC.c
            else:
                state["psbc_ac" if pb is psB else "psde_ac"] = sAC.c
            aw = sAC.now()
            E("vector", lambda e: e.tensor_sub(T3[:], T3[:], T2[:]), waits=[aw], inc=sDV)
            vw = sDV.now()
            E("scalar", lambda e: e.activation(T2[:], T3[:], AF.Sqrt, bias=eps_sb.ap()),
              waits=[vw], inc=sAC)
            aw2 = sAC.now()
            E("vector", lambda e: e.reciprocal(T3[:], T2[:]), waits=[aw2])
            E("vector", lambda e, r=rstd_bf: e.tensor_copy(r[:], T3[:]))
            E("vector", lambda e, m=nmu_bf: e.tensor_tensor(m[:], T1[:], T3[:], op=OP.mult), inc=sDV)
            state["stats_g"] = [sDV.now()]
            return sDV.now()

        def stats_finalize32(rstd_bf, nmu_bf, pbrow, pcrow, waits0):
            E("scalar", lambda e: e.mul(T1[0:32, :], pbrow, -1.0 / D),
              waits=list(waits0) + state["stats_g"], inc=sAC)
            E("scalar", lambda e: e.activation(T2[0:32, :], pbrow, AF.Square, scale=1.0 / D), inc=sAC)
            E("scalar", lambda e: e.mul(T3[0:32, :], pcrow, 1.0 / D), inc=sAC)
            state["fin_read"] = sAC.c
            aw = sAC.now()
            E("vector", lambda e: e.tensor_sub(T3[0:32, :], T3[0:32, :], T2[0:32, :]),
              waits=[aw], inc=sDV)
            vw = sDV.now()
            E("scalar", lambda e: e.activation(T2[0:32, :], T3[0:32, :], AF.Sqrt,
                                               bias=eps_sb[0:32, :]), waits=[vw], inc=sAC)
            aw2 = sAC.now()
            E("vector", lambda e: e.reciprocal(T3[0:32, :], T2[0:32, :]), waits=[aw2])
            E("vector", lambda e, r=rstd_bf: e.tensor_copy(r[0:32, :], T3[0:32, :]))
            E("vector", lambda e, m=nmu_bf: e.tensor_tensor(m[0:32, :], T1[0:32, :],
                                                            T3[0:32, :], op=OP.mult))
            for b in (32, 64, 96):
                E("vector", lambda e, r=rstd_bf, b=b: e.tensor_copy(r[b:b + 32, :], r[0:32, :]))
                E("vector", lambda e, m=nmu_bf, b=b: e.tensor_copy(m[b:b + 32, :], m[0:32, :]))
            E("vector", None, inc=sDV)
            state["stats_g"] = [sDV.now()]
            return sDV.now()

        def rope(dst_slice, src_tile, col0, tmp):
            E("vector", lambda e: e.tensor_copy(tmp[0:64, :], src_tile[64:128, :]))
            E("vector", lambda e: e.tensor_copy(tmp[64:128, :], src_tile[0:64, :]))
            E("vector", lambda e: e.tensor_tensor(tmp[:], tmp[:], ss_sb[:, col0:col0 + 512], op=OP.mult))
            E("vector", lambda e: e.tensor_tensor(dst_slice, src_tile[:], cos_sb[:, col0:col0 + 512], op=OP.mult))
            E("vector", lambda e: e.tensor_tensor(dst_slice, dst_slice, tmp[:], op=OP.add))

        xa_view = [BIG[:, 0:16384], BIG[:, 16384:32768]]
        xnbuf = BIG[:, 32768:49152]
        xa_free = [0, 0]
        xn_free_pe = 0
        xn_wr_tok = []
        xb_wr = [[], []]
        for s in range(5):
            t0 = 512 * s
            par = s % 2
            lw = []
            for g in range(4):
                lw.append(gpq.dma(
                    lambda e, s=s, g=g, par=par: e.dma_start(
                        out=xa_view[par].rearrange("p (c q) -> p c q", c=NCH)[:, g * 8:(g + 1) * 8, :],
                        in_=xin.ap()[g * 8:(g + 1) * 8, :, 512 * s:512 * s + 512].transpose([1, 0, 2])),
                    waits=(consts_gp if s == 0 else [])
                          + ([(sPE, xa_free[par])] if s >= 2 else [])
                          + xb_wr[par]))
            xb_wr[par] = []
            off0 = max(0, 512 * s - 256)
            t0o = max(0, 256 - 512 * s)
            ln = min(512 * s + 512, 2304) - max(512 * s, 256)
            for g in range(4):
                xb_wr[par].append(syq.dma(
                    lambda e, g=g, par=par, t0o=t0o, off0=off0, ln=ln: e.dma_start(
                        out=s_xb.ap()[g * 8:(g + 1) * 8, :, off0:off0 + ln].transpose([1, 0, 2]),
                        in_=xa_view[par].rearrange("p (c q) -> p c q", c=NCH)[:, g * 8:(g + 1) * 8, t0o:t0o + ln]),
                    waits=list(lw)))
            for c in range(NCH):
                sq = sqr[c % 2]
                E("vector", lambda e, c=c, par=par, q=sq: e.tensor_tensor(
                    q[:], xa_view[par][:, c * 512:(c + 1) * 512],
                    xa_view[par][:, c * 512:(c + 1) * 512], op=OP.mult),
                  waits=list(lw) + sq_g[c % 2], inc=sDV)
                sqw = sDV.now()
                E("tensor", lambda e, c=c, par=par: e.matmul(
                    psB[:], ones_sb[:], xa_view[par][:, c * 512:(c + 1) * 512],
                    start=(c == 0), stop=(c == NCH - 1)),
                  waits=list(lw) + (consts_sy + [(sAC, state["psbc_ac"])] if c == 0 else []))
                E("tensor", lambda e, q=sq, c=c: e.matmul(
                    psC[:], ones_sb[:], q[:], start=(c == 0), stop=(c == NCH - 1)),
                  waits=[sqw], inc=sPE)
                sq_g[c % 2] = [sPE.now()]
            stw = stats_finalize(rsA, nmA, sPE.now())
            for c in range(NCH):
                E("vector", lambda e, c=c, par=par: e.tensor_tensor(
                    xnbuf[:, c * 512:(c + 1) * 512], xa_view[par][:, c * 512:(c + 1) * 512],
                    rsA[:], op=OP.mult),
                  waits=([stw, (sPE, xn_free_pe)] + xn_wr_tok if c == 0 else []))
                E("vector", lambda e, c=c: e.tensor_tensor(
                    xnbuf[:, c * 512:(c + 1) * 512], xnbuf[:, c * 512:(c + 1) * 512],
                    nmA[:], op=OP.add), inc=sDV)
                xnw = sDV.now()
                E("tensor", lambda e, c=c: e.matmul(
                    psD[:], wk_sb[:, c * 128:(c + 1) * 128], xnbuf[:, c * 512:(c + 1) * 512],
                    start=(c == 0), stop=(c == NCH - 1)),
                  waits=[xnw] + (consts_gp if (s == 0 and c == 0) else []))
            for t4 in range(4):
                for c in range(NCH):
                    E("tensor", lambda e, t4=t4, c=c: e.matmul(
                        psE[:, t4 * 128:(t4 + 1) * 128],
                        xnbuf[:, c * 512 + t4 * 128:c * 512 + t4 * 128 + 128],
                        wv_sb[:, c * 128:(c + 1) * 128],
                        start=(c == 0), stop=(c == NCH - 1)))
            E("tensor", None, inc=sPE)
            kvw = sPE.now()
            xn_free_pe = sPE.c
            xa_free[par] = sPE.c
            E("scalar", lambda e: e.copy(sqr[0][:], psD[:]),
              waits=[kvw] + sq_g[0] + sq_g[1], inc=sAC)
            E("vector", None, waits=[sAC.now()])
            rope(Kr[:, t0:t0 + 512], sqr[0], t0, sqr[1])
            for t4 in range(4):
                E("vector", lambda e, t4=t4, s=s: e.tensor_copy(
                    Vt[:, (4 * s + t4) * 128:(4 * s + t4) * 128 + 128],
                    psE[:, t4 * 128:(t4 + 1) * 128]),
                  inc=(sDV if t4 == 3 else None))
            kv_done = sDV.now()
            sq_g[0] = sq_g[1] = [sDV.now()]
            xn_wr_tok = []
            for g in range(4):
                xn_wr_tok.append(syq.dma(
                    lambda e, s=s, g=g: e.dma_start(
                        out=s_xn.ap()[g * 8:(g + 1) * 8, :, 512 * s:512 * s + 512].transpose([1, 0, 2]),
                        in_=xnbuf.rearrange("p (c q) -> p c q", c=NCH)[:, g * 8:(g + 1) * 8, :]),
                    waits=[sDV.now()]))
        kv_all = kv_done
        a_pe_done = sPE.now()

        xn_ld = []
        for g in range(4):
            xn_ld.append(syq.dma(
                lambda e, g=g: e.dma_start(
                    out=BIG.rearrange("p (c q) -> p c q", c=NCH)[:, g * 8:(g + 1) * 8, :],
                    in_=s_xn.ap()[g * 8:(g + 1) * 8, :, 256:256 + TOK].transpose([1, 0, 2])),
                waits=syq.all() + [kv_all, a_pe_done]))
        XN = BIG

        o_sbv = T1.ap().bitcast(BF16)
        rd_bfv = T3.ap().bitcast(BF16)
        o_sb = [o_sbv[:, 0:512], o_sbv[:, 512:1024]]
        rd_bf = [rd_bfv[:, 0:512], rd_bfv[:, 512:1024]]
        wcol_free = [0, 0]
        est_free = [None, None]
        psb_evac = [0, 0]
        q_written = []
        qhs = [qh.ap(), acat_sb.ap()[:, 0:TOK]]
        wl_pre = {}
        qw_pre = {}
        b2s = {"sig_slot": [0, 0], "smm_tok": [None, None], "rc_free": [0, 0],
               "oev_free": [0, 0], "pp_free": [0, 0], "qh_free": [0, 0],
               "bst_g": [[], []]}
        blend_toks = [[] for _ in range(H)]

        def emit_b1_ot(ot):
            par = ot % 2
            wl = wl_pre.pop(ot, None)
            if wl is None:
                wl = gpq.dma(lambda e, ot=ot, par=par: e.dma_start(
                    out=wcol[par][:, 0:D], in_=wq.ap()[ot]),
                    waits=[(sPE, wcol_free[par])] if ot >= 2 else [])
            qtoks = []
            for qs in range(NQS):
                yield
                bank = psB if qs % 2 == 0 else psC
                for c in range(NCH):
                    E("tensor", lambda e, c=c, qs=qs, par=par, bank=bank: e.matmul(
                        bank[:],
                        wcol[par][:, c * 128:(c + 1) * 128],
                        XN[:, c * 2048 + qs * 512: c * 2048 + qs * 512 + 512],
                        start=(c == 0), stop=(c == NCH - 1)),
                      waits=([wl] + xn_ld if c == 0 and qs == 0 else [])
                            + ([(sAC, psb_evac[qs % 2])] if c == 0 and (ot > 0 or qs >= 2) else []))
                E("tensor", None, inc=sPE)
                mmw = sPE.now()
                if qs == NQS - 1:
                    wcol_free[par] = sPE.c
                ei = (ot * NQS + qs) % 2
                E("scalar", lambda e, ei=ei, bank=bank: e.copy(estage[ei][:], bank[:]),
                  waits=[mmw] + ([est_free[ei]] if est_free[ei] else []),
                  inc=sAC)
                psb_evac[qs % 2] = sAC.c
                E("vector", None, waits=[sAC.now()] + sq_g[0])
                rope(estage[ei][:], estage[ei], 256 + qs * 512, sqr[0])
                E("vector", None, inc=sDV)
                sq_g[0] = [sDV.now()]
                tok = syq.dma(lambda e, ot=ot, qs=qs, ei=ei: e.dma_start(
                    out=s_q.ap()[ot, :, qs * 512:qs * 512 + 512], in_=estage[ei][:]),
                    waits=[sDV.now()])
                est_free[ei] = tok
                qtoks.append(tok)
                if qs == 1 and ot + 1 < H:
                    npar = (ot + 1) % 2
                    wl_pre[ot + 1] = gpq.dma(lambda e, o1=ot + 1, npar=npar: e.dma_start(
                        out=wcol[npar][:, 0:D], in_=wq.ap()[o1]),
                        waits=[(sPE, wcol_free[npar])])
            q_written.append(qtoks)

        def emit_b2_head(h):
            yield
            par = h % 2
            myq = qhs[par]
            qw = qw_pre.pop(h, None)
            if qw is None:
                qw = syq.dma(lambda e, par=par, h=h: e.dma_start(
                    out=qhs[par], in_=s_q.ap()[h]),
                    waits=list(q_written[h]) + ([(sPE, b2s["qh_free"][par])] if h >= 2 else []))
            sigs = {}
            dn = [psA[:, 1024:1536], psA[:, 1536:2048]]

            def s_mms(w, cpair):
                qlo = max(0, 256 * w - 256)
                qhi = min(TOK, 256 * w + 256)
                wlen = qhi - qlo
                boff = qlo - (256 * w - 256)
                for c in (cpair, cpair + 1):
                    kt = 2 * w + c
                    E("tensor", lambda e, c=c, kt=kt, qlo=qlo, wlen=wlen, mq=myq: e.matmul(
                        psA[:, (c % 2) * 512: (c % 2) * 512 + wlen],
                        Kr[:, kt * 128:kt * 128 + 128],
                        mq[:, qlo:qlo + wlen],
                        start=True, stop=True),
                      waits=([qw, kv_all] if c == cpair else [])
                            + [(sAC, b2s["sig_slot"][c % 2])],
                      inc=sPE)
                    b2s["smm_tok"][c % 2] = sPE.now()
                return boff, wlen

            def sig_half(w, cpair, boff, wlen, pp):
                for c in (cpair, cpair + 1):
                    kt = 2 * w + c
                    masked = (w == 0 or w == NW - 1)
                    E("scalar", lambda e, c=c, kt=kt, boff=boff, wlen=wlen, pp=pp,
                                m=masked: e.activation(
                        P_sb[pp][:, c * 512 + boff: c * 512 + boff + wlen],
                        psA[:, (c % 2) * 512: (c % 2) * 512 + wlen],
                        AF.Sigmoid, scale=SCALE,
                        **(dict(bias=mask_sb[:, kt:kt + 1]) if m else {})),
                      waits=[b2s["smm_tok"][c % 2]]
                            + ([(sPE, b2s["pp_free"][pp])] if c == cpair else []),
                      inc=sAC)
                    b2s["sig_slot"][c % 2] = sAC.c

            for w in range(NW + 1):
                yield
                if w >= 1:
                    wp = w - 1
                    sw, pp_p, boff_p, wlen_p = sigs[wp]
                    ds = wp % 2
                    ob = [psD, psE][ds]
                if w <= NW - 1:
                    pp = (h * NW + w) % 2
                    boff, wlen = s_mms(w, 0)
                    sig_half(w, 0, boff, wlen, pp)
                if w >= 1:
                    for c in range(4):
                        E("tensor", lambda e, c=c, pp=pp_p, boff=boff_p, wlen=wlen_p, ds=ds: e.matmul(
                            dn[ds][:, boff:boff + wlen], ones_sb[:],
                            P_sb[pp][:, c * 512 + boff: c * 512 + boff + wlen],
                            start=(c == 0), stop=(c == 3)),
                          waits=[sw, (sDV, b2s["rc_free"][ds])] if c == 0 else [],
                          inc=(sPE if c == 3 else None))
                    dn_tok = sPE.now()
                if w <= NW - 1:
                    boff, wlen = s_mms(w, 2)
                    if w == NW - 1:
                        b2s["qh_free"][par] = sPE.c
                    sig_half(w, 2, boff, wlen, pp)
                    sigs[w] = (sAC.now(), pp, boff, wlen)
                    if w == 2 and h + 1 < H:
                        hn = h + 1
                        npar = hn % 2
                        qw_pre[hn] = syq.dma(lambda e, h1=hn, npar=npar: e.dma_start(
                            out=qhs[npar], in_=s_q.ap()[h1]),
                            waits=list(q_written[hn]) + [(sPE, b2s["qh_free"][npar])])
                if w >= 1:
                    for c in range(4):
                        kt = 2 * wp + c
                        E("tensor", lambda e, c=c, kt=kt, pp=pp_p, boff=boff_p, wlen=wlen_p, ob=ob: e.matmul(
                            ob[:, boff:boff + wlen],
                            Vt[:, kt * 128:kt * 128 + 128],
                            P_sb[pp][:, c * 512 + boff: c * 512 + boff + wlen],
                            start=(c == 0), stop=(c == 3)),
                          waits=([(sAC, b2s["oev_free"][ds])] if c == 0 else []),
                          inc=(sPE if c == 3 else None))
                    b2s["pp_free"][pp_p] = sPE.c
                    dnw = sPE.now()
                    E("scalar", lambda e, boff=boff_p, wlen=wlen_p, ob=ob, ds=ds: e.copy(
                        o_sb[ds][:, boff:boff + wlen], ob[:, boff:boff + wlen]),
                      waits=[dnw, (sDV, b2s["rc_free"][ds])], inc=sAC)
                    b2s["oev_free"][ds] = sAC.c
                    oev = sAC.now()

                    def _recip(e, boff=boff_p, wlen=wlen_p, ds=ds):
                        with nc.allow_low_precision("bf16 rounding of 1/denom; product is bf16 anyway"):
                            return e.reciprocal(rd_bf[ds][:, boff:boff + wlen], dn[ds][:, boff:boff + wlen])
                    E("vector", _recip, waits=[dn_tok])
                    E("vector", lambda e, wp=wp, boff=boff_p, wlen=wlen_p, ds=ds: e.tensor_tensor(
                        obuf[wp % 2][:, boff:boff + wlen], o_sb[ds][:, boff:boff + wlen],
                        rd_bf[ds][:, boff:boff + wlen], op=OP.mult), waits=[oev], inc=sDV)
                    b2s["rc_free"][ds] = sDV.c
                    if wp >= 1:
                        m = wp - 1
                        bi = m % 2
                        a_new, a_old = (alA, alB) if m == 0 else (alC, alD)
                        E("vector", lambda e, wp=wp, a_old=a_old: e.tensor_tensor(
                            T2b[:, 0:256], obuf[(wp - 1) % 2][:, 256:512], a_old[:], op=OP.mult))
                        E("vector", lambda e, wp=wp, a_new=a_new: e.tensor_tensor(
                            T2b[:, 256:512], obuf[wp % 2][:, 0:256], a_new[:], op=OP.mult))
                        E("vector", lambda e, bi=bi: e.tensor_tensor(
                            T2b[:, 512 + 256 * bi: 768 + 256 * bi],
                            T2b[:, 0:256], T2b[:, 256:512], op=OP.add),
                          waits=b2s["bst_g"][bi], inc=sDV)
                        b2s["bst_g"][bi] = [syq.dma(
                            lambda e, h=h, m=m, bi=bi: e.dma_start(
                                out=s_q.ap()[h, :, 256 * m:256 * m + 256],
                                in_=T2b[:, 512 + 256 * bi: 768 + 256 * bi]),
                            waits=[sDV.now(), qw])]
                        blend_toks[h].append(b2s["bst_g"][bi][0])

        LAG = 2
        acat_ld = []
        b1_pe_guard = [0]
        gens = []

        def _drive(g):
            try:
                next(g)
                return True
            except StopIteration:
                return False

        active_b1 = None
        active_b2 = None
        for k in range(H + LAG):
            if k < H:
                active_b1 = emit_b1_ot(k)
                next(active_b1)
            if k >= LAG:
                active_b2 = emit_b2_head(k - LAG)
                next(active_b2)
            sched = [0, 1, 1, 1, 0, 1, 1, 0, 1, 1, 1, 0, 1, 1, 1, 1]
            for pick in sched:
                if pick == 0:
                    if active_b1 is not None and not _drive(active_b1):
                        active_b1 = None
                else:
                    if active_b2 is not None and not _drive(active_b2):
                        active_b2 = None
            while active_b1 is not None and _drive(active_b1):
                pass
            if k == H - 1:
                b1_pe_guard[0] = sPE.c
                for g in range(3):
                    toks = [t for hh in range(8 * g, 8 * g + 8) for t in blend_toks[hh]]
                    acat_ld.append(syq.dma(
                        lambda e, g=g: e.dma_start(
                            out=BIG.rearrange("p (c q) -> p c q", c=NCH)[:, g * 8:(g + 1) * 8, :],
                            in_=s_q.ap()[g * 8:(g + 1) * 8, :, :].transpose([1, 0, 2])),
                        waits=toks + [(sPE, b1_pe_guard[0])]))
            while active_b2 is not None and _drive(active_b2):
                pass
        toks = [t for hh in range(24, 32) for t in blend_toks[hh]]
        acat_ld.append(syq.dma(
            lambda e: e.dma_start(
                out=BIG.rearrange("p (c q) -> p c q", c=NCH)[:, 24:32, :],
                in_=s_q.ap()[24:32, :, :].transpose([1, 0, 2])),
            waits=toks + [(sPE, b1_pe_guard[0])]))
        acat_w = gpq.dma(lambda e: e.dma_start(out=acat_sb[:], in_=acat.ap()),
                         waits=[(sPE, b2s["qh_free"][1])])
        state["psbc_ac"] = sAC.c
        state["psde_ac"] = sAC.c

        E("vector", lambda e: e.memset(psB[:], 0.0),
          waits=[(sAC, psb_evac[0]), (sAC, psb_evac[1])])
        E("vector", lambda e: e.memset(psC[:], 0.0), inc=sDV)
        b3ms = sDV.now()
        wcol_free = [sPE.c, sPE.c]
        est_free = [None, None]
        psa_evac = [sAC.c] * 4
        at_written = [[None] * NQS for _ in range(NCH)]
        pend3 = [None]
        b3_half_pe = [None, None]

        arB4 = ([Kr[:, i * 512:(i + 1) * 512] for i in range(5)]
                + [Vt[:, 512 + i * 512:1024 + i * 512] for i in range(3)])
        arB4_g = [[] for _ in range(8)]
        xrR = [qh[:, i * 512:(i + 1) * 512] for i in range(4)]
        xrR_g = [[] for _ in range(4)]
        sqb = [obuf[0][:], obuf[1][:]]
        sqb_g = [[], []]
        y1_written = [None] * NQS
        y1n_q = [None] * NQS
        fin32_read = [None] * NQS
        fin2_read = [None] * NQS
        b4st = {"yw": {}, "ms": {}}

        def b4_partA(qs):
            qo = qs * 512
            stw = stats_finalize32(rsA, nmA, psB[32 * qs:32 * qs + 32, :],
                                   psC[32 * qs:32 * qs + 32, :],
                                   [b3_half_pe[qs // 2]])
            fin32_read[qs] = state["fin_read"]
            E("vector", lambda e, qs=qs: e.memset(psB[32 * qs:32 * qs + 32, :], 0.0),
              waits=[(sAC, fin32_read[qs])])
            E("vector", lambda e, qs=qs: e.memset(psC[32 * qs:32 * qs + 32, :], 0.0),
              inc=sDV)
            b4st["ms"][qs] = sDV.now()
            aws, xws, yw = {}, {}, {}
            b4st["yw"] = yw

            def _lda(c, qs=qs, qo=qo):
                slot = c % 8
                aws[c] = syq.dma(lambda e, c=c, slot=slot, qs=qs, qo=qo: e.dma_start(
                    out=arB4[slot], in_=s_at.ap()[c, :, qo:qo + 512]),
                    waits=[at_written[c][qs]] + arB4_g[slot])
                arB4_g[slot] = []

            def _ldx(c, qs=qs, qo=qo):
                xslot = c % 4
                xws[c] = syq.dma(lambda e, c=c, xslot=xslot, qo=qo: e.dma_start(
                    out=xrR[xslot], in_=s_xb.ap()[c, :, qo:qo + 512]),
                    waits=xrR_g[xslot]
                          + (xn_ld + [b3_half_pe[0]] if qs == 0 and c < 4 else []))
                xrR_g[xslot] = []

            for c in range(8):
                _lda(c)
            for c in range(4):
                _ldx(c)
            for c in range(NCH):
                slot, xslot = c % 8, c % 4
                aw, xw = aws[c], xws[c]
                dst = XN[:, c * 2048 + qo: c * 2048 + qo + 512]
                E("vector", lambda e, slot=slot, d=dst: e.tensor_tensor(
                    d, arB4[slot], rsA[:], op=OP.mult),
                  waits=[aw, stw] + ([b3_half_pe[qs // 2]] if c == 0 else []))
                E("vector", lambda e, d=dst: e.tensor_tensor(d, d, nmA[:], op=OP.add))
                E("vector", lambda e, xslot=xslot, d=dst: e.tensor_tensor(
                    d, d, xrR[xslot], op=OP.add), waits=[xw], inc=sDV)
                arB4_g[slot].append(sDV.now())
                xrR_g[xslot].append(sDV.now())
                yw[c] = sDV.now()
                if c + 8 < NCH:
                    _lda(c + 8)
                if c + 4 < NCH:
                    _ldx(c + 4)
                syq.dma(lambda e, c=c, qo=qo, d=dst: e.dma_start(
                    out=s_y1.ap()[c, :, qo:qo + 512], in_=d), waits=[yw[c]])

        def b4_batch(qs, c0):
            qo = qs * 512
            for c in range(c0, c0 + 8):
                dst = XN[:, c * 2048 + qo: c * 2048 + qo + 512]
                sq = sqb[c % 2]
                E("scalar", lambda e, q=sq, d=dst: e.activation(q, d, AF.Square),
                  waits=[b4st["yw"][c]] + sqb_g[c % 2], inc=sAC)
                sqw = sAC.now()
                E("tensor", lambda e, d=dst, qs=qs: e.matmul(
                    psB[32 * qs:32 * qs + 32, :], ones_sb[:, 0:32], d,
                    start=False, stop=False, skip_group_check=True,
                    tile_position=(0, 32 * qs)),
                  waits=[b4st["yw"][c]] + ([b4st["ms"][qs]] if c == c0 else []))
                E("tensor", lambda e, q=sq, qs=qs: e.matmul(
                    psC[32 * qs:32 * qs + 32, :], ones_sb[:, 0:32], q,
                    start=False, stop=False, skip_group_check=True,
                    tile_position=(0, 32 * qs)),
                  waits=[sqw], inc=sPE)
                sqb_g[c % 2] = [sPE.now()]

        def b4_partB(qs):
            qo = qs * 512
            stw2 = stats_finalize32(rsB, nmB, psB[32 * qs:32 * qs + 32, :],
                                    psC[32 * qs:32 * qs + 32, :], [sPE.now()])
            fin2_read[qs] = state["fin_read"]
            wr_done = syq.all()
            for c in range(NCH):
                dst = XN[:, c * 2048 + qo: c * 2048 + qo + 512]
                E("vector", lambda e, d=dst: e.tensor_tensor(d, d, rsB[:], op=OP.mult),
                  waits=([stw2] + wr_done) if c == 0 else [])
                E("vector", lambda e, d=dst: e.tensor_tensor(d, d, nmB[:], op=OP.add),
                  inc=(sDV if c == NCH - 1 else None))
            y1n_q[qs] = sDV.now()
            y1_written[qs] = wr_done

        def b4_sched3(ot):
            if ot == 0:
                b4_partA(0)
            elif ot in (4, 6, 8, 10):
                b4_batch(0, 8 * (ot - 4) // 2)
            elif ot == 11:
                b4_partB(0)
                b4_partA(1)
            elif ot in (14, 16, 18, 20):
                b4_batch(1, 8 * (ot - 14) // 2)
            elif ot == 21:
                b4_partB(1)
            elif 22 <= ot <= 26:
                gstep(0)
                gstep(0)
            elif 27 <= ot <= 31:
                gstep(1)
                gstep(1)

        GA = psA[:, 0:512]
        HB = psA[:, 512:1024]
        hw_rdy = [None] * 4
        gate_psde_ac = [0]
        g5 = {"hw_dv": 0, "ga_free": sAC.c}

        def gate_gen(qs):
            qo = qs * 512
            for c in range(NCH):
                E("tensor", lambda e, c=c, qo=qo: e.matmul(
                    psA[0:33, 0:512], wg9_sb[:, c * 33:(c + 1) * 33],
                    XN[:, c * 2048 + qo: c * 2048 + qo + 512],
                    start=(c == 0), stop=(c == NCH - 1)),
                  waits=(([y1n_q[qs], (sAC, g5["ga_free"]),
                           (sAC, psa_evac[0]), (sAC, psa_evac[1]),
                           (sDV, g5["hw_dv"])] + consts_gp)
                         if c == 0 else []))
            yield
            for c in range(NCH):
                E("tensor", lambda e, c=c, qo=qo: e.matmul(
                    HB, acat_sb[:, c * 128:(c + 1) * 128],
                    XN[:, c * 2048 + qo: c * 2048 + qo + 512],
                    start=(c == 0), stop=(c == NCH - 1)),
                  waits=([acat_w] if c == 0 else []),
                  inc=(sPE if c == NCH - 1 else None))
            gh_mm = sPE.now()
            yield
            E("scalar", lambda e: e.activation(gvA[0:8, :], psA[0:8, 0:512], AF.Exp),
              waits=[gh_mm], inc=sAC)
            E("scalar", lambda e: e.activation(gvA[32:33, :], psA[32:33, 0:512], AF.Tanh,
                                               scale=0.5, bias=bthr2[32:33, :]), inc=sAC)
            g5["ga_free"] = sAC.c
            gw = sAC.now()
            E("vector", lambda e: e.tensor_scalar(gvA[32:33, :], gvA[32:33, :], 0.0625, 0.0625,
                                                  op0=OP.mult, op1=OP.add), waits=[gw])
            E("vector", lambda e: e.tensor_copy(ebf, gvA[0:8, :]))
            E("vector", lambda e: e.tensor_copy(tbf, gvA[32:33, :]), inc=sDV)
            bfw = sDV.now()
            yield
            E("tensor", lambda e: e.matmul(psA[0:8, 0:512], ones_sb[0:8, 0:8], ebf,
                                           start=True, stop=True),
              waits=[bfw, (sAC, g5["ga_free"])])
            E("tensor", lambda e: e.matmul(psD[0:8, :], ones_sb[0:1, 0:8], tbf,
                                           start=True, stop=True,
                                           skip_group_check=True),
              waits=[(sAC, gate_psde_ac[0])], inc=sPE)
            sbw = sPE.now()
            yield
            E("scalar", lambda e: e.copy(gvB[0:8, :], psD[0:8, :]), waits=[sbw], inc=sAC)
            gate_psde_ac[0] = sAC.c
            E("vector", lambda e: e.tensor_tensor(gvC[0:8, :], psA[0:8, 0:512], gvB[0:8, :],
                                                  op=OP.mult), waits=[sAC.now()])
            E("vector", lambda e: e.tensor_sub(gvA[0:8, :], gvA[0:8, :], gvC[0:8, :]))
            E("vector", lambda e: e.tensor_scalar_max(gvA[0:8, :], gvA[0:8, :], 0.0))
            E("vector", lambda e: e.tensor_copy(webf, gvA[0:8, :]), inc=sDV)
            wew = sDV.now()
            yield
            E("tensor", lambda e: e.matmul(psA[0:8, 0:512], ones_sb[0:8, 0:8], webf,
                                           start=True, stop=True), waits=[wew], inc=sPE)
            wsw = sPE.now()
            yield
            E("scalar", lambda e: e.copy(gvB[0:8, :], psA[0:8, 0:512]), waits=[wsw], inc=sAC)
            g5["ga_free"] = sAC.c
            E("vector", lambda e: e.tensor_scalar(gvC[0:8, :], gvB[0:8, :], 0.0, None,
                                                  op0=OP.is_equal), waits=[sAC.now()])
            E("vector", lambda e: e.tensor_add(gvB[0:8, :], gvB[0:8, :], gvC[0:8, :]))
            E("vector", lambda e: e.reciprocal(gvB[0:8, :], gvB[0:8, :]))
            E("vector", lambda e: e.tensor_mul(WN[:], gvA[0:8, :], gvB[0:8, :]), inc=sDV)
            wnw = sDV.now()
            yield
            E("tensor", lambda e: e.matmul(psD[:], e16_sb[:], WN[:], start=True, stop=True),
              waits=[wnw, sAC.now()] + consts_sy, inc=sPE)
            exw = sPE.now()
            yield
            E("scalar", lambda e: e.copy(wful[:], psD[:]), waits=[exw], inc=sAC)
            gate_psde_ac[0] = sAC.c
            E("vector", lambda e, qo=qo, qs=qs: e.tensor_tensor(
                hw_all[:, qo:qo + 512], HB, wful[:], op=OP.mult),
              waits=[sAC.now()], inc=sDV)
            g5["hw_dv"] = sDV.c
            hw_rdy[qs] = sDV.now()

        gq = {q: gate_gen(q) for q in range(4)}

        def gstep(q):
            g6 = gq.get(q)
            if g6 is not None:
                try:
                    next(g6)
                except StopIteration:
                    gq.pop(q, None)

        def gdrain(q):
            while q in gq:
                gstep(q)

        for half in (0, 1):
            for ot in range(NCH):
                par = ot % 2
                wl = gpq.dma(lambda e, ot=ot, par=par: e.dma_start(
                    out=wcol[par][:, 0:D], in_=wo.ap()[ot]),
                    waits=[(sPE, wcol_free[par])])
                for qs in (2 * half, 2 * half + 1):
                    for c in range(NCH):
                        E("tensor", lambda e, c=c, qs=qs, par=par: e.matmul(
                            psA[:, qs * 512:qs * 512 + 512],
                            wcol[par][:, c * 128:(c + 1) * 128],
                            XN[:, c * 2048 + qs * 512: c * 2048 + qs * 512 + 512],
                            start=(c == 0), stop=(c == NCH - 1)),
                          waits=(([wl] + (acat_ld if half == 0 and ot == 0 and qs == 0 else []))
                                 if c == 0 and qs == 2 * half else [])
                                + ([(sAC, psa_evac[qs])] if c == 0 and ot >= 1 else []))
                    if pend3[0] is not None:
                        pend3[0]()
                    E("tensor", None, inc=sPE)
                    mmw = sPE.now()
                    if qs == 2 * half + 1:
                        wcol_free[par] = sPE.c
                    ei = qs % 2
                    E("scalar", lambda e, ei=ei, qs=qs: e.copy(
                        estage[ei][:], psA[:, qs * 512:qs * 512 + 512]),
                      waits=[mmw] + (est_free[ei] if est_free[ei] else []),
                      inc=sAC)
                    psa_evac[qs] = sAC.c
                    evt = sAC.now()
                    tok = syq.dma(lambda e, ot=ot, qs=qs, ei=ei: e.dma_start(
                        out=s_at.ap()[ot, :, qs * 512:qs * 512 + 512], in_=estage[ei][:]),
                        waits=[evt])
                    sq = sqr[ei]
                    E("scalar", lambda e, ei=ei, q=sq: e.activation(q[:], estage[ei][:], AF.Square),
                      waits=sq_g[ei], inc=sAC)
                    sqw = sAC.now()

                    def _stats3(ei=ei, qs=qs, ot=ot, evt=evt, sqw=sqw, sq=sq, tok=tok):
                        E("tensor", lambda e, ei=ei, qs=qs: e.matmul(
                            psB[32 * qs:32 * qs + 32, :], ones_sb[:, 0:32], estage[ei][:],
                            start=False, stop=False, skip_group_check=True,
                            tile_position=(0, 32 * qs)),
                          waits=[evt] + ([b3ms] if ot == 0 else []))
                        E("tensor", lambda e, q=sq, qs=qs: e.matmul(
                            psC[32 * qs:32 * qs + 32, :], ones_sb[:, 0:32], q[:],
                            start=False, stop=False, skip_group_check=True,
                            tile_position=(0, 32 * qs)),
                          waits=[sqw], inc=sPE)
                        sq_g[ei] = [sPE.now()]
                        est_free[ei] = [tok, sPE.now()]
                    pend3[0] = _stats3
                    at_written[ot][qs] = tok
                if half == 1:
                    b4_sched3(ot)
            pend3[0]()
            pend3[0] = None
            b3_half_pe[half] = sPE.now()


        gdrain(0)
        gdrain(1)

        ms5 = [None] * NQS

        def b5_memset(qs):
            E("vector", lambda e, qs=qs: e.memset(psB[32 * qs:32 * qs + 32, :], 0.0),
              waits=[(sAC, fin2_read[qs])])
            E("vector", lambda e, qs=qs: e.memset(psC[32 * qs:32 * qs + 32, :], 0.0),
              inc=sDV)
            ms5[qs] = sDV.now()

        b5_memset(0)
        b5_memset(1)

        def b4_sched5(ot):
            if ot == 0:
                b4_partA(2)
            elif ot in (4, 6, 8, 10):
                b4_batch(2, 8 * (ot - 4) // 2)
            elif ot == 11:
                b4_partB(2)
                b5_memset(2)
                b4_partA(3)
            elif ot in (14, 16, 18, 20):
                b4_batch(3, 8 * (ot - 14) // 2)
            elif ot == 21:
                b4_partB(3)
                b5_memset(3)
            elif 22 <= ot <= 26:
                gstep(2)
                gstep(2)
            elif 27 <= ot <= 31:
                gstep(3)
                gstep(3)

        wcol_free = [sPE.c, sPE.c]
        psb_evac5 = [sAC.c, sAC.c]
        ff_written = [[None] * NQS for _ in range(NCH)]
        pend5 = [None]
        b5_half_pe = [None, None]
        for half in (0, 1):
            if half == 1:
                gdrain(2)
                gdrain(3)
            for ot in range(NCH):
                par = ot % 2
                wl = gpq.dma(lambda e, ot=ot, par=par: e.dma_start(
                    out=wcol[par][:], in_=wb.ap()[ot]),
                    waits=[(sPE, wcol_free[par])])
                for qs in (2 * half, 2 * half + 1):
                    bank = psA[:, 1024:1536] if qs % 2 == 0 else psA[:, 1536:2048]
                    for c in range(NCH):
                        E("tensor", lambda e, c=c, qs=qs, par=par, bank=bank: e.matmul(
                            bank, wcol[par][:, c * 128:(c + 1) * 128],
                            XN[:, c * 2048 + qs * 512: c * 2048 + qs * 512 + 512],
                            start=(c == 0), stop=False),
                          waits=(([wl] if c == 0 and qs == 2 * half else [])
                                 + (([y1n_q[qs]]
                                     + ([(sAC, psa_evac[2 + qs % 2])] if half == 0 else []))
                                    if c == 0 and ot == 0 else [])
                                + ([(sAC, psb_evac5[qs % 2])] if c == 0 and (ot > 0 or half == 1) else [])))
                    E("tensor", lambda e, qs=qs, par=par, bank=bank: e.matmul(
                        bank, wcol[par][:, D:D + 128],
                        hw_all[:, qs * 512:qs * 512 + 512],
                        start=False, stop=True),
                      waits=[hw_rdy[qs]] if ot == 0 else [])
                    if pend5[0] is not None:
                        pend5[0]()
                    E("tensor", None, inc=sPE)
                    mmw = sPE.now()
                    if qs == 2 * half + 1:
                        wcol_free[par] = sPE.c
                    ei = qs % 2
                    E("scalar", lambda e, ei=ei, bank=bank, ot=ot: e.add(
                        estage[ei][:], bank, bbase_sb[:, ot:ot + 1]),
                      waits=[mmw] + (est_free[ei] if est_free[ei] else []),
                      inc=sAC)
                    psb_evac5[qs % 2] = sAC.c
                    evt = sAC.now()
                    tok = syq.dma(lambda e, ot=ot, qs=qs, ei=ei: e.dma_start(
                        out=s_ff.ap()[ot, :, qs * 512:qs * 512 + 512], in_=estage[ei][:]),
                        waits=[evt])
                    sq = sqr[ei]
                    E("scalar", lambda e, ei=ei, q=sq: e.activation(q[:], estage[ei][:], AF.Square),
                      waits=sq_g[ei], inc=sAC)
                    sqw = sAC.now()

                    def _stats5(ei=ei, qs=qs, ot=ot, evt=evt, sqw=sqw, sq=sq, tok=tok):
                        E("tensor", lambda e, ei=ei, qs=qs: e.matmul(
                            psB[32 * qs:32 * qs + 32, :], ones_sb[:, 0:32], estage[ei][:],
                            start=False, stop=False, skip_group_check=True,
                            tile_position=(0, 32 * qs)),
                          waits=[evt] + ([ms5[qs]] if ot == 0 else []))
                        E("tensor", lambda e, q=sq, qs=qs: e.matmul(
                            psC[32 * qs:32 * qs + 32, :], ones_sb[:, 0:32], q[:],
                            start=False, stop=False, skip_group_check=True,
                            tile_position=(0, 32 * qs)),
                          waits=[sqw], inc=sPE)
                        sq_g[ei] = [sPE.now()]
                        est_free[ei] = [tok, sPE.now()]
                    pend5[0] = _stats5
                    ff_written[ot][qs] = tok
                if half == 0:
                    b4_sched5(ot)
            pend5[0]()
            pend5[0] = None
            b5_half_pe[half] = sPE.now()
        b5_pe = b5_half_pe[1]

        vt_f = Vt.ap().bitcast(F32)
        ysR = [kr_f[:, 0:512], kr_f[:, 512:1024], vt_f[:, 0:512], vt_f[:, 512:1024],
               cos_f[:, 0:512], cos_f[:, 512:1024], ss_f[:, 0:512], ss_f[:, 512:1024]]
        ysR_g = [[] for _ in range(8)]
        arR6 = ([P_sb[0][:, i * 512:(i + 1) * 512] for i in range(4)]
                + [acat_sb[:, i * 512:(i + 1) * 512] for i in range(4)])
        arR6_g = [[hw_rdy[3]] for _ in range(4)] + [[acat_w] for _ in range(4)]
        xrR6 = xrR + [obuf[1][:]]
        xrR_g6 = [list(xrR_g[i]) for i in range(4)] + [list(sqb_g[1])]
        for qs in range(NQS):
            qo = qs * 512
            hf = qs // 2
            stw = stats_finalize32(rsA, nmA, psB[32 * qs:32 * qs + 32, :],
                                   psC[32 * qs:32 * qs + 32, :], [b5_half_pe[hf]])
            aws, xws = {}, {}

            def _lda6(c, qs=qs, qo=qo, hf=hf):
                slot = c % 8
                aws[c] = syq.dma(lambda e, c=c, slot=slot, qo=qo: e.dma_start(
                    out=arR6[slot], in_=s_ff.ap()[c, :, qo:qo + 512]),
                    waits=[ff_written[c][qs], b5_half_pe[hf]] + arR6_g[slot])
                arR6_g[slot] = []

            def _ldy6(c, qs=qs, qo=qo):
                xslot = c % 5
                xws[c] = gpq.dma(lambda e, c=c, xslot=xslot, qo=qo: e.dma_start(
                    out=xrR6[xslot], in_=s_y1.ap()[c, :, qo:qo + 512]),
                    waits=list(y1_written[qs]) + xrR_g6[xslot])
                xrR_g6[xslot] = []

            for c in range(8):
                _lda6(c)
            for c in range(5):
                _ldy6(c)
            for c in range(NCH):
                slot = c % 8
                xslot = c % 5
                aw = aws[c]
                yw = xws[c]
                ypar = c % 8
                E("vector", lambda e, slot=slot: e.tensor_tensor(
                    obuf[0][:], arR6[slot], rsA[:], op=OP.mult), waits=[aw, stw])
                E("vector", lambda e: e.tensor_tensor(obuf[0][:], obuf[0][:], nmA[:], op=OP.add))
                E("vector", lambda e, xslot=xslot, ypar=ypar: e.tensor_tensor(
                    ysR[ypar], obuf[0][:], xrR6[xslot], op=OP.add),
                  waits=[yw] + ysR_g[ypar], inc=sDV)
                arR6_g[slot].append(sDV.now())
                xrR_g6[xslot].append(sDV.now())
                if c + 8 < NCH:
                    _lda6(c + 8)
                if c + 5 < NCH:
                    _ldy6(c + 5)
                ysR_g[ypar] = [syq.dma(lambda e, c=c, qo=qo, ypar=ypar: e.dma_start(
                    out=y2.ap()[c, :, qo:qo + 512], in_=ysR[ypar]),
                    waits=[sDV.now()])]
        E("sync", None, waits=syq.all())

        with nc.Block() as block:
            @block.sync
            def _(e):
                p.replay("sync", e)

            @block.gpsimd
            def _(e):
                p.replay("gpsimd", e)

            @block.tensor
            def _(e):
                p.replay("tensor", e)

            @block.scalar
            def _(e):
                p.replay("scalar", e)

            @block.vector
            def _(e):
                p.replay("vector", e)

    return nc


def _rope_tables():
    inv_freq = 1.0 / 10000.0 ** (np.arange(0, DH, 2, dtype=np.float32) / DH)
    t = np.arange(S, dtype=np.float32)
    f = np.outer(t, inv_freq)
    cos = np.zeros((S, DH), dtype=np.float32)
    sin = np.zeros((S, DH), dtype=np.float32)
    cos[:, 0::2] = np.cos(f)
    cos[:, 1::2] = np.cos(f)
    sin[:, 0::2] = np.sin(f)
    sin[:, 1::2] = np.sin(f)
    return cos, sin


def host_inputs(L0_attn_x, L0_attn_Wq, L0_attn_Wk, L0_attn_Wv, L0_attn_Wo,
                L0_ffn_W_base, L0_ffn_b_base, L0_ffn_W_gate, L0_ffn_W_thr,
                L0_ffn_b_thr, L0_ffn_A_w, L0_ffn_B_w):
    x = np.asarray(L0_attn_x, dtype=np.float32)
    Wq = np.asarray(L0_attn_Wq, dtype=np.float32)
    Wk = np.asarray(L0_attn_Wk, dtype=np.float32)
    Wv = np.asarray(L0_attn_Wv, dtype=np.float32)
    Wo = np.asarray(L0_attn_Wo, dtype=np.float32)
    Wb = np.asarray(L0_ffn_W_base, dtype=np.float32)
    bb = np.asarray(L0_ffn_b_base, dtype=np.float32)
    Wg = np.asarray(L0_ffn_W_gate, dtype=np.float32)
    Wt = np.asarray(L0_ffn_W_thr, dtype=np.float32)
    bt = np.asarray(L0_ffn_b_thr, dtype=np.float32)
    Aw = np.asarray(L0_ffn_A_w, dtype=np.float32)
    Bw = np.asarray(L0_ffn_B_w, dtype=np.float32)

    def slab(W):
        return np.ascontiguousarray(W.reshape(32, 128, 32, 128).transpose(0, 3, 2, 1)
                                    .reshape(32, 128, D))

    wq_s = slab(Wq)
    wo_s = slab(Wo)
    wb_s = slab(Wb)
    B2 = Bw.transpose(0, 2, 1).reshape(128, D)
    b2col = np.ascontiguousarray(B2.reshape(128, 32, 128).transpose(1, 0, 2))
    wb_s = np.ascontiguousarray(np.concatenate([wb_s, b2col], axis=2))

    def kvlay(W):
        return np.ascontiguousarray(
            W.T.reshape(32, 128, 128).transpose(1, 0, 2).reshape(128, D))
    wk_l = kvlay(Wk)
    wv_l = kvlay(Wv)

    Wg33 = np.zeros((33, D), np.float32)
    Wg33[0:8] = Wg
    Wg33[32] = Wt
    wg9_l = np.ascontiguousarray(
        Wg33.T.reshape(32, 128, 33).transpose(1, 0, 2).reshape(128, 32 * 33))
    Acat = Aw.reshape(128, D)
    acat_l = np.ascontiguousarray(
        Acat.T.reshape(32, 128, 128).transpose(1, 0, 2).reshape(128, D))
    bbase_l = np.ascontiguousarray(bb.reshape(32, 128).T)
    bthr_l = np.ascontiguousarray(bt.reshape(1, 1))

    ones_c = np.ones((128, 128), dtype=BF)
    e16_c = np.zeros((8, 128), dtype=np.float32)
    for e in range(8):
        e16_c[e, e * 16:(e + 1) * 16] = 2.0
    e16_c = e16_c.astype(BF)
    cosT, sinT = _rope_tables()

    al = np.linspace(0.0, 1.0, 256, dtype=np.float32)
    alC_c = np.ascontiguousarray(np.broadcast_to(al, (128, 256))).astype(BF)
    alD_c = np.ascontiguousarray(np.broadcast_to(1.0 - al, (128, 256))).astype(BF)
    al1 = np.ones((128, 256), dtype=BF)
    al0 = np.zeros((128, 256), dtype=BF)

    in_maps = []
    for core in range(NCORE):
        seq = core // 4
        c0 = (core % 4) * TOK
        lo, hi = c0 - 256, c0 + TOK + 256
        lo_c, hi_c = max(lo, 0), min(hi, S)
        xs = x[seq, lo_c:hi_c]
        xs = np.pad(xs, ((lo_c - lo, hi - hi_c), (0, 0)))
        xin_l = np.ascontiguousarray(xs.T.reshape(32, 128, HALO))
        pos = np.clip(np.arange(lo, hi), 0, S - 1)
        cos_l = np.ascontiguousarray(cosT[pos].T).astype(BF)
        ssv = sinT[pos].T.copy()
        ssv[0:64] *= -1.0
        ss_l = np.ascontiguousarray(ssv).astype(BF)
        maskv = np.zeros(HALO, dtype=np.float32)
        if lo < 0:
            maskv[: -lo] = MASKVAL
        if hi > S:
            maskv[-(hi - S):] = MASKVAL
        mask_l = np.ascontiguousarray(
            np.broadcast_to(maskv.reshape(20, 128).T, (128, 20)))
        first = (core % 4 == 0)
        in_maps.append(dict(
            xin=xin_l, wq=wq_s, wo=wo_s, wb=wb_s, wk=wk_l, wv=wv_l,
            wg9=wg9_l, acat=acat_l, bbase=bbase_l, bthr=bthr_l,
            ones_in=ones_c, e16_in=e16_c, cos_in=cos_l, ss_in=ss_l,
            alA_in=(al1 if first else alC_c), alB_in=(al0 if first else alD_c),
            alC_in=alC_c, alD_in=alD_c, mask_in=mask_l))
    return in_maps


def kernel(**inputs):
    nc = build_nc()
    in_maps = host_inputs(**inputs)
    res = run_bass_kernel_spmd(nc, in_maps, core_ids=list(range(NCORE)))
    out = np.empty((Bb, S, D), dtype=np.float32)
    for core in range(NCORE):
        y = np.asarray(res.results[core]["y2"])
        seq = core // 4
        c0 = (core % 4) * TOK
        out[seq, c0:c0 + TOK] = y.reshape(D, TOK).T
    return out


# version 18
# speedup vs baseline: 1.1939x; 1.0793x over previous
import contextlib
import numpy as np
import ml_dtypes
import concourse.bass as bass
import concourse.mybir as mybir
from concourse.bass_utils import run_bass_kernel_spmd

BF = ml_dtypes.bfloat16
F32 = mybir.dt.float32
BF16 = mybir.dt.bfloat16
AF = mybir.ActivationFunctionType
OP = mybir.AluOpType

NCORE = 8
Bb, S, D = 2, 8192, 4096
TOK = 2048
HALO = 2560
NCH = 32
H, DH = 32, 128
SCALE = float(1.0 / np.sqrt(128.0))
NW = 9
NQS = 4
EPS = 1e-5
MASKVAL = -30000.0


class Sem:
    def __init__(self, h, step):
        self.h = h
        self.c = 0
        self.step = step

    def now(self):
        return (self, self.c)


class Prog:
    ENG = ("sync", "gpsimd", "tensor", "scalar", "vector")

    def __init__(self):
        self.ops = {k: [] for k in self.ENG}

    def emit(self, eng, build=None, waits=(), inc=None):
        if inc is not None:
            inc.c += inc.step
        self.ops[eng].append((build, tuple(waits), inc))

    def replay(self, eng_name, e):
        auto_drain = eng_name in ("vector", "scalar")
        prev_real = False
        for build, waits, inc in self.ops[eng_name]:
            for s, v in waits:
                if v > 0:
                    e.wait_ge(s.h, v)
            if build is not None and auto_drain and prev_real:
                e.drain()
            ins = build(e) if build is not None else None
            if build is not None:
                prev_real = True
            if inc is not None:
                if ins is None:
                    ins = e.drain()
                ins.then_inc(inc.h, inc.step)


class DmaQ:

    def __init__(self, prog, eng, nc, es, name, n=8):
        self.p = prog
        self.eng = eng
        self.sems = [Sem(es.enter_context(nc.semaphore(f"{name}{i}")), 16)
                     for i in range(n)]
        self.i = 0

    def dma(self, build, waits=()):
        s = self.sems[self.i]
        self.i = (self.i + 1) % len(self.sems)
        w = list(waits) + ([(s, s.c)] if s.c else [])
        self.p.emit(self.eng, build, waits=w, inc=s)
        return (s, s.c)

    def all(self):
        return [(s, s.c) for s in self.sems if s.c]


def build_nc(dbg=False):
    nc = bass.Bass("TRN2", target_bir_lowering=False, dynamic_dma_scratch_size=16384)

    def din(name, shape, dt):
        return nc.dram_tensor(name, list(shape), dt, kind="ExternalInput")

    def dscr(name, shape, dt):
        return nc.dram_tensor(name, list(shape), dt,
                              kind="ExternalOutput" if dbg else "Internal")

    xin = din("xin", (NCH, 128, HALO), F32)
    wq = din("wq", (NCH, 128, D), F32)
    wo = din("wo", (NCH, 128, D), F32)
    wb = din("wb", (NCH, 128, D + 128), F32)
    wk = din("wk", (128, D), F32)
    wv = din("wv", (128, D), F32)
    wg9 = din("wg9", (128, NCH * 33), F32)
    acat = din("acat", (128, D), F32)
    bbase = din("bbase", (128, 32), F32)
    bthr = din("bthr", (1, 1), F32)
    ones_in = din("ones_in", (128, 128), BF16)
    e16_in = din("e16_in", (8, 128), BF16)
    cos_in = din("cos_in", (128, HALO), BF16)
    ss_in = din("ss_in", (128, HALO), BF16)
    alA_in = din("alA_in", (128, 256), BF16)
    alB_in = din("alB_in", (128, 256), BF16)
    alC_in = din("alC_in", (128, 256), BF16)
    alD_in = din("alD_in", (128, 256), BF16)
    mask_in = din("mask_in", (128, 20), F32)

    s_xn = dscr("s_xn", (NCH, 128, HALO), BF16)
    s_q = dscr("s_q", (H, 128, TOK), BF16)
    s_at = dscr("s_at", (NCH, 128, TOK), BF16)
    s_y1 = dscr("s_y1", (NCH, 128, TOK), BF16)
    s_ff = dscr("s_ff", (NCH, 128, TOK), BF16)
    s_xb = dscr("s_xb", (NCH, 128, TOK), BF16)
    y2 = nc.dram_tensor("y2", [NCH, 128, TOK], F32, kind="ExternalOutput")

    p = Prog()
    E = p.emit

    with contextlib.ExitStack() as es:
        def sb(name, shape, dt):
            return es.enter_context(nc.sbuf_tensor(name, list(shape), dt))

        BIG = sb("BIG", (128, 65536), BF16)
        wcol = [sb(f"wcol{i}", (128, D + 128), BF16) for i in range(2)]
        ones_sb = sb("ones_sb", (128, 128), BF16)
        e16_sb = sb("e16_sb", (8, 128), BF16)
        cos_sb = sb("cos_sb", (128, HALO), BF16)
        ss_sb = sb("ss_sb", (128, HALO), BF16)
        alA = sb("alA", (128, 256), BF16)
        alB = sb("alB", (128, 256), BF16)
        alC = sb("alC", (128, 256), BF16)
        alD = sb("alD", (128, 256), BF16)
        mask_sb = sb("mask_sb", (128, 20), F32)
        bbase_sb = sb("bbase_sb", (128, 32), F32)
        bthr_sb = sb("bthr_sb", (1, 1), F32)
        bthr2 = sb("bthr2", (33, 1), F32)
        eps_sb = sb("eps_sb", (128, 1), F32)
        wg9_sb = sb("wg9_sb", (128, NCH * 33), BF16)
        acat_sb = sb("acat_sb", (128, D), BF16)
        Kr = sb("Kr", (128, HALO), BF16)
        Vt = sb("Vt", (128, HALO), BF16)
        qh = sb("qh", (128, TOK), BF16)
        estage = [sb(f"estage{i}", (128, 512), BF16) for i in range(2)]
        P_sb = [sb(f"P{i}", (128, 2048), BF16) for i in range(2)]
        obuf = [sb(f"obuf{i}", (128, 512), BF16) for i in range(2)]
        T1 = sb("T1", (128, 512), F32)
        T2 = sb("T2", (128, 512), F32)
        T3 = sb("T3", (128, 512), F32)
        rsA = sb("rsA", (128, 512), BF16)
        nmA = sb("nmA", (128, 512), BF16)
        sqr = [sb(f"sqr{i}", (128, 512), BF16) for i in range(2)]
        ar = [sb(f"ar{i}", (128, 512), BF16) for i in range(2)]
        xr = [sb(f"xr{i}", (128, 512), BF16) for i in range(2)]
        cos_f = cos_sb.ap().bitcast(F32)
        ss_f = ss_sb.ap().bitcast(F32)
        kr_f = Kr.ap().bitcast(F32)
        T2b = T2.ap().bitcast(BF16)
        gvA = cos_f[:, 0:512]
        gvB = cos_f[:, 512:1024]
        gvC = ss_f[:, 0:512]
        WN = ss_sb[0:8, 2048:2560]
        ebf = P_sb[0][0:8, 512:1024]
        tbf = P_sb[0][0:1, 1024:1536]
        webf = Vt[0:8, 0:512]
        wful = P_sb[0][:, 0:512]
        hw_all = P_sb[1][:]
        ystage = [kr_f[:, 0:512], kr_f[:, 512:1024]]
        rsB = Vt[:, 2048:2560]
        nmB = cos_sb[:, 2048:2560]

        psA = es.enter_context(nc.psum_tensor("psA", [128, 2048], F32))
        psB = es.enter_context(nc.psum_tensor("psB", [128, 512], F32))
        psC = es.enter_context(nc.psum_tensor("psC", [128, 512], F32))
        psD = es.enter_context(nc.psum_tensor("psD", [128, 512], F32))
        psE = es.enter_context(nc.psum_tensor("psE", [128, 512], F32))

        sPE = Sem(es.enter_context(nc.semaphore("sPE")), 1)
        sAC = Sem(es.enter_context(nc.semaphore("sAC")), 1)
        sDV = Sem(es.enter_context(nc.semaphore("sDV")), 1)
        syq = DmaQ(p, "sync", nc, es, "sy", 8)
        gpq = DmaQ(p, "gpsimd", nc, es, "gp", 8)

        state = {"psbc_ac": 0, "psde_ac": 0, "stats_g": [], "g_ln2": [0, 0]}
        ar_g = [[], []]
        xr_g = [[], []]
        sq_g = [[], []]
        ys_g = [[], []]

        for dst, src in ((ones_sb, ones_in), (e16_sb, e16_in), (cos_sb, cos_in),
                         (ss_sb, ss_in), (alA, alA_in), (alB, alB_in),
                         (alC, alC_in), (alD, alD_in), (mask_sb, mask_in),
                         (bbase_sb, bbase), (bthr_sb, bthr)):
            syq.dma(lambda e, d=dst, s=src: e.dma_start(out=d[:], in_=s.ap()))
        consts_sy = syq.all()
        gpq.dma(lambda e: e.dma_start(out=wg9_sb[:], in_=wg9.ap()))
        wk_sb = BIG[:, 49152:53248]
        wv_sb = BIG[:, 53248:57344]
        gpq.dma(lambda e: e.dma_start(out=wk_sb, in_=wk.ap()))
        gpq.dma(lambda e: e.dma_start(out=wv_sb, in_=wv.ap()))
        consts_gp = gpq.all()
        E("scalar", lambda e: e.mul(bthr2[0:1, :], bthr_sb[:], 0.5),
          waits=consts_sy, inc=sAC)
        E("vector", lambda e: e.tensor_copy(bthr2[32:33, :], bthr2[0:1, :]),
          waits=[sAC.now()], inc=sDV)
        E("vector", lambda e: e.memset(eps_sb[:], EPS), inc=sDV)

        def stats_finalize(rstd_bf, nmu_bf, mm_wait, pb=None, pc=None, gk2=None):
            pb = psB if pb is None else pb
            pc = psC if pc is None else pc
            E("scalar", lambda e: e.mul(T1[:], pb[:], -1.0 / D),
              waits=[mm_wait] + state["stats_g"], inc=sAC)
            E("scalar", lambda e: e.activation(T2[:], pb[:], AF.Square, scale=1.0 / D), inc=sAC)
            E("scalar", lambda e: e.mul(T3[:], pc[:], 1.0 / D), inc=sAC)
            if gk2 is not None:
                state["g_ln2"][gk2] = sAC.c
            else:
                state["psbc_ac" if pb is psB else "psde_ac"] = sAC.c
            aw = sAC.now()
            E("vector", lambda e: e.tensor_sub(T3[:], T3[:], T2[:]), waits=[aw], inc=sDV)
            vw = sDV.now()
            E("scalar", lambda e: e.activation(T2[:], T3[:], AF.Sqrt, bias=eps_sb.ap()),
              waits=[vw], inc=sAC)
            aw2 = sAC.now()
            E("vector", lambda e: e.reciprocal(T3[:], T2[:]), waits=[aw2])
            E("vector", lambda e, r=rstd_bf: e.tensor_copy(r[:], T3[:]))
            E("vector", lambda e, m=nmu_bf: e.tensor_tensor(m[:], T1[:], T3[:], op=OP.mult), inc=sDV)
            state["stats_g"] = [sDV.now()]
            return sDV.now()

        def stats_finalize32(rstd_bf, nmu_bf, pbrow, pcrow, waits0):
            E("scalar", lambda e: e.mul(T1[0:32, :], pbrow, -1.0 / D),
              waits=list(waits0) + state["stats_g"], inc=sAC)
            E("scalar", lambda e: e.activation(T2[0:32, :], pbrow, AF.Square, scale=1.0 / D), inc=sAC)
            E("scalar", lambda e: e.mul(T3[0:32, :], pcrow, 1.0 / D), inc=sAC)
            state["fin_read"] = sAC.c
            aw = sAC.now()
            E("vector", lambda e: e.tensor_sub(T3[0:32, :], T3[0:32, :], T2[0:32, :]),
              waits=[aw], inc=sDV)
            vw = sDV.now()
            E("scalar", lambda e: e.activation(T2[0:32, :], T3[0:32, :], AF.Sqrt,
                                               bias=eps_sb[0:32, :]), waits=[vw], inc=sAC)
            aw2 = sAC.now()
            E("vector", lambda e: e.reciprocal(T3[0:32, :], T2[0:32, :]), waits=[aw2])
            E("vector", lambda e, r=rstd_bf: e.tensor_copy(r[0:32, :], T3[0:32, :]))
            E("vector", lambda e, m=nmu_bf: e.tensor_tensor(m[0:32, :], T1[0:32, :],
                                                            T3[0:32, :], op=OP.mult))
            for b in (32, 64, 96):
                E("vector", lambda e, r=rstd_bf, b=b: e.tensor_copy(r[b:b + 32, :], r[0:32, :]))
                E("vector", lambda e, m=nmu_bf, b=b: e.tensor_copy(m[b:b + 32, :], m[0:32, :]))
            E("vector", None, inc=sDV)
            state["stats_g"] = [sDV.now()]
            return sDV.now()

        def rope(dst_slice, src_tile, col0, tmp):
            E("vector", lambda e: e.tensor_copy(tmp[0:64, :], src_tile[64:128, :]))
            E("vector", lambda e: e.tensor_copy(tmp[64:128, :], src_tile[0:64, :]))
            E("vector", lambda e: e.tensor_tensor(tmp[:], tmp[:], ss_sb[:, col0:col0 + 512], op=OP.mult))
            E("vector", lambda e: e.tensor_tensor(dst_slice, src_tile[:], cos_sb[:, col0:col0 + 512], op=OP.mult))
            E("vector", lambda e: e.tensor_tensor(dst_slice, dst_slice, tmp[:], op=OP.add))

        xa_view = [BIG[:, 0:16384], BIG[:, 16384:32768]]
        xnbuf = BIG[:, 32768:49152]
        xa_free = [0, 0]
        xn_free_pe = 0
        xn_wr_tok = []
        xb_wr = [[], []]
        for s in range(5):
            t0 = 512 * s
            par = s % 2
            lw = []
            for g in range(4):
                lw.append(gpq.dma(
                    lambda e, s=s, g=g, par=par: e.dma_start(
                        out=xa_view[par].rearrange("p (c q) -> p c q", c=NCH)[:, g * 8:(g + 1) * 8, :],
                        in_=xin.ap()[g * 8:(g + 1) * 8, :, 512 * s:512 * s + 512].transpose([1, 0, 2])),
                    waits=(consts_gp if s == 0 else [])
                          + ([(sPE, xa_free[par])] if s >= 2 else [])
                          + xb_wr[par]))
            xb_wr[par] = []
            off0 = max(0, 512 * s - 256)
            t0o = max(0, 256 - 512 * s)
            ln = min(512 * s + 512, 2304) - max(512 * s, 256)
            for g in range(4):
                xb_wr[par].append(syq.dma(
                    lambda e, g=g, par=par, t0o=t0o, off0=off0, ln=ln: e.dma_start(
                        out=s_xb.ap()[g * 8:(g + 1) * 8, :, off0:off0 + ln].transpose([1, 0, 2]),
                        in_=xa_view[par].rearrange("p (c q) -> p c q", c=NCH)[:, g * 8:(g + 1) * 8, t0o:t0o + ln]),
                    waits=list(lw)))
            for c in range(NCH):
                sq = sqr[c % 2]
                E("vector", lambda e, c=c, par=par, q=sq: e.tensor_tensor(
                    q[:], xa_view[par][:, c * 512:(c + 1) * 512],
                    xa_view[par][:, c * 512:(c + 1) * 512], op=OP.mult),
                  waits=list(lw) + sq_g[c % 2], inc=sDV)
                sqw = sDV.now()
                E("tensor", lambda e, c=c, par=par: e.matmul(
                    psB[:], ones_sb[:], xa_view[par][:, c * 512:(c + 1) * 512],
                    start=(c == 0), stop=(c == NCH - 1)),
                  waits=list(lw) + (consts_sy + [(sAC, state["psbc_ac"])] if c == 0 else []))
                E("tensor", lambda e, q=sq, c=c: e.matmul(
                    psC[:], ones_sb[:], q[:], start=(c == 0), stop=(c == NCH - 1)),
                  waits=[sqw], inc=sPE)
                sq_g[c % 2] = [sPE.now()]
            stw = stats_finalize(rsA, nmA, sPE.now())
            for c in range(NCH):
                E("vector", lambda e, c=c, par=par: e.tensor_tensor(
                    xnbuf[:, c * 512:(c + 1) * 512], xa_view[par][:, c * 512:(c + 1) * 512],
                    rsA[:], op=OP.mult),
                  waits=([stw, (sPE, xn_free_pe)] + xn_wr_tok if c == 0 else []))
                E("vector", lambda e, c=c: e.tensor_tensor(
                    xnbuf[:, c * 512:(c + 1) * 512], xnbuf[:, c * 512:(c + 1) * 512],
                    nmA[:], op=OP.add), inc=sDV)
                xnw = sDV.now()
                E("tensor", lambda e, c=c: e.matmul(
                    psD[:], wk_sb[:, c * 128:(c + 1) * 128], xnbuf[:, c * 512:(c + 1) * 512],
                    start=(c == 0), stop=(c == NCH - 1)),
                  waits=[xnw] + (consts_gp if (s == 0 and c == 0) else []))
            for t4 in range(4):
                for c in range(NCH):
                    E("tensor", lambda e, t4=t4, c=c: e.matmul(
                        psE[:, t4 * 128:(t4 + 1) * 128],
                        xnbuf[:, c * 512 + t4 * 128:c * 512 + t4 * 128 + 128],
                        wv_sb[:, c * 128:(c + 1) * 128],
                        start=(c == 0), stop=(c == NCH - 1)))
            E("tensor", None, inc=sPE)
            kvw = sPE.now()
            xn_free_pe = sPE.c
            xa_free[par] = sPE.c
            E("scalar", lambda e: e.copy(sqr[0][:], psD[:]),
              waits=[kvw] + sq_g[0] + sq_g[1], inc=sAC)
            E("vector", None, waits=[sAC.now()])
            rope(Kr[:, t0:t0 + 512], sqr[0], t0, sqr[1])
            for t4 in range(4):
                E("vector", lambda e, t4=t4, s=s: e.tensor_copy(
                    Vt[:, (4 * s + t4) * 128:(4 * s + t4) * 128 + 128],
                    psE[:, t4 * 128:(t4 + 1) * 128]),
                  inc=(sDV if t4 == 3 else None))
            kv_done = sDV.now()
            sq_g[0] = sq_g[1] = [sDV.now()]
            xn_wr_tok = []
            for g in range(4):
                xn_wr_tok.append(syq.dma(
                    lambda e, s=s, g=g: e.dma_start(
                        out=s_xn.ap()[g * 8:(g + 1) * 8, :, 512 * s:512 * s + 512].transpose([1, 0, 2]),
                        in_=xnbuf.rearrange("p (c q) -> p c q", c=NCH)[:, g * 8:(g + 1) * 8, :]),
                    waits=[sDV.now()]))
        kv_all = kv_done
        a_pe_done = sPE.now()

        xn_ld = []
        for g in range(4):
            xn_ld.append(syq.dma(
                lambda e, g=g: e.dma_start(
                    out=BIG.rearrange("p (c q) -> p c q", c=NCH)[:, g * 8:(g + 1) * 8, :],
                    in_=s_xn.ap()[g * 8:(g + 1) * 8, :, 256:256 + TOK].transpose([1, 0, 2])),
                waits=syq.all() + [kv_all, a_pe_done]))
        XN = BIG

        o_sbv = T1.ap().bitcast(BF16)
        rd_bfv = T3.ap().bitcast(BF16)
        o_sb = [o_sbv[:, 0:512], o_sbv[:, 512:1024]]
        rd_bf = [rd_bfv[:, 0:512], rd_bfv[:, 512:1024]]
        wcol_free = [0, 0]
        est_free = [None, None]
        psb_evac = [0, 0]
        q_written = []
        qhs = [qh.ap(), acat_sb.ap()[:, 0:TOK]]
        wl_pre = {}
        qw_pre = {}
        b2s = {"sig_slot": [0, 0], "smm_tok": [None, None], "rc_free": [0, 0],
               "oev_free": [0, 0], "pp_free": [0, 0], "qh_free": [0, 0],
               "bst_g": [[], []]}
        blend_toks = [[] for _ in range(H)]

        def emit_b1_ot(ot):
            par = ot % 2
            wl = wl_pre.pop(ot, None)
            if wl is None:
                wl = gpq.dma(lambda e, ot=ot, par=par: e.dma_start(
                    out=wcol[par][:, 0:D], in_=wq.ap()[ot]),
                    waits=[(sPE, wcol_free[par])] if ot >= 2 else [])
            qtoks = []
            for qs in range(NQS):
                yield
                bank = psB if qs % 2 == 0 else psC
                for c in range(NCH):
                    E("tensor", lambda e, c=c, qs=qs, par=par, bank=bank: e.matmul(
                        bank[:],
                        wcol[par][:, c * 128:(c + 1) * 128],
                        XN[:, c * 2048 + qs * 512: c * 2048 + qs * 512 + 512],
                        start=(c == 0), stop=(c == NCH - 1)),
                      waits=([wl] + xn_ld if c == 0 and qs == 0 else [])
                            + ([(sAC, psb_evac[qs % 2])] if c == 0 and (ot > 0 or qs >= 2) else []))
                E("tensor", None, inc=sPE)
                mmw = sPE.now()
                if qs == NQS - 1:
                    wcol_free[par] = sPE.c
                ei = (ot * NQS + qs) % 2
                E("scalar", lambda e, ei=ei, bank=bank: e.copy(estage[ei][:], bank[:]),
                  waits=[mmw] + ([est_free[ei]] if est_free[ei] else []),
                  inc=sAC)
                psb_evac[qs % 2] = sAC.c
                E("vector", None, waits=[sAC.now()] + sq_g[0])
                rope(estage[ei][:], estage[ei], 256 + qs * 512, sqr[0])
                E("vector", None, inc=sDV)
                sq_g[0] = [sDV.now()]
                tok = syq.dma(lambda e, ot=ot, qs=qs, ei=ei: e.dma_start(
                    out=s_q.ap()[ot, :, qs * 512:qs * 512 + 512], in_=estage[ei][:]),
                    waits=[sDV.now()])
                est_free[ei] = tok
                qtoks.append(tok)
                if qs == 1 and ot + 1 < H:
                    npar = (ot + 1) % 2
                    wl_pre[ot + 1] = gpq.dma(lambda e, o1=ot + 1, npar=npar: e.dma_start(
                        out=wcol[npar][:, 0:D], in_=wq.ap()[o1]),
                        waits=[(sPE, wcol_free[npar])])
            q_written.append(qtoks)

        def emit_b2_head(h):
            yield
            par = h % 2
            myq = qhs[par]
            qw = qw_pre.pop(h, None)
            if qw is None:
                qw = syq.dma(lambda e, par=par, h=h: e.dma_start(
                    out=qhs[par], in_=s_q.ap()[h]),
                    waits=list(q_written[h]) + ([(sPE, b2s["qh_free"][par])] if h >= 2 else []))
            sigs = {}
            dn = [psA[:, 1024:1536], psA[:, 1536:2048]]

            def s_mms(w, cpair):
                qlo = max(0, 256 * w - 256)
                qhi = min(TOK, 256 * w + 256)
                wlen = qhi - qlo
                boff = qlo - (256 * w - 256)
                for c in (cpair, cpair + 1):
                    kt = 2 * w + c
                    E("tensor", lambda e, c=c, kt=kt, qlo=qlo, wlen=wlen, mq=myq: e.matmul(
                        psA[:, (c % 2) * 512: (c % 2) * 512 + wlen],
                        Kr[:, kt * 128:kt * 128 + 128],
                        mq[:, qlo:qlo + wlen],
                        start=True, stop=True),
                      waits=([qw, kv_all] if c == cpair else [])
                            + [(sAC, b2s["sig_slot"][c % 2])],
                      inc=sPE)
                    b2s["smm_tok"][c % 2] = sPE.now()
                return boff, wlen

            def sig_half(w, cpair, boff, wlen, pp):
                for c in (cpair, cpair + 1):
                    kt = 2 * w + c
                    masked = (w == 0 or w == NW - 1)
                    E("scalar", lambda e, c=c, kt=kt, boff=boff, wlen=wlen, pp=pp,
                                m=masked: e.activation(
                        P_sb[pp][:, c * 512 + boff: c * 512 + boff + wlen],
                        psA[:, (c % 2) * 512: (c % 2) * 512 + wlen],
                        AF.Sigmoid, scale=SCALE,
                        **(dict(bias=mask_sb[:, kt:kt + 1]) if m else {})),
                      waits=[b2s["smm_tok"][c % 2]]
                            + ([(sPE, b2s["pp_free"][pp])] if c == cpair else []),
                      inc=sAC)
                    b2s["sig_slot"][c % 2] = sAC.c

            for w in range(NW + 1):
                yield
                if w >= 1:
                    wp = w - 1
                    sw, pp_p, boff_p, wlen_p = sigs[wp]
                    ds = wp % 2
                    ob = [psD, psE][ds]
                if w <= NW - 1:
                    pp = (h * NW + w) % 2
                    boff, wlen = s_mms(w, 0)
                    sig_half(w, 0, boff, wlen, pp)
                if w >= 1:
                    for c in range(4):
                        E("tensor", lambda e, c=c, pp=pp_p, boff=boff_p, wlen=wlen_p, ds=ds: e.matmul(
                            dn[ds][:, boff:boff + wlen], ones_sb[:],
                            P_sb[pp][:, c * 512 + boff: c * 512 + boff + wlen],
                            start=(c == 0), stop=(c == 3)),
                          waits=[sw, (sDV, b2s["rc_free"][ds])] if c == 0 else [],
                          inc=(sPE if c == 3 else None))
                    dn_tok = sPE.now()
                if w <= NW - 1:
                    boff, wlen = s_mms(w, 2)
                    if w == NW - 1:
                        b2s["qh_free"][par] = sPE.c
                    sig_half(w, 2, boff, wlen, pp)
                    sigs[w] = (sAC.now(), pp, boff, wlen)
                    if w == 2 and h + 1 < H:
                        hn = h + 1
                        npar = hn % 2
                        qw_pre[hn] = syq.dma(lambda e, h1=hn, npar=npar: e.dma_start(
                            out=qhs[npar], in_=s_q.ap()[h1]),
                            waits=list(q_written[hn]) + [(sPE, b2s["qh_free"][npar])])
                if w >= 1:
                    for c in range(4):
                        kt = 2 * wp + c
                        E("tensor", lambda e, c=c, kt=kt, pp=pp_p, boff=boff_p, wlen=wlen_p, ob=ob: e.matmul(
                            ob[:, boff:boff + wlen],
                            Vt[:, kt * 128:kt * 128 + 128],
                            P_sb[pp][:, c * 512 + boff: c * 512 + boff + wlen],
                            start=(c == 0), stop=(c == 3)),
                          waits=([(sAC, b2s["oev_free"][ds])] if c == 0 else []),
                          inc=(sPE if c == 3 else None))
                    b2s["pp_free"][pp_p] = sPE.c
                    dnw = sPE.now()
                    E("scalar", lambda e, boff=boff_p, wlen=wlen_p, ob=ob, ds=ds: e.copy(
                        o_sb[ds][:, boff:boff + wlen], ob[:, boff:boff + wlen]),
                      waits=[dnw, (sDV, b2s["rc_free"][ds])], inc=sAC)
                    b2s["oev_free"][ds] = sAC.c
                    oev = sAC.now()

                    def _recip(e, boff=boff_p, wlen=wlen_p, ds=ds):
                        with nc.allow_low_precision("bf16 rounding of 1/denom; product is bf16 anyway"):
                            return e.reciprocal(rd_bf[ds][:, boff:boff + wlen], dn[ds][:, boff:boff + wlen])
                    E("vector", _recip, waits=[dn_tok])
                    E("vector", lambda e, wp=wp, boff=boff_p, wlen=wlen_p, ds=ds: e.tensor_tensor(
                        obuf[wp % 2][:, boff:boff + wlen], o_sb[ds][:, boff:boff + wlen],
                        rd_bf[ds][:, boff:boff + wlen], op=OP.mult), waits=[oev], inc=sDV)
                    b2s["rc_free"][ds] = sDV.c
                    if wp >= 1:
                        m = wp - 1
                        bi = m % 2
                        a_new, a_old = (alA, alB) if m == 0 else (alC, alD)
                        E("vector", lambda e, wp=wp, a_old=a_old: e.tensor_tensor(
                            T2b[:, 0:256], obuf[(wp - 1) % 2][:, 256:512], a_old[:], op=OP.mult))
                        E("vector", lambda e, wp=wp, a_new=a_new: e.tensor_tensor(
                            T2b[:, 256:512], obuf[wp % 2][:, 0:256], a_new[:], op=OP.mult))
                        E("vector", lambda e, bi=bi: e.tensor_tensor(
                            T2b[:, 512 + 256 * bi: 768 + 256 * bi],
                            T2b[:, 0:256], T2b[:, 256:512], op=OP.add),
                          waits=b2s["bst_g"][bi], inc=sDV)
                        b2s["bst_g"][bi] = [syq.dma(
                            lambda e, h=h, m=m, bi=bi: e.dma_start(
                                out=s_q.ap()[h, :, 256 * m:256 * m + 256],
                                in_=T2b[:, 512 + 256 * bi: 768 + 256 * bi]),
                            waits=[sDV.now(), qw])]
                        blend_toks[h].append(b2s["bst_g"][bi][0])

        LAG = 2
        acat_ld = []
        b1_pe_guard = [0]
        gens = []

        def _drive(g):
            try:
                next(g)
                return True
            except StopIteration:
                return False

        active_b1 = None
        active_b2 = None
        for k in range(H + LAG):
            if k < H:
                active_b1 = emit_b1_ot(k)
                next(active_b1)
            if k >= LAG:
                active_b2 = emit_b2_head(k - LAG)
                next(active_b2)
            sched = [0, 1, 1, 1, 0, 1, 1, 0, 1, 1, 1, 0, 1, 1, 1, 1]
            for pick in sched:
                if pick == 0:
                    if active_b1 is not None and not _drive(active_b1):
                        active_b1 = None
                else:
                    if active_b2 is not None and not _drive(active_b2):
                        active_b2 = None
            while active_b1 is not None and _drive(active_b1):
                pass
            if k == H - 1:
                b1_pe_guard[0] = sPE.c
                for g in range(3):
                    toks = [t for hh in range(8 * g, 8 * g + 8) for t in blend_toks[hh]]
                    acat_ld.append(syq.dma(
                        lambda e, g=g: e.dma_start(
                            out=BIG.rearrange("p (c q) -> p c q", c=NCH)[:, g * 8:(g + 1) * 8, :],
                            in_=s_q.ap()[g * 8:(g + 1) * 8, :, :].transpose([1, 0, 2])),
                        waits=toks + [(sPE, b1_pe_guard[0])]))
            while active_b2 is not None and _drive(active_b2):
                pass
        toks = [t for hh in range(24, 32) for t in blend_toks[hh]]
        acat_ld.append(syq.dma(
            lambda e: e.dma_start(
                out=BIG.rearrange("p (c q) -> p c q", c=NCH)[:, 24:32, :],
                in_=s_q.ap()[24:32, :, :].transpose([1, 0, 2])),
            waits=toks + [(sPE, b1_pe_guard[0])]))
        acat_w = gpq.dma(lambda e: e.dma_start(out=acat_sb[:], in_=acat.ap()),
                         waits=[(sPE, b2s["qh_free"][1])])
        state["psbc_ac"] = sAC.c
        state["psde_ac"] = sAC.c

        E("vector", lambda e: e.memset(psB[:], 0.0),
          waits=[(sAC, psb_evac[0]), (sAC, psb_evac[1])])
        E("vector", lambda e: e.memset(psC[:], 0.0), inc=sDV)
        b3ms = sDV.now()
        wcol_free = [sPE.c, sPE.c]
        est_free = [None, None]
        psa_evac = [sAC.c] * 4
        at_written = [[None] * NQS for _ in range(NCH)]
        pend3 = []
        b3_half_pe = [None, None]

        arB4 = ([Kr[:, i * 512:(i + 1) * 512] for i in range(5)]
                + [Vt[:, 512 + i * 512:1024 + i * 512] for i in range(3)])
        arB4_g = [[] for _ in range(8)]
        xrR = [qh[:, i * 512:(i + 1) * 512] for i in range(4)]
        xrR_g = [[] for _ in range(4)]
        sqb = [obuf[0][:], obuf[1][:]]
        sqb_g = [[], []]
        y1_written = [None] * NQS
        y1n_q = [None] * NQS
        fin32_read = [None] * NQS
        fin2_read = [None] * NQS
        b4st = {"yw": {}, "ms": {}}

        def b4_partA(qs):
            qo = qs * 512
            stw = stats_finalize32(rsA, nmA, psB[32 * qs:32 * qs + 32, :],
                                   psC[32 * qs:32 * qs + 32, :],
                                   [b3_half_pe[qs // 2]])
            fin32_read[qs] = state["fin_read"]
            E("vector", lambda e, qs=qs: e.memset(psB[32 * qs:32 * qs + 32, :], 0.0),
              waits=[(sAC, fin32_read[qs])])
            E("vector", lambda e, qs=qs: e.memset(psC[32 * qs:32 * qs + 32, :], 0.0),
              inc=sDV)
            b4st["ms"][qs] = sDV.now()
            aws, xws, yw = {}, {}, {}
            b4st["yw"] = yw

            def _lda(c, qs=qs, qo=qo):
                slot = c % 8
                aws[c] = syq.dma(lambda e, c=c, slot=slot, qs=qs, qo=qo: e.dma_start(
                    out=arB4[slot], in_=s_at.ap()[c, :, qo:qo + 512]),
                    waits=[at_written[c][qs]] + arB4_g[slot])
                arB4_g[slot] = []

            def _ldx(c, qs=qs, qo=qo):
                xslot = c % 4
                xws[c] = syq.dma(lambda e, c=c, xslot=xslot, qo=qo: e.dma_start(
                    out=xrR[xslot], in_=s_xb.ap()[c, :, qo:qo + 512]),
                    waits=xrR_g[xslot]
                          + (xn_ld + [b3_half_pe[0]] if qs == 0 and c < 4 else []))
                xrR_g[xslot] = []

            for c in range(8):
                _lda(c)
            for c in range(4):
                _ldx(c)
            for c in range(NCH):
                slot, xslot = c % 8, c % 4
                aw, xw = aws[c], xws[c]
                dst = XN[:, c * 2048 + qo: c * 2048 + qo + 512]
                E("vector", lambda e, slot=slot, d=dst: e.tensor_tensor(
                    d, arB4[slot], rsA[:], op=OP.mult),
                  waits=[aw, stw] + ([b3_half_pe[qs // 2]] if c == 0 else []))
                E("vector", lambda e, d=dst: e.tensor_tensor(d, d, nmA[:], op=OP.add))
                E("vector", lambda e, xslot=xslot, d=dst: e.tensor_tensor(
                    d, d, xrR[xslot], op=OP.add), waits=[xw], inc=sDV)
                arB4_g[slot].append(sDV.now())
                xrR_g[xslot].append(sDV.now())
                yw[c] = sDV.now()
                if c + 8 < NCH:
                    _lda(c + 8)
                if c + 4 < NCH:
                    _ldx(c + 4)
                syq.dma(lambda e, c=c, qo=qo, d=dst: e.dma_start(
                    out=s_y1.ap()[c, :, qo:qo + 512], in_=d), waits=[yw[c]])

        def b4_batch(qs, c0):
            qo = qs * 512
            for c in range(c0, c0 + 8):
                dst = XN[:, c * 2048 + qo: c * 2048 + qo + 512]
                sq = sqb[c % 2]
                E("vector", lambda e, q=sq, d=dst: e.tensor_tensor(q, d, d, op=OP.mult),
                  waits=[b4st["yw"][c]] + sqb_g[c % 2], inc=sDV)
                sqw = sDV.now()
                E("tensor", lambda e, d=dst, qs=qs: e.matmul(
                    psB[32 * qs:32 * qs + 32, :], ones_sb[:, 0:32], d,
                    start=False, stop=False, skip_group_check=True,
                    tile_position=(0, 32 * qs)),
                  waits=[b4st["yw"][c]] + ([b4st["ms"][qs]] if c == c0 else []))
                E("tensor", lambda e, q=sq, qs=qs: e.matmul(
                    psC[32 * qs:32 * qs + 32, :], ones_sb[:, 0:32], q,
                    start=False, stop=False, skip_group_check=True,
                    tile_position=(0, 32 * qs)),
                  waits=[sqw], inc=sPE)
                sqb_g[c % 2] = [sPE.now()]

        def b4_partB(qs):
            qo = qs * 512
            stw2 = stats_finalize32(rsB, nmB, psB[32 * qs:32 * qs + 32, :],
                                    psC[32 * qs:32 * qs + 32, :], [sPE.now()])
            fin2_read[qs] = state["fin_read"]
            wr_done = syq.all()
            for c in range(NCH):
                dst = XN[:, c * 2048 + qo: c * 2048 + qo + 512]
                E("vector", lambda e, d=dst: e.tensor_tensor(d, d, rsB[:], op=OP.mult),
                  waits=([stw2] + wr_done) if c == 0 else [])
                E("vector", lambda e, d=dst: e.tensor_tensor(d, d, nmB[:], op=OP.add),
                  inc=(sDV if c == NCH - 1 else None))
            y1n_q[qs] = sDV.now()
            y1_written[qs] = wr_done

        def b4_sched3(ot):
            if ot == 0:
                b4_partA(0)
            elif ot in (4, 6, 8, 10):
                b4_batch(0, 8 * (ot - 4) // 2)
            elif ot == 11:
                b4_partB(0)
                b4_partA(1)
            elif ot in (14, 16, 18, 20):
                b4_batch(1, 8 * (ot - 14) // 2)
            elif ot == 21:
                b4_partB(1)
            elif 22 <= ot <= 26:
                gstep(0)
                gstep(0)
            elif 27 <= ot <= 31:
                gstep(1)
                gstep(1)

        GA = psA[:, 0:512]
        HB = psA[:, 512:1024]
        hw_rdy = [None] * 4
        gate_psde_ac = [0]
        g5 = {"hw_dv": 0, "ga_free": sAC.c}

        def gate_gen(qs):
            qo = qs * 512
            for c in range(NCH):
                E("tensor", lambda e, c=c, qo=qo: e.matmul(
                    psA[0:33, 0:512], wg9_sb[:, c * 33:(c + 1) * 33],
                    XN[:, c * 2048 + qo: c * 2048 + qo + 512],
                    start=(c == 0), stop=(c == NCH - 1)),
                  waits=(([y1n_q[qs], (sAC, g5["ga_free"]),
                           (sAC, psa_evac[0]), (sAC, psa_evac[1]),
                           (sDV, g5["hw_dv"])] + consts_gp)
                         if c == 0 else []))
            yield
            for c in range(NCH):
                E("tensor", lambda e, c=c, qo=qo: e.matmul(
                    HB, acat_sb[:, c * 128:(c + 1) * 128],
                    XN[:, c * 2048 + qo: c * 2048 + qo + 512],
                    start=(c == 0), stop=(c == NCH - 1)),
                  waits=([acat_w] if c == 0 else []),
                  inc=(sPE if c == NCH - 1 else None))
            gh_mm = sPE.now()
            yield
            E("scalar", lambda e: e.activation(gvA[0:8, :], psA[0:8, 0:512], AF.Exp),
              waits=[gh_mm], inc=sAC)
            E("scalar", lambda e: e.activation(gvA[32:33, :], psA[32:33, 0:512], AF.Tanh,
                                               scale=0.5, bias=bthr2[32:33, :]), inc=sAC)
            g5["ga_free"] = sAC.c
            gw = sAC.now()
            E("vector", lambda e: e.tensor_scalar(gvA[32:33, :], gvA[32:33, :], 0.0625, 0.0625,
                                                  op0=OP.mult, op1=OP.add), waits=[gw])
            E("vector", lambda e: e.tensor_copy(ebf, gvA[0:8, :]))
            E("vector", lambda e: e.tensor_copy(tbf, gvA[32:33, :]), inc=sDV)
            bfw = sDV.now()
            yield
            E("tensor", lambda e: e.matmul(psA[0:8, 0:512], ones_sb[0:8, 0:8], ebf,
                                           start=True, stop=True),
              waits=[bfw, (sAC, g5["ga_free"])])
            E("tensor", lambda e: e.matmul(psD[0:8, :], ones_sb[0:1, 0:8], tbf,
                                           start=True, stop=True,
                                           skip_group_check=True),
              waits=[(sAC, gate_psde_ac[0])], inc=sPE)
            sbw = sPE.now()
            yield
            E("scalar", lambda e: e.copy(gvB[0:8, :], psD[0:8, :]), waits=[sbw], inc=sAC)
            gate_psde_ac[0] = sAC.c
            E("vector", lambda e: e.tensor_tensor(gvC[0:8, :], psA[0:8, 0:512], gvB[0:8, :],
                                                  op=OP.mult), waits=[sAC.now()])
            E("vector", lambda e: e.tensor_sub(gvA[0:8, :], gvA[0:8, :], gvC[0:8, :]))
            E("vector", lambda e: e.tensor_scalar_max(gvA[0:8, :], gvA[0:8, :], 0.0))
            E("vector", lambda e: e.tensor_copy(webf, gvA[0:8, :]), inc=sDV)
            wew = sDV.now()
            yield
            E("tensor", lambda e: e.matmul(psA[0:8, 0:512], ones_sb[0:8, 0:8], webf,
                                           start=True, stop=True), waits=[wew], inc=sPE)
            wsw = sPE.now()
            yield
            E("scalar", lambda e: e.copy(gvB[0:8, :], psA[0:8, 0:512]), waits=[wsw], inc=sAC)
            g5["ga_free"] = sAC.c
            E("vector", lambda e: e.tensor_scalar(gvC[0:8, :], gvB[0:8, :], 0.0, None,
                                                  op0=OP.is_equal), waits=[sAC.now()])
            E("vector", lambda e: e.tensor_add(gvB[0:8, :], gvB[0:8, :], gvC[0:8, :]))
            E("vector", lambda e: e.reciprocal(gvB[0:8, :], gvB[0:8, :]))
            E("vector", lambda e: e.tensor_mul(WN[:], gvA[0:8, :], gvB[0:8, :]), inc=sDV)
            wnw = sDV.now()
            yield
            E("tensor", lambda e: e.matmul(psD[:], e16_sb[:], WN[:], start=True, stop=True),
              waits=[wnw, sAC.now()] + consts_sy, inc=sPE)
            exw = sPE.now()
            yield
            E("scalar", lambda e: e.copy(wful[:], psD[:]), waits=[exw], inc=sAC)
            gate_psde_ac[0] = sAC.c
            E("vector", lambda e, qo=qo, qs=qs: e.tensor_tensor(
                hw_all[:, qo:qo + 512], HB, wful[:], op=OP.mult),
              waits=[sAC.now()], inc=sDV)
            g5["hw_dv"] = sDV.c
            hw_rdy[qs] = sDV.now()

        gq = {q: gate_gen(q) for q in range(4)}

        def gstep(q):
            g6 = gq.get(q)
            if g6 is not None:
                try:
                    next(g6)
                except StopIteration:
                    gq.pop(q, None)

        def gdrain(q):
            while q in gq:
                gstep(q)

        for half in (0, 1):
            for ot in range(NCH):
                par = ot % 2
                wl = gpq.dma(lambda e, ot=ot, par=par: e.dma_start(
                    out=wcol[par][:, 0:D], in_=wo.ap()[ot]),
                    waits=[(sPE, wcol_free[par])])
                for qs in (2 * half, 2 * half + 1):
                    for c in range(NCH):
                        E("tensor", lambda e, c=c, qs=qs, par=par: e.matmul(
                            psA[:, qs * 512:qs * 512 + 512],
                            wcol[par][:, c * 128:(c + 1) * 128],
                            XN[:, c * 2048 + qs * 512: c * 2048 + qs * 512 + 512],
                            start=(c == 0), stop=(c == NCH - 1)),
                          waits=(([wl] + (acat_ld if half == 0 and ot == 0 and qs == 0 else []))
                                 if c == 0 and qs == 2 * half else [])
                                + ([(sAC, psa_evac[qs])] if c == 0 and ot >= 1 else []))
                    if len(pend3) >= 2:
                        pend3.pop(0)()
                    E("tensor", None, inc=sPE)
                    mmw = sPE.now()
                    if qs == 2 * half + 1:
                        wcol_free[par] = sPE.c
                    ei = qs % 2
                    E("scalar", lambda e, ei=ei, qs=qs: e.copy(
                        estage[ei][:], psA[:, qs * 512:qs * 512 + 512]),
                      waits=[mmw] + (est_free[ei] if est_free[ei] else []),
                      inc=sAC)
                    psa_evac[qs] = sAC.c
                    evt = sAC.now()
                    tok = syq.dma(lambda e, ot=ot, qs=qs, ei=ei: e.dma_start(
                        out=s_at.ap()[ot, :, qs * 512:qs * 512 + 512], in_=estage[ei][:]),
                        waits=[evt])
                    sq = sqr[ei]
                    E("scalar", lambda e, ei=ei, q=sq: e.activation(q[:], estage[ei][:], AF.Square),
                      waits=sq_g[ei], inc=sAC)
                    sqw = sAC.now()

                    def _stats3(ei=ei, qs=qs, ot=ot, evt=evt, sqw=sqw, sq=sq, tok=tok):
                        E("tensor", lambda e, ei=ei, qs=qs: e.matmul(
                            psB[32 * qs:32 * qs + 32, :], ones_sb[:, 0:32], estage[ei][:],
                            start=False, stop=False, skip_group_check=True,
                            tile_position=(0, 32 * qs)),
                          waits=[evt] + ([b3ms] if ot == 0 else []))
                        E("tensor", lambda e, q=sq, qs=qs: e.matmul(
                            psC[32 * qs:32 * qs + 32, :], ones_sb[:, 0:32], q[:],
                            start=False, stop=False, skip_group_check=True,
                            tile_position=(0, 32 * qs)),
                          waits=[sqw], inc=sPE)
                        sq_g[ei] = [sPE.now()]
                        est_free[ei] = [tok, sPE.now()]
                    pend3.append(_stats3)
                    at_written[ot][qs] = tok
                if half == 1:
                    b4_sched3(ot)
            while pend3:
                pend3.pop(0)()
            b3_half_pe[half] = sPE.now()


        gdrain(0)
        gdrain(1)

        ms5 = [None] * NQS

        def b5_memset(qs):
            E("vector", lambda e, qs=qs: e.memset(psB[32 * qs:32 * qs + 32, :], 0.0),
              waits=[(sAC, fin2_read[qs])])
            E("vector", lambda e, qs=qs: e.memset(psC[32 * qs:32 * qs + 32, :], 0.0),
              inc=sDV)
            ms5[qs] = sDV.now()

        b5_memset(0)
        b5_memset(1)

        def b4_sched5(ot):
            if ot == 0:
                b4_partA(2)
            elif ot in (4, 6, 8, 10):
                b4_batch(2, 8 * (ot - 4) // 2)
            elif ot == 11:
                b4_partB(2)
                b5_memset(2)
                b4_partA(3)
            elif ot in (14, 16, 18, 20):
                b4_batch(3, 8 * (ot - 14) // 2)
            elif ot == 21:
                b4_partB(3)
                b5_memset(3)
            elif 22 <= ot <= 26:
                gstep(2)
                gstep(2)
            elif 27 <= ot <= 31:
                gstep(3)
                gstep(3)

        wcol_free = [sPE.c, sPE.c]
        psb_evac5 = [sAC.c, sAC.c]
        ff_written = [[None] * NQS for _ in range(NCH)]
        pend5 = []
        b5_half_pe = [None, None]
        for half in (0, 1):
            if half == 1:
                gdrain(2)
                gdrain(3)
            for ot in range(NCH):
                par = ot % 2
                wl = gpq.dma(lambda e, ot=ot, par=par: e.dma_start(
                    out=wcol[par][:], in_=wb.ap()[ot]),
                    waits=[(sPE, wcol_free[par])])
                for qs in (2 * half, 2 * half + 1):
                    bank = psA[:, 1024:1536] if qs % 2 == 0 else psA[:, 1536:2048]
                    for c in range(NCH):
                        E("tensor", lambda e, c=c, qs=qs, par=par, bank=bank: e.matmul(
                            bank, wcol[par][:, c * 128:(c + 1) * 128],
                            XN[:, c * 2048 + qs * 512: c * 2048 + qs * 512 + 512],
                            start=(c == 0), stop=False),
                          waits=(([wl] if c == 0 and qs == 2 * half else [])
                                 + (([y1n_q[qs]]
                                     + ([(sAC, psa_evac[2 + qs % 2])] if half == 0 else []))
                                    if c == 0 and ot == 0 else [])
                                + ([(sAC, psb_evac5[qs % 2])] if c == 0 and (ot > 0 or half == 1) else [])))
                    E("tensor", lambda e, qs=qs, par=par, bank=bank: e.matmul(
                        bank, wcol[par][:, D:D + 128],
                        hw_all[:, qs * 512:qs * 512 + 512],
                        start=False, stop=True),
                      waits=[hw_rdy[qs]] if ot == 0 else [])
                    if len(pend5) >= 2:
                        pend5.pop(0)()
                    E("tensor", None, inc=sPE)
                    mmw = sPE.now()
                    if qs == 2 * half + 1:
                        wcol_free[par] = sPE.c
                    ei = qs % 2
                    E("scalar", lambda e, ei=ei, bank=bank, ot=ot: e.add(
                        estage[ei][:], bank, bbase_sb[:, ot:ot + 1]),
                      waits=[mmw] + (est_free[ei] if est_free[ei] else []),
                      inc=sAC)
                    psb_evac5[qs % 2] = sAC.c
                    evt = sAC.now()
                    tok = syq.dma(lambda e, ot=ot, qs=qs, ei=ei: e.dma_start(
                        out=s_ff.ap()[ot, :, qs * 512:qs * 512 + 512], in_=estage[ei][:]),
                        waits=[evt])
                    sq = sqr[ei]
                    E("scalar", lambda e, ei=ei, q=sq: e.activation(q[:], estage[ei][:], AF.Square),
                      waits=sq_g[ei], inc=sAC)
                    sqw = sAC.now()

                    def _stats5(ei=ei, qs=qs, ot=ot, evt=evt, sqw=sqw, sq=sq, tok=tok):
                        E("tensor", lambda e, ei=ei, qs=qs: e.matmul(
                            psB[32 * qs:32 * qs + 32, :], ones_sb[:, 0:32], estage[ei][:],
                            start=False, stop=False, skip_group_check=True,
                            tile_position=(0, 32 * qs)),
                          waits=[evt] + ([ms5[qs]] if ot == 0 else []))
                        E("tensor", lambda e, q=sq, qs=qs: e.matmul(
                            psC[32 * qs:32 * qs + 32, :], ones_sb[:, 0:32], q[:],
                            start=False, stop=False, skip_group_check=True,
                            tile_position=(0, 32 * qs)),
                          waits=[sqw], inc=sPE)
                        sq_g[ei] = [sPE.now()]
                        est_free[ei] = [tok, sPE.now()]
                    pend5.append(_stats5)
                    ff_written[ot][qs] = tok
                if half == 0:
                    b4_sched5(ot)
            while pend5:
                pend5.pop(0)()
            b5_half_pe[half] = sPE.now()
        b5_pe = b5_half_pe[1]

        vt_f = Vt.ap().bitcast(F32)
        ysR = [kr_f[:, 0:512], kr_f[:, 512:1024], vt_f[:, 0:512], vt_f[:, 512:1024],
               cos_f[:, 0:512], cos_f[:, 512:1024], ss_f[:, 0:512], ss_f[:, 512:1024]]
        ysR_g = [[] for _ in range(8)]
        arR6 = ([P_sb[0][:, i * 512:(i + 1) * 512] for i in range(4)]
                + [acat_sb[:, i * 512:(i + 1) * 512] for i in range(4)])
        arR6_g = [[hw_rdy[3]] for _ in range(4)] + [[acat_w] for _ in range(4)]
        xrR6 = xrR + [obuf[1][:]]
        xrR_g6 = [list(xrR_g[i]) for i in range(4)] + [list(sqb_g[1])]
        for qs in range(NQS):
            qo = qs * 512
            hf = qs // 2
            stw = stats_finalize32(rsA, nmA, psB[32 * qs:32 * qs + 32, :],
                                   psC[32 * qs:32 * qs + 32, :], [b5_half_pe[hf]])
            aws, xws = {}, {}

            def _lda6(c, qs=qs, qo=qo, hf=hf):
                slot = c % 8
                aws[c] = syq.dma(lambda e, c=c, slot=slot, qo=qo: e.dma_start(
                    out=arR6[slot], in_=s_ff.ap()[c, :, qo:qo + 512]),
                    waits=[ff_written[c][qs], b5_half_pe[hf]] + arR6_g[slot])
                arR6_g[slot] = []

            def _ldy6(c, qs=qs, qo=qo):
                xslot = c % 5
                xws[c] = gpq.dma(lambda e, c=c, xslot=xslot, qo=qo: e.dma_start(
                    out=xrR6[xslot], in_=s_y1.ap()[c, :, qo:qo + 512]),
                    waits=list(y1_written[qs]) + xrR_g6[xslot])
                xrR_g6[xslot] = []

            for c in range(8):
                _lda6(c)
            for c in range(5):
                _ldy6(c)
            for c in range(NCH):
                slot = c % 8
                xslot = c % 5
                aw = aws[c]
                yw = xws[c]
                ypar = c % 8
                E("vector", lambda e, slot=slot: e.tensor_tensor(
                    obuf[0][:], arR6[slot], rsA[:], op=OP.mult), waits=[aw, stw])
                E("vector", lambda e: e.tensor_tensor(obuf[0][:], obuf[0][:], nmA[:], op=OP.add))
                E("vector", lambda e, xslot=xslot, ypar=ypar: e.tensor_tensor(
                    ysR[ypar], obuf[0][:], xrR6[xslot], op=OP.add),
                  waits=[yw] + ysR_g[ypar], inc=sDV)
                arR6_g[slot].append(sDV.now())
                xrR_g6[xslot].append(sDV.now())
                if c + 8 < NCH:
                    _lda6(c + 8)
                if c + 5 < NCH:
                    _ldy6(c + 5)
                ysR_g[ypar] = [syq.dma(lambda e, c=c, qo=qo, ypar=ypar: e.dma_start(
                    out=y2.ap()[c, :, qo:qo + 512], in_=ysR[ypar]),
                    waits=[sDV.now()])]
        E("sync", None, waits=syq.all())

        with nc.Block() as block:
            @block.sync
            def _(e):
                p.replay("sync", e)

            @block.gpsimd
            def _(e):
                p.replay("gpsimd", e)

            @block.tensor
            def _(e):
                p.replay("tensor", e)

            @block.scalar
            def _(e):
                p.replay("scalar", e)

            @block.vector
            def _(e):
                p.replay("vector", e)

    return nc


def _rope_tables():
    inv_freq = 1.0 / 10000.0 ** (np.arange(0, DH, 2, dtype=np.float32) / DH)
    t = np.arange(S, dtype=np.float32)
    f = np.outer(t, inv_freq)
    cos = np.zeros((S, DH), dtype=np.float32)
    sin = np.zeros((S, DH), dtype=np.float32)
    cos[:, 0::2] = np.cos(f)
    cos[:, 1::2] = np.cos(f)
    sin[:, 0::2] = np.sin(f)
    sin[:, 1::2] = np.sin(f)
    return cos, sin


def host_inputs(L0_attn_x, L0_attn_Wq, L0_attn_Wk, L0_attn_Wv, L0_attn_Wo,
                L0_ffn_W_base, L0_ffn_b_base, L0_ffn_W_gate, L0_ffn_W_thr,
                L0_ffn_b_thr, L0_ffn_A_w, L0_ffn_B_w):
    x = np.asarray(L0_attn_x, dtype=np.float32)
    Wq = np.asarray(L0_attn_Wq, dtype=np.float32)
    Wk = np.asarray(L0_attn_Wk, dtype=np.float32)
    Wv = np.asarray(L0_attn_Wv, dtype=np.float32)
    Wo = np.asarray(L0_attn_Wo, dtype=np.float32)
    Wb = np.asarray(L0_ffn_W_base, dtype=np.float32)
    bb = np.asarray(L0_ffn_b_base, dtype=np.float32)
    Wg = np.asarray(L0_ffn_W_gate, dtype=np.float32)
    Wt = np.asarray(L0_ffn_W_thr, dtype=np.float32)
    bt = np.asarray(L0_ffn_b_thr, dtype=np.float32)
    Aw = np.asarray(L0_ffn_A_w, dtype=np.float32)
    Bw = np.asarray(L0_ffn_B_w, dtype=np.float32)

    def slab(W):
        return np.ascontiguousarray(W.reshape(32, 128, 32, 128).transpose(0, 3, 2, 1)
                                    .reshape(32, 128, D))

    wq_s = slab(Wq)
    wo_s = slab(Wo)
    wb_s = slab(Wb)
    B2 = Bw.transpose(0, 2, 1).reshape(128, D)
    b2col = np.ascontiguousarray(B2.reshape(128, 32, 128).transpose(1, 0, 2))
    wb_s = np.ascontiguousarray(np.concatenate([wb_s, b2col], axis=2))

    def kvlay(W):
        return np.ascontiguousarray(
            W.T.reshape(32, 128, 128).transpose(1, 0, 2).reshape(128, D))
    wk_l = kvlay(Wk)
    wv_l = kvlay(Wv)

    Wg33 = np.zeros((33, D), np.float32)
    Wg33[0:8] = Wg
    Wg33[32] = Wt
    wg9_l = np.ascontiguousarray(
        Wg33.T.reshape(32, 128, 33).transpose(1, 0, 2).reshape(128, 32 * 33))
    Acat = Aw.reshape(128, D)
    acat_l = np.ascontiguousarray(
        Acat.T.reshape(32, 128, 128).transpose(1, 0, 2).reshape(128, D))
    bbase_l = np.ascontiguousarray(bb.reshape(32, 128).T)
    bthr_l = np.ascontiguousarray(bt.reshape(1, 1))

    ones_c = np.ones((128, 128), dtype=BF)
    e16_c = np.zeros((8, 128), dtype=np.float32)
    for e in range(8):
        e16_c[e, e * 16:(e + 1) * 16] = 2.0
    e16_c = e16_c.astype(BF)
    cosT, sinT = _rope_tables()

    al = np.linspace(0.0, 1.0, 256, dtype=np.float32)
    alC_c = np.ascontiguousarray(np.broadcast_to(al, (128, 256))).astype(BF)
    alD_c = np.ascontiguousarray(np.broadcast_to(1.0 - al, (128, 256))).astype(BF)
    al1 = np.ones((128, 256), dtype=BF)
    al0 = np.zeros((128, 256), dtype=BF)

    in_maps = []
    for core in range(NCORE):
        seq = core // 4
        c0 = (core % 4) * TOK
        lo, hi = c0 - 256, c0 + TOK + 256
        lo_c, hi_c = max(lo, 0), min(hi, S)
        xs = x[seq, lo_c:hi_c]
        xs = np.pad(xs, ((lo_c - lo, hi - hi_c), (0, 0)))
        xin_l = np.ascontiguousarray(xs.T.reshape(32, 128, HALO))
        pos = np.clip(np.arange(lo, hi), 0, S - 1)
        cos_l = np.ascontiguousarray(cosT[pos].T).astype(BF)
        ssv = sinT[pos].T.copy()
        ssv[0:64] *= -1.0
        ss_l = np.ascontiguousarray(ssv).astype(BF)
        maskv = np.zeros(HALO, dtype=np.float32)
        if lo < 0:
            maskv[: -lo] = MASKVAL
        if hi > S:
            maskv[-(hi - S):] = MASKVAL
        mask_l = np.ascontiguousarray(
            np.broadcast_to(maskv.reshape(20, 128).T, (128, 20)))
        first = (core % 4 == 0)
        in_maps.append(dict(
            xin=xin_l, wq=wq_s, wo=wo_s, wb=wb_s, wk=wk_l, wv=wv_l,
            wg9=wg9_l, acat=acat_l, bbase=bbase_l, bthr=bthr_l,
            ones_in=ones_c, e16_in=e16_c, cos_in=cos_l, ss_in=ss_l,
            alA_in=(al1 if first else alC_c), alB_in=(al0 if first else alD_c),
            alC_in=alC_c, alD_in=alD_c, mask_in=mask_l))
    return in_maps


def kernel(**inputs):
    nc = build_nc()
    in_maps = host_inputs(**inputs)
    res = run_bass_kernel_spmd(nc, in_maps, core_ids=list(range(NCORE)))
    out = np.empty((Bb, S, D), dtype=np.float32)
    for core in range(NCORE):
        y = np.asarray(res.results[core]["y2"])
        seq = core // 4
        c0 = (core % 4) * TOK
        out[seq, c0:c0 + TOK] = y.reshape(D, TOK).T
    return out


# version 19
# speedup vs baseline: 1.2010x; 1.0059x over previous
import contextlib
import numpy as np
import ml_dtypes
import concourse.bass as bass
import concourse.mybir as mybir
from concourse.bass_utils import run_bass_kernel_spmd

BF = ml_dtypes.bfloat16
F32 = mybir.dt.float32
BF16 = mybir.dt.bfloat16
AF = mybir.ActivationFunctionType
OP = mybir.AluOpType

NCORE = 8
Bb, S, D = 2, 8192, 4096
TOK = 2048
HALO = 2560
NCH = 32
H, DH = 32, 128
SCALE = float(1.0 / np.sqrt(128.0))
NW = 9
NQS = 4
EPS = 1e-5
MASKVAL = -30000.0


class Sem:
    def __init__(self, h, step):
        self.h = h
        self.c = 0
        self.step = step

    def now(self):
        return (self, self.c)


class Prog:
    ENG = ("sync", "gpsimd", "tensor", "scalar", "vector")

    def __init__(self):
        self.ops = {k: [] for k in self.ENG}

    def emit(self, eng, build=None, waits=(), inc=None):
        if inc is not None:
            inc.c += inc.step
        self.ops[eng].append((build, tuple(waits), inc))

    def replay(self, eng_name, e):
        auto_drain = eng_name in ("vector", "scalar")
        prev_real = False
        for build, waits, inc in self.ops[eng_name]:
            for s, v in waits:
                if v > 0:
                    e.wait_ge(s.h, v)
            if build is not None and auto_drain and prev_real:
                e.drain()
            ins = build(e) if build is not None else None
            if build is not None:
                prev_real = True
            if inc is not None:
                if ins is None:
                    ins = e.drain()
                ins.then_inc(inc.h, inc.step)


class DmaQ:

    def __init__(self, prog, eng, nc, es, name, n=8):
        self.p = prog
        self.eng = eng
        self.sems = [Sem(es.enter_context(nc.semaphore(f"{name}{i}")), 16)
                     for i in range(n)]
        self.i = 0

    def dma(self, build, waits=()):
        s = self.sems[self.i]
        self.i = (self.i + 1) % len(self.sems)
        w = list(waits) + ([(s, s.c)] if s.c else [])
        self.p.emit(self.eng, build, waits=w, inc=s)
        return (s, s.c)

    def all(self):
        return [(s, s.c) for s in self.sems if s.c]


def build_nc(dbg=False):
    nc = bass.Bass("TRN2", target_bir_lowering=False, dynamic_dma_scratch_size=16384)

    def din(name, shape, dt):
        return nc.dram_tensor(name, list(shape), dt, kind="ExternalInput")

    def dscr(name, shape, dt):
        return nc.dram_tensor(name, list(shape), dt,
                              kind="ExternalOutput" if dbg else "Internal")

    xin = din("xin", (NCH, 128, HALO), F32)
    wq = din("wq", (NCH, 128, D), F32)
    wo = din("wo", (NCH, 128, D), F32)
    wb = din("wb", (NCH, 128, D + 128), F32)
    wk = din("wk", (128, D), F32)
    wv = din("wv", (128, D), F32)
    wg9 = din("wg9", (128, NCH * 33), F32)
    acat = din("acat", (128, D), F32)
    bbase = din("bbase", (128, 32), F32)
    bthr = din("bthr", (1, 1), F32)
    ones_in = din("ones_in", (128, 128), BF16)
    e16_in = din("e16_in", (8, 128), BF16)
    cos_in = din("cos_in", (128, HALO), BF16)
    ss_in = din("ss_in", (128, HALO), BF16)
    alA_in = din("alA_in", (128, 256), BF16)
    alB_in = din("alB_in", (128, 256), BF16)
    alC_in = din("alC_in", (128, 256), BF16)
    alD_in = din("alD_in", (128, 256), BF16)
    mask_in = din("mask_in", (128, 20), F32)

    s_xn = dscr("s_xn", (NCH, 128, HALO), BF16)
    s_q = dscr("s_q", (H, 128, TOK), BF16)
    s_at = dscr("s_at", (NCH, 128, TOK), BF16)
    s_y1 = dscr("s_y1", (NCH, 128, TOK), BF16)
    s_ff = dscr("s_ff", (NCH, 128, TOK), BF16)
    s_xb = dscr("s_xb", (NCH, 128, TOK), BF16)
    y2 = nc.dram_tensor("y2", [NCH, 128, TOK], F32, kind="ExternalOutput")

    p = Prog()
    E = p.emit

    with contextlib.ExitStack() as es:
        def sb(name, shape, dt):
            return es.enter_context(nc.sbuf_tensor(name, list(shape), dt))

        BIG = sb("BIG", (128, 65536), BF16)
        wcol = [sb(f"wcol{i}", (128, D + 128), BF16) for i in range(2)]
        ones_sb = sb("ones_sb", (128, 128), BF16)
        e16_sb = sb("e16_sb", (8, 128), BF16)
        cos_sb = sb("cos_sb", (128, HALO), BF16)
        ss_sb = sb("ss_sb", (128, HALO), BF16)
        alA = sb("alA", (128, 256), BF16)
        alB = sb("alB", (128, 256), BF16)
        alC = sb("alC", (128, 256), BF16)
        alD = sb("alD", (128, 256), BF16)
        mask_sb = sb("mask_sb", (128, 20), F32)
        bbase_sb = sb("bbase_sb", (128, 32), F32)
        bthr_sb = sb("bthr_sb", (1, 1), F32)
        bthr2 = sb("bthr2", (33, 1), F32)
        eps_sb = sb("eps_sb", (128, 1), F32)
        wg9_sb = sb("wg9_sb", (128, NCH * 33), BF16)
        acat_sb = sb("acat_sb", (128, D), BF16)
        Kr = sb("Kr", (128, HALO), BF16)
        Vt = sb("Vt", (128, HALO), BF16)
        qh = sb("qh", (128, TOK), BF16)
        estage = [sb(f"estage{i}", (128, 512), BF16) for i in range(2)]
        P_sb = [sb(f"P{i}", (128, 2048), BF16) for i in range(2)]
        obuf = [sb(f"obuf{i}", (128, 512), BF16) for i in range(2)]
        T1 = sb("T1", (128, 512), F32)
        T2 = sb("T2", (128, 512), F32)
        T3 = sb("T3", (128, 512), F32)
        rsA = sb("rsA", (128, 512), BF16)
        nmA = sb("nmA", (128, 512), BF16)
        sqr = [sb(f"sqr{i}", (128, 512), BF16) for i in range(2)]
        ar = [sb(f"ar{i}", (128, 512), BF16) for i in range(2)]
        xr = [sb(f"xr{i}", (128, 512), BF16) for i in range(2)]
        cos_f = cos_sb.ap().bitcast(F32)
        ss_f = ss_sb.ap().bitcast(F32)
        kr_f = Kr.ap().bitcast(F32)
        T2b = T2.ap().bitcast(BF16)
        gvA = cos_f[:, 0:512]
        gvB = cos_f[:, 512:1024]
        gvC = ss_f[:, 0:512]
        WN = ss_sb[0:8, 2048:2560]
        ebf = P_sb[0][0:8, 512:1024]
        tbf = P_sb[0][0:1, 1024:1536]
        webf = Vt[0:8, 0:512]
        wful = P_sb[0][:, 0:512]
        hw_all = P_sb[1][:]
        ystage = [kr_f[:, 0:512], kr_f[:, 512:1024]]
        rsB = Vt[:, 2048:2560]
        nmB = cos_sb[:, 2048:2560]

        psA = es.enter_context(nc.psum_tensor("psA", [128, 2048], F32))
        psB = es.enter_context(nc.psum_tensor("psB", [128, 512], F32))
        psC = es.enter_context(nc.psum_tensor("psC", [128, 512], F32))
        psD = es.enter_context(nc.psum_tensor("psD", [128, 512], F32))
        psE = es.enter_context(nc.psum_tensor("psE", [128, 512], F32))

        sPE = Sem(es.enter_context(nc.semaphore("sPE")), 1)
        sAC = Sem(es.enter_context(nc.semaphore("sAC")), 1)
        sDV = Sem(es.enter_context(nc.semaphore("sDV")), 1)
        syq = DmaQ(p, "sync", nc, es, "sy", 8)
        gpq = DmaQ(p, "gpsimd", nc, es, "gp", 8)

        state = {"psbc_ac": 0, "psde_ac": 0, "stats_g": [], "g_ln2": [0, 0]}
        ar_g = [[], []]
        xr_g = [[], []]
        sq_g = [[], []]
        ys_g = [[], []]

        for dst, src in ((ones_sb, ones_in), (e16_sb, e16_in), (cos_sb, cos_in),
                         (ss_sb, ss_in), (alA, alA_in), (alB, alB_in),
                         (alC, alC_in), (alD, alD_in), (mask_sb, mask_in),
                         (bbase_sb, bbase), (bthr_sb, bthr)):
            syq.dma(lambda e, d=dst, s=src: e.dma_start(out=d[:], in_=s.ap()))
        consts_sy = syq.all()
        gpq.dma(lambda e: e.dma_start(out=wg9_sb[:], in_=wg9.ap()))
        wk_sb = BIG[:, 49152:53248]
        wv_sb = BIG[:, 53248:57344]
        gpq.dma(lambda e: e.dma_start(out=wk_sb, in_=wk.ap()))
        gpq.dma(lambda e: e.dma_start(out=wv_sb, in_=wv.ap()))
        consts_gp = gpq.all()
        E("scalar", lambda e: e.mul(bthr2[0:1, :], bthr_sb[:], 0.5),
          waits=consts_sy, inc=sAC)
        E("vector", lambda e: e.tensor_copy(bthr2[32:33, :], bthr2[0:1, :]),
          waits=[sAC.now()], inc=sDV)
        E("vector", lambda e: e.memset(eps_sb[:], EPS), inc=sDV)

        def stats_finalize(rstd_bf, nmu_bf, mm_wait, pb=None, pc=None, gk2=None):
            pb = psB if pb is None else pb
            pc = psC if pc is None else pc
            E("scalar", lambda e: e.mul(T1[:], pb[:], -1.0 / D),
              waits=[mm_wait] + state["stats_g"], inc=sAC)
            E("scalar", lambda e: e.activation(T2[:], pb[:], AF.Square, scale=1.0 / D), inc=sAC)
            E("scalar", lambda e: e.mul(T3[:], pc[:], 1.0 / D), inc=sAC)
            if gk2 is not None:
                state["g_ln2"][gk2] = sAC.c
            else:
                state["psbc_ac" if pb is psB else "psde_ac"] = sAC.c
            aw = sAC.now()
            E("vector", lambda e: e.tensor_sub(T3[:], T3[:], T2[:]), waits=[aw], inc=sDV)
            vw = sDV.now()
            E("scalar", lambda e: e.activation(T2[:], T3[:], AF.Sqrt, bias=eps_sb.ap()),
              waits=[vw], inc=sAC)
            aw2 = sAC.now()
            E("vector", lambda e: e.reciprocal(T3[:], T2[:]), waits=[aw2])
            E("vector", lambda e, r=rstd_bf: e.tensor_copy(r[:], T3[:]))
            E("vector", lambda e, m=nmu_bf: e.tensor_tensor(m[:], T1[:], T3[:], op=OP.mult), inc=sDV)
            state["stats_g"] = [sDV.now()]
            return sDV.now()

        def stats_finalize32(rstd_bf, nmu_bf, pbrow, pcrow, waits0):
            E("scalar", lambda e: e.mul(T1[0:32, :], pbrow, -1.0 / D),
              waits=list(waits0) + state["stats_g"], inc=sAC)
            E("scalar", lambda e: e.activation(T2[0:32, :], pbrow, AF.Square, scale=1.0 / D), inc=sAC)
            E("scalar", lambda e: e.mul(T3[0:32, :], pcrow, 1.0 / D), inc=sAC)
            state["fin_read"] = sAC.c
            aw = sAC.now()
            E("vector", lambda e: e.tensor_sub(T3[0:32, :], T3[0:32, :], T2[0:32, :]),
              waits=[aw], inc=sDV)
            vw = sDV.now()
            E("scalar", lambda e: e.activation(T2[0:32, :], T3[0:32, :], AF.Sqrt,
                                               bias=eps_sb[0:32, :]), waits=[vw], inc=sAC)
            aw2 = sAC.now()
            E("vector", lambda e: e.reciprocal(T3[0:32, :], T2[0:32, :]), waits=[aw2])
            E("vector", lambda e, r=rstd_bf: e.tensor_copy(r[0:32, :], T3[0:32, :]))
            E("vector", lambda e, m=nmu_bf: e.tensor_tensor(m[0:32, :], T1[0:32, :],
                                                            T3[0:32, :], op=OP.mult))
            for b in (32, 64, 96):
                E("vector", lambda e, r=rstd_bf, b=b: e.tensor_copy(r[b:b + 32, :], r[0:32, :]))
                E("vector", lambda e, m=nmu_bf, b=b: e.tensor_copy(m[b:b + 32, :], m[0:32, :]))
            E("vector", None, inc=sDV)
            state["stats_g"] = [sDV.now()]
            return sDV.now()

        def rope(dst_slice, src_tile, col0, tmp):
            E("vector", lambda e: e.tensor_copy(tmp[0:64, :], src_tile[64:128, :]))
            E("vector", lambda e: e.tensor_copy(tmp[64:128, :], src_tile[0:64, :]))
            E("vector", lambda e: e.tensor_tensor(tmp[:], tmp[:], ss_sb[:, col0:col0 + 512], op=OP.mult))
            E("vector", lambda e: e.tensor_tensor(dst_slice, src_tile[:], cos_sb[:, col0:col0 + 512], op=OP.mult))
            E("vector", lambda e: e.tensor_tensor(dst_slice, dst_slice, tmp[:], op=OP.add))

        xa_view = [BIG[:, 0:16384], BIG[:, 16384:32768]]
        xnbuf = BIG[:, 32768:49152]
        xa_free = [0, 0]
        xn_free_pe = 0
        xn_wr_tok = []
        xb_wr = [[], []]
        for s in range(5):
            t0 = 512 * s
            par = s % 2
            lw = []
            for g in range(4):
                lw.append(gpq.dma(
                    lambda e, s=s, g=g, par=par: e.dma_start(
                        out=xa_view[par].rearrange("p (c q) -> p c q", c=NCH)[:, g * 8:(g + 1) * 8, :],
                        in_=xin.ap()[g * 8:(g + 1) * 8, :, 512 * s:512 * s + 512].transpose([1, 0, 2])),
                    waits=(consts_gp if s == 0 else [])
                          + ([(sPE, xa_free[par])] if s >= 2 else [])
                          + xb_wr[par]))
            xb_wr[par] = []
            off0 = max(0, 512 * s - 256)
            t0o = max(0, 256 - 512 * s)
            ln = min(512 * s + 512, 2304) - max(512 * s, 256)
            for g in range(4):
                xb_wr[par].append(syq.dma(
                    lambda e, g=g, par=par, t0o=t0o, off0=off0, ln=ln: e.dma_start(
                        out=s_xb.ap()[g * 8:(g + 1) * 8, :, off0:off0 + ln].transpose([1, 0, 2]),
                        in_=xa_view[par].rearrange("p (c q) -> p c q", c=NCH)[:, g * 8:(g + 1) * 8, t0o:t0o + ln]),
                    waits=list(lw)))
            for c in range(NCH):
                sq = sqr[c % 2]
                E("vector", lambda e, c=c, par=par, q=sq: e.tensor_tensor(
                    q[:], xa_view[par][:, c * 512:(c + 1) * 512],
                    xa_view[par][:, c * 512:(c + 1) * 512], op=OP.mult),
                  waits=list(lw) + sq_g[c % 2], inc=sDV)
                sqw = sDV.now()
                E("tensor", lambda e, c=c, par=par: e.matmul(
                    psB[:], ones_sb[:], xa_view[par][:, c * 512:(c + 1) * 512],
                    start=(c == 0), stop=(c == NCH - 1)),
                  waits=list(lw) + (consts_sy + [(sAC, state["psbc_ac"])] if c == 0 else []))
                E("tensor", lambda e, q=sq, c=c: e.matmul(
                    psC[:], ones_sb[:], q[:], start=(c == 0), stop=(c == NCH - 1)),
                  waits=[sqw], inc=sPE)
                sq_g[c % 2] = [sPE.now()]
            stw = stats_finalize(rsA, nmA, sPE.now())
            for c in range(NCH):
                E("vector", lambda e, c=c, par=par: e.tensor_tensor(
                    xnbuf[:, c * 512:(c + 1) * 512], xa_view[par][:, c * 512:(c + 1) * 512],
                    rsA[:], op=OP.mult),
                  waits=([stw, (sPE, xn_free_pe)] + xn_wr_tok if c == 0 else []))
                E("vector", lambda e, c=c: e.tensor_tensor(
                    xnbuf[:, c * 512:(c + 1) * 512], xnbuf[:, c * 512:(c + 1) * 512],
                    nmA[:], op=OP.add), inc=sDV)
                xnw = sDV.now()
                E("tensor", lambda e, c=c: e.matmul(
                    psD[:], wk_sb[:, c * 128:(c + 1) * 128], xnbuf[:, c * 512:(c + 1) * 512],
                    start=(c == 0), stop=(c == NCH - 1)),
                  waits=[xnw] + (consts_gp if (s == 0 and c == 0) else []))
            for t4 in range(4):
                for c in range(NCH):
                    E("tensor", lambda e, t4=t4, c=c: e.matmul(
                        psE[:, t4 * 128:(t4 + 1) * 128],
                        xnbuf[:, c * 512 + t4 * 128:c * 512 + t4 * 128 + 128],
                        wv_sb[:, c * 128:(c + 1) * 128],
                        start=(c == 0), stop=(c == NCH - 1)))
            E("tensor", None, inc=sPE)
            kvw = sPE.now()
            xn_free_pe = sPE.c
            xa_free[par] = sPE.c
            E("scalar", lambda e: e.copy(sqr[0][:], psD[:]),
              waits=[kvw] + sq_g[0] + sq_g[1], inc=sAC)
            E("vector", None, waits=[sAC.now()])
            rope(Kr[:, t0:t0 + 512], sqr[0], t0, sqr[1])
            for t4 in range(4):
                E("vector", lambda e, t4=t4, s=s: e.tensor_copy(
                    Vt[:, (4 * s + t4) * 128:(4 * s + t4) * 128 + 128],
                    psE[:, t4 * 128:(t4 + 1) * 128]),
                  inc=(sDV if t4 == 3 else None))
            kv_done = sDV.now()
            sq_g[0] = sq_g[1] = [sDV.now()]
            xn_wr_tok = []
            for g in range(4):
                xn_wr_tok.append(syq.dma(
                    lambda e, s=s, g=g: e.dma_start(
                        out=s_xn.ap()[g * 8:(g + 1) * 8, :, 512 * s:512 * s + 512].transpose([1, 0, 2]),
                        in_=xnbuf.rearrange("p (c q) -> p c q", c=NCH)[:, g * 8:(g + 1) * 8, :]),
                    waits=[sDV.now()]))
        kv_all = kv_done
        a_pe_done = sPE.now()

        xn_ld = []
        for g in range(4):
            xn_ld.append(syq.dma(
                lambda e, g=g: e.dma_start(
                    out=BIG.rearrange("p (c q) -> p c q", c=NCH)[:, g * 8:(g + 1) * 8, :],
                    in_=s_xn.ap()[g * 8:(g + 1) * 8, :, 256:256 + TOK].transpose([1, 0, 2])),
                waits=syq.all() + [kv_all, a_pe_done]))
        XN = BIG

        o_sbv = T1.ap().bitcast(BF16)
        rd_bfv = T3.ap().bitcast(BF16)
        o_sb = [o_sbv[:, 0:512], o_sbv[:, 512:1024]]
        rd_bf = [rd_bfv[:, 0:512], rd_bfv[:, 512:1024]]
        wcol_free = [0, 0]
        est_free = [None, None]
        psb_evac = [0, 0]
        q_written = []
        qhs = [qh.ap(), acat_sb.ap()[:, 0:TOK]]
        wl_pre = {}
        qw_pre = {}
        b2s = {"sig_slot": [0, 0], "smm_tok": [None, None], "rc_free": [0, 0],
               "oev_free": [0, 0], "pp_free": [0, 0], "qh_free": [0, 0],
               "bst_g": [[], []]}
        blend_toks = [[] for _ in range(H)]

        def emit_b1_ot(ot):
            par = ot % 2
            wl = wl_pre.pop(ot, None)
            if wl is None:
                wl = gpq.dma(lambda e, ot=ot, par=par: e.dma_start(
                    out=wcol[par][:, 0:D], in_=wq.ap()[ot]),
                    waits=[(sPE, wcol_free[par])] if ot >= 2 else [])
            qtoks = []
            for qs in range(NQS):
                yield
                bank = psB if qs % 2 == 0 else psC
                for c in range(NCH):
                    E("tensor", lambda e, c=c, qs=qs, par=par, bank=bank: e.matmul(
                        bank[:],
                        wcol[par][:, c * 128:(c + 1) * 128],
                        XN[:, c * 2048 + qs * 512: c * 2048 + qs * 512 + 512],
                        start=(c == 0), stop=(c == NCH - 1)),
                      waits=([wl] + xn_ld if c == 0 and qs == 0 else [])
                            + ([(sAC, psb_evac[qs % 2])] if c == 0 and (ot > 0 or qs >= 2) else []))
                E("tensor", None, inc=sPE)
                mmw = sPE.now()
                if qs == NQS - 1:
                    wcol_free[par] = sPE.c
                ei = (ot * NQS + qs) % 2
                E("scalar", lambda e, ei=ei, bank=bank: e.copy(estage[ei][:], bank[:]),
                  waits=[mmw] + ([est_free[ei]] if est_free[ei] else []),
                  inc=sAC)
                psb_evac[qs % 2] = sAC.c
                E("vector", None, waits=[sAC.now()] + sq_g[0])
                rope(estage[ei][:], estage[ei], 256 + qs * 512, sqr[0])
                E("vector", None, inc=sDV)
                sq_g[0] = [sDV.now()]
                tok = syq.dma(lambda e, ot=ot, qs=qs, ei=ei: e.dma_start(
                    out=s_q.ap()[ot, :, qs * 512:qs * 512 + 512], in_=estage[ei][:]),
                    waits=[sDV.now()])
                est_free[ei] = tok
                qtoks.append(tok)
                if qs == 1 and ot + 1 < H:
                    npar = (ot + 1) % 2
                    wl_pre[ot + 1] = gpq.dma(lambda e, o1=ot + 1, npar=npar: e.dma_start(
                        out=wcol[npar][:, 0:D], in_=wq.ap()[o1]),
                        waits=[(sPE, wcol_free[npar])])
            q_written.append(qtoks)

        def emit_b2_head(h):
            yield
            par = h % 2
            myq = qhs[par]
            qw = qw_pre.pop(h, None)
            if qw is None:
                qw = syq.dma(lambda e, par=par, h=h: e.dma_start(
                    out=qhs[par], in_=s_q.ap()[h]),
                    waits=list(q_written[h]) + ([(sPE, b2s["qh_free"][par])] if h >= 2 else []))
            sigs = {}
            dn = [psA[:, 1024:1536], psA[:, 1536:2048]]

            def s_mms(w, cpair):
                qlo = max(0, 256 * w - 256)
                qhi = min(TOK, 256 * w + 256)
                wlen = qhi - qlo
                boff = qlo - (256 * w - 256)
                for c in (cpair, cpair + 1):
                    kt = 2 * w + c
                    E("tensor", lambda e, c=c, kt=kt, qlo=qlo, wlen=wlen, mq=myq: e.matmul(
                        psA[:, (c % 2) * 512: (c % 2) * 512 + wlen],
                        Kr[:, kt * 128:kt * 128 + 128],
                        mq[:, qlo:qlo + wlen],
                        start=True, stop=True),
                      waits=([qw, kv_all] if c == cpair else [])
                            + [(sAC, b2s["sig_slot"][c % 2])],
                      inc=sPE)
                    b2s["smm_tok"][c % 2] = sPE.now()
                return boff, wlen

            def sig_half(w, cpair, boff, wlen, pp):
                for c in (cpair, cpair + 1):
                    kt = 2 * w + c
                    masked = (w == 0 or w == NW - 1)
                    E("scalar", lambda e, c=c, kt=kt, boff=boff, wlen=wlen, pp=pp,
                                m=masked: e.activation(
                        P_sb[pp][:, c * 512 + boff: c * 512 + boff + wlen],
                        psA[:, (c % 2) * 512: (c % 2) * 512 + wlen],
                        AF.Sigmoid, scale=SCALE,
                        **(dict(bias=mask_sb[:, kt:kt + 1]) if m else {})),
                      waits=[b2s["smm_tok"][c % 2]]
                            + ([(sPE, b2s["pp_free"][pp])] if c == cpair else []),
                      inc=sAC)
                    b2s["sig_slot"][c % 2] = sAC.c

            for w in range(NW + 1):
                yield
                if w >= 1:
                    wp = w - 1
                    sw, pp_p, boff_p, wlen_p = sigs[wp]
                    ds = wp % 2
                    ob = [psD, psE][ds]
                if w <= NW - 1:
                    pp = (h * NW + w) % 2
                    boff, wlen = s_mms(w, 0)
                    sig_half(w, 0, boff, wlen, pp)
                if w >= 1:
                    for c in range(4):
                        E("tensor", lambda e, c=c, pp=pp_p, boff=boff_p, wlen=wlen_p, ds=ds: e.matmul(
                            dn[ds][:, boff:boff + wlen], ones_sb[:],
                            P_sb[pp][:, c * 512 + boff: c * 512 + boff + wlen],
                            start=(c == 0), stop=(c == 3)),
                          waits=[sw, (sDV, b2s["rc_free"][ds])] if c == 0 else [],
                          inc=(sPE if c == 3 else None))
                    dn_tok = sPE.now()
                if w <= NW - 1:
                    boff, wlen = s_mms(w, 2)
                    if w == NW - 1:
                        b2s["qh_free"][par] = sPE.c
                    sig_half(w, 2, boff, wlen, pp)
                    sigs[w] = (sAC.now(), pp, boff, wlen)
                    if w == 2 and h + 1 < H:
                        hn = h + 1
                        npar = hn % 2
                        qw_pre[hn] = syq.dma(lambda e, h1=hn, npar=npar: e.dma_start(
                            out=qhs[npar], in_=s_q.ap()[h1]),
                            waits=list(q_written[hn]) + [(sPE, b2s["qh_free"][npar])])
                if w >= 1:
                    for c in range(4):
                        kt = 2 * wp + c
                        E("tensor", lambda e, c=c, kt=kt, pp=pp_p, boff=boff_p, wlen=wlen_p, ob=ob: e.matmul(
                            ob[:, boff:boff + wlen],
                            Vt[:, kt * 128:kt * 128 + 128],
                            P_sb[pp][:, c * 512 + boff: c * 512 + boff + wlen],
                            start=(c == 0), stop=(c == 3)),
                          waits=([(sAC, b2s["oev_free"][ds])] if c == 0 else []),
                          inc=(sPE if c == 3 else None))
                    b2s["pp_free"][pp_p] = sPE.c
                    dnw = sPE.now()
                    E("scalar", lambda e, boff=boff_p, wlen=wlen_p, ob=ob, ds=ds: e.copy(
                        o_sb[ds][:, boff:boff + wlen], ob[:, boff:boff + wlen]),
                      waits=[dnw, (sDV, b2s["rc_free"][ds])], inc=sAC)
                    b2s["oev_free"][ds] = sAC.c
                    oev = sAC.now()

                    def _recip(e, boff=boff_p, wlen=wlen_p, ds=ds):
                        with nc.allow_low_precision("bf16 rounding of 1/denom; product is bf16 anyway"):
                            return e.reciprocal(rd_bf[ds][:, boff:boff + wlen], dn[ds][:, boff:boff + wlen])
                    E("vector", _recip, waits=[dn_tok])
                    E("vector", lambda e, wp=wp, boff=boff_p, wlen=wlen_p, ds=ds: e.tensor_tensor(
                        obuf[wp % 2][:, boff:boff + wlen], o_sb[ds][:, boff:boff + wlen],
                        rd_bf[ds][:, boff:boff + wlen], op=OP.mult), waits=[oev], inc=sDV)
                    b2s["rc_free"][ds] = sDV.c
                    if wp >= 1:
                        m = wp - 1
                        bi = m % 2
                        a_new, a_old = (alA, alB) if m == 0 else (alC, alD)
                        E("vector", lambda e, wp=wp, a_old=a_old: e.tensor_tensor(
                            T2b[:, 0:256], obuf[(wp - 1) % 2][:, 256:512], a_old[:], op=OP.mult))
                        E("vector", lambda e, wp=wp, a_new=a_new: e.tensor_tensor(
                            T2b[:, 256:512], obuf[wp % 2][:, 0:256], a_new[:], op=OP.mult))
                        E("vector", lambda e, bi=bi: e.tensor_tensor(
                            T2b[:, 512 + 256 * bi: 768 + 256 * bi],
                            T2b[:, 0:256], T2b[:, 256:512], op=OP.add),
                          waits=b2s["bst_g"][bi], inc=sDV)
                        b2s["bst_g"][bi] = [syq.dma(
                            lambda e, h=h, m=m, bi=bi: e.dma_start(
                                out=s_q.ap()[h, :, 256 * m:256 * m + 256],
                                in_=T2b[:, 512 + 256 * bi: 768 + 256 * bi]),
                            waits=[sDV.now(), qw])]
                        blend_toks[h].append(b2s["bst_g"][bi][0])

        LAG = 2
        acat_ld = []
        b1_pe_guard = [0]
        gens = []

        def _drive(g):
            try:
                next(g)
                return True
            except StopIteration:
                return False

        active_b1 = None
        active_b2 = None
        for k in range(H + LAG):
            if k < H:
                active_b1 = emit_b1_ot(k)
                next(active_b1)
            if k >= LAG:
                active_b2 = emit_b2_head(k - LAG)
                next(active_b2)
            sched = [0, 1, 1, 1, 0, 1, 1, 0, 1, 1, 1, 0, 1, 1, 1, 1]
            for pick in sched:
                if pick == 0:
                    if active_b1 is not None and not _drive(active_b1):
                        active_b1 = None
                else:
                    if active_b2 is not None and not _drive(active_b2):
                        active_b2 = None
            while active_b1 is not None and _drive(active_b1):
                pass
            if k == H - 1:
                b1_pe_guard[0] = sPE.c
                for g in range(3):
                    toks = [t for hh in range(8 * g, 8 * g + 8) for t in blend_toks[hh]]
                    acat_ld.append(syq.dma(
                        lambda e, g=g: e.dma_start(
                            out=BIG.rearrange("p (c q) -> p c q", c=NCH)[:, g * 8:(g + 1) * 8, :],
                            in_=s_q.ap()[g * 8:(g + 1) * 8, :, :].transpose([1, 0, 2])),
                        waits=toks + [(sPE, b1_pe_guard[0])]))
            while active_b2 is not None and _drive(active_b2):
                pass
        toks = [t for hh in range(24, 32) for t in blend_toks[hh]]
        acat_ld.append(syq.dma(
            lambda e: e.dma_start(
                out=BIG.rearrange("p (c q) -> p c q", c=NCH)[:, 24:32, :],
                in_=s_q.ap()[24:32, :, :].transpose([1, 0, 2])),
            waits=toks + [(sPE, b1_pe_guard[0])]))
        acat_w = gpq.dma(lambda e: e.dma_start(out=acat_sb[:], in_=acat.ap()),
                         waits=[(sPE, b2s["qh_free"][1])])
        state["psbc_ac"] = sAC.c
        state["psde_ac"] = sAC.c

        E("vector", lambda e: e.memset(psB[:], 0.0),
          waits=[(sAC, psb_evac[0]), (sAC, psb_evac[1])])
        E("vector", lambda e: e.memset(psC[:], 0.0), inc=sDV)
        b3ms = sDV.now()
        wcol_free = [sPE.c, sPE.c]
        est_free = [None, None]
        psa_evac = [sAC.c] * 4
        at_written = [[None] * NQS for _ in range(NCH)]
        pend3 = []
        b3_half_pe = [None, None]

        arB4 = ([Kr[:, i * 512:(i + 1) * 512] for i in range(5)]
                + [Vt[:, 512 + i * 512:1024 + i * 512] for i in range(3)])
        arB4_g = [[] for _ in range(8)]
        xrR = [qh[:, i * 512:(i + 1) * 512] for i in range(4)]
        xrR_g = [[] for _ in range(4)]
        sqb = [obuf[0][:], obuf[1][:]]
        sqb_g = [[], []]
        y1_written = [None] * NQS
        y1n_q = [None] * NQS
        fin32_read = [None] * NQS
        fin2_read = [None] * NQS
        b4st = {"yw": {}, "ms": {}}

        def b4_partA(qs):
            qo = qs * 512
            stw = stats_finalize32(rsA, nmA, psB[32 * qs:32 * qs + 32, :],
                                   psC[32 * qs:32 * qs + 32, :],
                                   [b3_half_pe[qs // 2]])
            fin32_read[qs] = state["fin_read"]
            E("vector", lambda e, qs=qs: e.memset(psB[32 * qs:32 * qs + 32, :], 0.0),
              waits=[(sAC, fin32_read[qs])])
            E("vector", lambda e, qs=qs: e.memset(psC[32 * qs:32 * qs + 32, :], 0.0),
              inc=sDV)
            b4st["ms"][qs] = sDV.now()
            aws, xws, yw = {}, {}, {}
            b4st["yw"] = yw

            def _lda(c, qs=qs, qo=qo):
                slot = c % 8
                aws[c] = syq.dma(lambda e, c=c, slot=slot, qs=qs, qo=qo: e.dma_start(
                    out=arB4[slot], in_=s_at.ap()[c, :, qo:qo + 512]),
                    waits=[at_written[c][qs]] + arB4_g[slot])
                arB4_g[slot] = []

            def _ldx(c, qs=qs, qo=qo):
                xslot = c % 4
                xws[c] = syq.dma(lambda e, c=c, xslot=xslot, qo=qo: e.dma_start(
                    out=xrR[xslot], in_=s_xb.ap()[c, :, qo:qo + 512]),
                    waits=xrR_g[xslot]
                          + (xn_ld + [b3_half_pe[0]] if qs == 0 and c < 4 else []))
                xrR_g[xslot] = []

            for c in range(8):
                _lda(c)
            for c in range(4):
                _ldx(c)
            for c in range(NCH):
                slot, xslot = c % 8, c % 4
                aw, xw = aws[c], xws[c]
                dst = XN[:, c * 2048 + qo: c * 2048 + qo + 512]
                E("vector", lambda e, slot=slot, d=dst: e.tensor_tensor(
                    d, arB4[slot], rsA[:], op=OP.mult),
                  waits=[aw, stw] + ([b3_half_pe[qs // 2]] if c == 0 else []))
                E("vector", lambda e, d=dst: e.tensor_tensor(d, d, nmA[:], op=OP.add))
                E("vector", lambda e, xslot=xslot, d=dst: e.tensor_tensor(
                    d, d, xrR[xslot], op=OP.add), waits=[xw], inc=sDV)
                arB4_g[slot].append(sDV.now())
                xrR_g[xslot].append(sDV.now())
                yw[c] = sDV.now()
                if c + 8 < NCH:
                    _lda(c + 8)
                if c + 4 < NCH:
                    _ldx(c + 4)
                syq.dma(lambda e, c=c, qo=qo, d=dst: e.dma_start(
                    out=s_y1.ap()[c, :, qo:qo + 512], in_=d), waits=[yw[c]])

        def b4_batch(qs, c0):
            qo = qs * 512
            for c in range(c0, c0 + 8):
                dst = XN[:, c * 2048 + qo: c * 2048 + qo + 512]
                sq = sqb[c % 2]
                E("vector", lambda e, q=sq, d=dst: e.tensor_tensor(q, d, d, op=OP.mult),
                  waits=[b4st["yw"][c]] + sqb_g[c % 2], inc=sDV)
                sqw = sDV.now()
                E("tensor", lambda e, d=dst, qs=qs: e.matmul(
                    psB[32 * qs:32 * qs + 32, :], ones_sb[:, 0:32], d,
                    start=False, stop=False, skip_group_check=True,
                    tile_position=(0, 32 * qs)),
                  waits=[b4st["yw"][c]] + ([b4st["ms"][qs]] if c == c0 else []))
                E("tensor", lambda e, q=sq, qs=qs: e.matmul(
                    psC[32 * qs:32 * qs + 32, :], ones_sb[:, 0:32], q,
                    start=False, stop=False, skip_group_check=True,
                    tile_position=(0, 32 * qs)),
                  waits=[sqw], inc=sPE)
                sqb_g[c % 2] = [sPE.now()]

        def b4_partB(qs):
            qo = qs * 512
            stw2 = stats_finalize32(rsB, nmB, psB[32 * qs:32 * qs + 32, :],
                                    psC[32 * qs:32 * qs + 32, :], [sPE.now()])
            fin2_read[qs] = state["fin_read"]
            wr_done = syq.all()
            for c in range(NCH):
                dst = XN[:, c * 2048 + qo: c * 2048 + qo + 512]
                E("vector", lambda e, d=dst: e.tensor_tensor(d, d, rsB[:], op=OP.mult),
                  waits=([stw2] + wr_done) if c == 0 else [])
                E("vector", lambda e, d=dst: e.tensor_tensor(d, d, nmB[:], op=OP.add),
                  inc=(sDV if c == NCH - 1 else None))
            y1n_q[qs] = sDV.now()
            y1_written[qs] = wr_done

        def b4_sched3(ot):
            if ot == 0:
                b4_partA(0)
            elif ot in (4, 6, 8, 10):
                b4_batch(0, 8 * (ot - 4) // 2)
            elif ot == 11:
                b4_partB(0)
                b4_partA(1)
            elif ot in (14, 16, 18, 20):
                b4_batch(1, 8 * (ot - 14) // 2)
            elif ot == 21:
                b4_partB(1)
            elif 22 <= ot <= 26:
                gstep(0)
                gstep(0)
            elif 27 <= ot <= 31:
                gstep(1)
                gstep(1)

        GA = psA[:, 0:512]
        HB = psA[:, 512:1024]
        hw_rdy = [None] * 4
        gate_psde_ac = [0]
        g5 = {"hw_dv": 0, "ga_free": sAC.c}

        def gate_gen(qs):
            qo = qs * 512
            for c in range(NCH):
                E("tensor", lambda e, c=c, qo=qo: e.matmul(
                    psA[0:33, 0:512], wg9_sb[:, c * 33:(c + 1) * 33],
                    XN[:, c * 2048 + qo: c * 2048 + qo + 512],
                    start=(c == 0), stop=(c == NCH - 1)),
                  waits=(([y1n_q[qs], (sAC, g5["ga_free"]),
                           (sAC, psa_evac[0]), (sAC, psa_evac[1]),
                           (sDV, g5["hw_dv"])] + consts_gp)
                         if c == 0 else []))
            yield
            for c in range(NCH):
                E("tensor", lambda e, c=c, qo=qo: e.matmul(
                    HB, acat_sb[:, c * 128:(c + 1) * 128],
                    XN[:, c * 2048 + qo: c * 2048 + qo + 512],
                    start=(c == 0), stop=(c == NCH - 1)),
                  waits=([acat_w] if c == 0 else []),
                  inc=(sPE if c == NCH - 1 else None))
            gh_mm = sPE.now()
            yield
            E("scalar", lambda e: e.activation(gvA[0:8, :], psA[0:8, 0:512], AF.Exp),
              waits=[gh_mm], inc=sAC)
            E("scalar", lambda e: e.activation(gvA[32:33, :], psA[32:33, 0:512], AF.Tanh,
                                               scale=0.5, bias=bthr2[32:33, :]), inc=sAC)
            g5["ga_free"] = sAC.c
            gw = sAC.now()
            E("vector", lambda e: e.tensor_scalar(gvA[32:33, :], gvA[32:33, :], 0.0625, 0.0625,
                                                  op0=OP.mult, op1=OP.add), waits=[gw])
            E("vector", lambda e: e.tensor_copy(ebf, gvA[0:8, :]))
            E("vector", lambda e: e.tensor_copy(tbf, gvA[32:33, :]), inc=sDV)
            bfw = sDV.now()
            yield
            E("tensor", lambda e: e.matmul(psA[0:8, 0:512], ones_sb[0:8, 0:8], ebf,
                                           start=True, stop=True),
              waits=[bfw, (sAC, g5["ga_free"])])
            E("tensor", lambda e: e.matmul(psD[0:8, :], ones_sb[0:1, 0:8], tbf,
                                           start=True, stop=True,
                                           skip_group_check=True),
              waits=[(sAC, gate_psde_ac[0])], inc=sPE)
            sbw = sPE.now()
            yield
            E("scalar", lambda e: e.copy(gvB[0:8, :], psD[0:8, :]), waits=[sbw], inc=sAC)
            gate_psde_ac[0] = sAC.c
            E("vector", lambda e: e.tensor_tensor(gvC[0:8, :], psA[0:8, 0:512], gvB[0:8, :],
                                                  op=OP.mult), waits=[sAC.now()])
            E("vector", lambda e: e.tensor_sub(gvA[0:8, :], gvA[0:8, :], gvC[0:8, :]))
            E("vector", lambda e: e.tensor_scalar_max(gvA[0:8, :], gvA[0:8, :], 0.0))
            E("vector", lambda e: e.tensor_copy(webf, gvA[0:8, :]), inc=sDV)
            wew = sDV.now()
            yield
            E("tensor", lambda e: e.matmul(psA[0:8, 0:512], ones_sb[0:8, 0:8], webf,
                                           start=True, stop=True), waits=[wew], inc=sPE)
            wsw = sPE.now()
            yield
            E("scalar", lambda e: e.copy(gvB[0:8, :], psA[0:8, 0:512]), waits=[wsw], inc=sAC)
            g5["ga_free"] = sAC.c
            E("vector", lambda e: e.tensor_scalar(gvC[0:8, :], gvB[0:8, :], 0.0, None,
                                                  op0=OP.is_equal), waits=[sAC.now()])
            E("vector", lambda e: e.tensor_add(gvB[0:8, :], gvB[0:8, :], gvC[0:8, :]))
            E("vector", lambda e: e.reciprocal(gvB[0:8, :], gvB[0:8, :]))
            E("vector", lambda e: e.tensor_mul(WN[:], gvA[0:8, :], gvB[0:8, :]), inc=sDV)
            wnw = sDV.now()
            yield
            E("tensor", lambda e: e.matmul(psD[:], e16_sb[:], WN[:], start=True, stop=True),
              waits=[wnw, sAC.now()] + consts_sy, inc=sPE)
            exw = sPE.now()
            yield
            E("scalar", lambda e: e.copy(wful[:], psD[:]), waits=[exw], inc=sAC)
            gate_psde_ac[0] = sAC.c
            E("vector", lambda e, qo=qo, qs=qs: e.tensor_tensor(
                hw_all[:, qo:qo + 512], HB, wful[:], op=OP.mult),
              waits=[sAC.now()], inc=sDV)
            g5["hw_dv"] = sDV.c
            hw_rdy[qs] = sDV.now()

        gq = {q: gate_gen(q) for q in range(4)}

        def gstep(q):
            g6 = gq.get(q)
            if g6 is not None:
                try:
                    next(g6)
                except StopIteration:
                    gq.pop(q, None)

        def gdrain(q):
            while q in gq:
                gstep(q)

        for half in (0, 1):
            ots = range(NCH) if half == 0 else range(NCH - 1, -1, -1)
            for k3, ot in enumerate(ots):
                par = ot % 2
                if half == 1 and k3 < 2:
                    wl = None
                else:
                    wl = gpq.dma(lambda e, ot=ot, par=par: e.dma_start(
                        out=wcol[par][:, 0:D], in_=wo.ap()[ot]),
                        waits=[(sPE, wcol_free[par])])
                for qs in (2 * half, 2 * half + 1):
                    for c in range(NCH):
                        E("tensor", lambda e, c=c, qs=qs, par=par: e.matmul(
                            psA[:, qs * 512:qs * 512 + 512],
                            wcol[par][:, c * 128:(c + 1) * 128],
                            XN[:, c * 2048 + qs * 512: c * 2048 + qs * 512 + 512],
                            start=(c == 0), stop=(c == NCH - 1)),
                          waits=((([wl] if wl else []) + (acat_ld if half == 0 and ot == 0 and qs == 0 else []))
                                 if c == 0 and qs == 2 * half else [])
                                + ([(sAC, psa_evac[qs])] if c == 0 and k3 >= 1 else []))
                    if len(pend3) >= 2:
                        pend3.pop(0)()
                    E("tensor", None, inc=sPE)
                    mmw = sPE.now()
                    if qs == 2 * half + 1:
                        wcol_free[par] = sPE.c
                    ei = qs % 2
                    E("scalar", lambda e, ei=ei, qs=qs: e.copy(
                        estage[ei][:], psA[:, qs * 512:qs * 512 + 512]),
                      waits=[mmw] + (est_free[ei] if est_free[ei] else []),
                      inc=sAC)
                    psa_evac[qs] = sAC.c
                    evt = sAC.now()
                    tok = syq.dma(lambda e, ot=ot, qs=qs, ei=ei: e.dma_start(
                        out=s_at.ap()[ot, :, qs * 512:qs * 512 + 512], in_=estage[ei][:]),
                        waits=[evt])
                    sq = sqr[ei]
                    E("scalar", lambda e, ei=ei, q=sq: e.activation(q[:], estage[ei][:], AF.Square),
                      waits=sq_g[ei], inc=sAC)
                    sqw = sAC.now()

                    def _stats3(ei=ei, qs=qs, kk=k3, evt=evt, sqw=sqw, sq=sq, tok=tok):
                        E("tensor", lambda e, ei=ei, qs=qs: e.matmul(
                            psB[32 * qs:32 * qs + 32, :], ones_sb[:, 0:32], estage[ei][:],
                            start=False, stop=False, skip_group_check=True,
                            tile_position=(0, 32 * qs)),
                          waits=[evt] + ([b3ms] if kk == 0 else []))
                        E("tensor", lambda e, q=sq, qs=qs: e.matmul(
                            psC[32 * qs:32 * qs + 32, :], ones_sb[:, 0:32], q[:],
                            start=False, stop=False, skip_group_check=True,
                            tile_position=(0, 32 * qs)),
                          waits=[sqw], inc=sPE)
                        sq_g[ei] = [sPE.now()]
                        est_free[ei] = [tok, sPE.now()]
                    pend3.append(_stats3)
                    at_written[ot][qs] = tok
                if half == 1:
                    b4_sched3(k3)
            while pend3:
                pend3.pop(0)()
            b3_half_pe[half] = sPE.now()


        gdrain(0)
        gdrain(1)

        ms5 = [None] * NQS

        def b5_memset(qs):
            E("vector", lambda e, qs=qs: e.memset(psB[32 * qs:32 * qs + 32, :], 0.0),
              waits=[(sAC, fin2_read[qs])])
            E("vector", lambda e, qs=qs: e.memset(psC[32 * qs:32 * qs + 32, :], 0.0),
              inc=sDV)
            ms5[qs] = sDV.now()

        b5_memset(0)
        b5_memset(1)

        def b4_sched5(ot):
            if ot == 0:
                b4_partA(2)
            elif ot in (4, 6, 8, 10):
                b4_batch(2, 8 * (ot - 4) // 2)
            elif ot == 11:
                b4_partB(2)
                b5_memset(2)
                b4_partA(3)
            elif ot in (14, 16, 18, 20):
                b4_batch(3, 8 * (ot - 14) // 2)
            elif ot == 21:
                b4_partB(3)
                b5_memset(3)
            elif 22 <= ot <= 26:
                gstep(2)
                gstep(2)
            elif 27 <= ot <= 31:
                gstep(3)
                gstep(3)

        wcol_free = [sPE.c, sPE.c]
        psb_evac5 = [sAC.c, sAC.c]
        ff_written = [[None] * NQS for _ in range(NCH)]
        pend5 = []
        b5_half_pe = [None, None]
        for half in (0, 1):
            if half == 1:
                gdrain(2)
                gdrain(3)
            ots = range(NCH) if half == 0 else range(NCH - 1, -1, -1)
            for k5, ot in enumerate(ots):
                par = ot % 2
                if half == 1 and k5 < 2:
                    wl = None
                else:
                    wl = gpq.dma(lambda e, ot=ot, par=par: e.dma_start(
                        out=wcol[par][:], in_=wb.ap()[ot]),
                        waits=[(sPE, wcol_free[par])])
                for qs in (2 * half, 2 * half + 1):
                    bank = psA[:, 1024:1536] if qs % 2 == 0 else psA[:, 1536:2048]
                    for c in range(NCH):
                        E("tensor", lambda e, c=c, qs=qs, par=par, bank=bank: e.matmul(
                            bank, wcol[par][:, c * 128:(c + 1) * 128],
                            XN[:, c * 2048 + qs * 512: c * 2048 + qs * 512 + 512],
                            start=(c == 0), stop=False),
                          waits=((([wl] if wl else []) if c == 0 and qs == 2 * half else [])
                                 + (([y1n_q[qs]]
                                     + ([(sAC, psa_evac[2 + qs % 2])] if half == 0 else []))
                                    if c == 0 and k5 == 0 else [])
                                + ([(sAC, psb_evac5[qs % 2])] if c == 0 and (k5 > 0 or half == 1) else [])))
                    E("tensor", lambda e, qs=qs, par=par, bank=bank: e.matmul(
                        bank, wcol[par][:, D:D + 128],
                        hw_all[:, qs * 512:qs * 512 + 512],
                        start=False, stop=True),
                      waits=[hw_rdy[qs]] if k5 == 0 else [])
                    if len(pend5) >= 2:
                        pend5.pop(0)()
                    E("tensor", None, inc=sPE)
                    mmw = sPE.now()
                    if qs == 2 * half + 1:
                        wcol_free[par] = sPE.c
                    ei = qs % 2
                    E("scalar", lambda e, ei=ei, bank=bank, ot=ot: e.add(
                        estage[ei][:], bank, bbase_sb[:, ot:ot + 1]),
                      waits=[mmw] + (est_free[ei] if est_free[ei] else []),
                      inc=sAC)
                    psb_evac5[qs % 2] = sAC.c
                    evt = sAC.now()
                    tok = syq.dma(lambda e, ot=ot, qs=qs, ei=ei: e.dma_start(
                        out=s_ff.ap()[ot, :, qs * 512:qs * 512 + 512], in_=estage[ei][:]),
                        waits=[evt])
                    sq = sqr[ei]
                    E("scalar", lambda e, ei=ei, q=sq: e.activation(q[:], estage[ei][:], AF.Square),
                      waits=sq_g[ei], inc=sAC)
                    sqw = sAC.now()

                    def _stats5(ei=ei, qs=qs, kk=k5, evt=evt, sqw=sqw, sq=sq, tok=tok):
                        E("tensor", lambda e, ei=ei, qs=qs: e.matmul(
                            psB[32 * qs:32 * qs + 32, :], ones_sb[:, 0:32], estage[ei][:],
                            start=False, stop=False, skip_group_check=True,
                            tile_position=(0, 32 * qs)),
                          waits=[evt] + ([ms5[qs]] if kk == 0 else []))
                        E("tensor", lambda e, q=sq, qs=qs: e.matmul(
                            psC[32 * qs:32 * qs + 32, :], ones_sb[:, 0:32], q[:],
                            start=False, stop=False, skip_group_check=True,
                            tile_position=(0, 32 * qs)),
                          waits=[sqw], inc=sPE)
                        sq_g[ei] = [sPE.now()]
                        est_free[ei] = [tok, sPE.now()]
                    pend5.append(_stats5)
                    ff_written[ot][qs] = tok
                if half == 0:
                    b4_sched5(k5)
            while pend5:
                pend5.pop(0)()
            b5_half_pe[half] = sPE.now()
        b5_pe = b5_half_pe[1]

        vt_f = Vt.ap().bitcast(F32)
        ysR = [kr_f[:, 0:512], kr_f[:, 512:1024], vt_f[:, 0:512], vt_f[:, 512:1024],
               cos_f[:, 0:512], cos_f[:, 512:1024], ss_f[:, 0:512], ss_f[:, 512:1024]]
        ysR_g = [[] for _ in range(8)]
        arR6 = ([P_sb[0][:, i * 512:(i + 1) * 512] for i in range(4)]
                + [acat_sb[:, i * 512:(i + 1) * 512] for i in range(4)])
        arR6_g = [[hw_rdy[3]] for _ in range(4)] + [[acat_w] for _ in range(4)]
        xrR6 = xrR + [obuf[1][:]]
        xrR_g6 = [list(xrR_g[i]) for i in range(4)] + [list(sqb_g[1])]
        for qs in range(NQS):
            qo = qs * 512
            hf = qs // 2
            stw = stats_finalize32(rsA, nmA, psB[32 * qs:32 * qs + 32, :],
                                   psC[32 * qs:32 * qs + 32, :], [b5_half_pe[hf]])
            aws, xws = {}, {}

            def _lda6(c, qs=qs, qo=qo, hf=hf):
                slot = c % 8
                aws[c] = syq.dma(lambda e, c=c, slot=slot, qo=qo: e.dma_start(
                    out=arR6[slot], in_=s_ff.ap()[c, :, qo:qo + 512]),
                    waits=[ff_written[c][qs], b5_half_pe[hf]] + arR6_g[slot])
                arR6_g[slot] = []

            def _ldy6(c, qs=qs, qo=qo):
                xslot = c % 5
                xws[c] = gpq.dma(lambda e, c=c, xslot=xslot, qo=qo: e.dma_start(
                    out=xrR6[xslot], in_=s_y1.ap()[c, :, qo:qo + 512]),
                    waits=list(y1_written[qs]) + xrR_g6[xslot])
                xrR_g6[xslot] = []

            for c in range(8):
                _lda6(c)
            for c in range(5):
                _ldy6(c)
            for c in range(NCH):
                slot = c % 8
                xslot = c % 5
                aw = aws[c]
                yw = xws[c]
                ypar = c % 8
                E("vector", lambda e, slot=slot: e.tensor_tensor(
                    obuf[0][:], arR6[slot], rsA[:], op=OP.mult), waits=[aw, stw])
                E("vector", lambda e: e.tensor_tensor(obuf[0][:], obuf[0][:], nmA[:], op=OP.add))
                E("vector", lambda e, xslot=xslot, ypar=ypar: e.tensor_tensor(
                    ysR[ypar], obuf[0][:], xrR6[xslot], op=OP.add),
                  waits=[yw] + ysR_g[ypar], inc=sDV)
                arR6_g[slot].append(sDV.now())
                xrR_g6[xslot].append(sDV.now())
                if c + 8 < NCH:
                    _lda6(c + 8)
                if c + 5 < NCH:
                    _ldy6(c + 5)
                ysR_g[ypar] = [syq.dma(lambda e, c=c, qo=qo, ypar=ypar: e.dma_start(
                    out=y2.ap()[c, :, qo:qo + 512], in_=ysR[ypar]),
                    waits=[sDV.now()])]
        E("sync", None, waits=syq.all())

        with nc.Block() as block:
            @block.sync
            def _(e):
                p.replay("sync", e)

            @block.gpsimd
            def _(e):
                p.replay("gpsimd", e)

            @block.tensor
            def _(e):
                p.replay("tensor", e)

            @block.scalar
            def _(e):
                p.replay("scalar", e)

            @block.vector
            def _(e):
                p.replay("vector", e)

    return nc


def _rope_tables():
    inv_freq = 1.0 / 10000.0 ** (np.arange(0, DH, 2, dtype=np.float32) / DH)
    t = np.arange(S, dtype=np.float32)
    f = np.outer(t, inv_freq)
    cos = np.zeros((S, DH), dtype=np.float32)
    sin = np.zeros((S, DH), dtype=np.float32)
    cos[:, 0::2] = np.cos(f)
    cos[:, 1::2] = np.cos(f)
    sin[:, 0::2] = np.sin(f)
    sin[:, 1::2] = np.sin(f)
    return cos, sin


def host_inputs(L0_attn_x, L0_attn_Wq, L0_attn_Wk, L0_attn_Wv, L0_attn_Wo,
                L0_ffn_W_base, L0_ffn_b_base, L0_ffn_W_gate, L0_ffn_W_thr,
                L0_ffn_b_thr, L0_ffn_A_w, L0_ffn_B_w):
    x = np.asarray(L0_attn_x, dtype=np.float32)
    Wq = np.asarray(L0_attn_Wq, dtype=np.float32)
    Wk = np.asarray(L0_attn_Wk, dtype=np.float32)
    Wv = np.asarray(L0_attn_Wv, dtype=np.float32)
    Wo = np.asarray(L0_attn_Wo, dtype=np.float32)
    Wb = np.asarray(L0_ffn_W_base, dtype=np.float32)
    bb = np.asarray(L0_ffn_b_base, dtype=np.float32)
    Wg = np.asarray(L0_ffn_W_gate, dtype=np.float32)
    Wt = np.asarray(L0_ffn_W_thr, dtype=np.float32)
    bt = np.asarray(L0_ffn_b_thr, dtype=np.float32)
    Aw = np.asarray(L0_ffn_A_w, dtype=np.float32)
    Bw = np.asarray(L0_ffn_B_w, dtype=np.float32)

    def slab(W):
        return np.ascontiguousarray(W.reshape(32, 128, 32, 128).transpose(0, 3, 2, 1)
                                    .reshape(32, 128, D))

    wq_s = slab(Wq)
    wo_s = slab(Wo)
    wb_s = slab(Wb)
    B2 = Bw.transpose(0, 2, 1).reshape(128, D)
    b2col = np.ascontiguousarray(B2.reshape(128, 32, 128).transpose(1, 0, 2))
    wb_s = np.ascontiguousarray(np.concatenate([wb_s, b2col], axis=2))

    def kvlay(W):
        return np.ascontiguousarray(
            W.T.reshape(32, 128, 128).transpose(1, 0, 2).reshape(128, D))
    wk_l = kvlay(Wk)
    wv_l = kvlay(Wv)

    Wg33 = np.zeros((33, D), np.float32)
    Wg33[0:8] = Wg
    Wg33[32] = Wt
    wg9_l = np.ascontiguousarray(
        Wg33.T.reshape(32, 128, 33).transpose(1, 0, 2).reshape(128, 32 * 33))
    Acat = Aw.reshape(128, D)
    acat_l = np.ascontiguousarray(
        Acat.T.reshape(32, 128, 128).transpose(1, 0, 2).reshape(128, D))
    bbase_l = np.ascontiguousarray(bb.reshape(32, 128).T)
    bthr_l = np.ascontiguousarray(bt.reshape(1, 1))

    ones_c = np.ones((128, 128), dtype=BF)
    e16_c = np.zeros((8, 128), dtype=np.float32)
    for e in range(8):
        e16_c[e, e * 16:(e + 1) * 16] = 2.0
    e16_c = e16_c.astype(BF)
    cosT, sinT = _rope_tables()

    al = np.linspace(0.0, 1.0, 256, dtype=np.float32)
    alC_c = np.ascontiguousarray(np.broadcast_to(al, (128, 256))).astype(BF)
    alD_c = np.ascontiguousarray(np.broadcast_to(1.0 - al, (128, 256))).astype(BF)
    al1 = np.ones((128, 256), dtype=BF)
    al0 = np.zeros((128, 256), dtype=BF)

    in_maps = []
    for core in range(NCORE):
        seq = core // 4
        c0 = (core % 4) * TOK
        lo, hi = c0 - 256, c0 + TOK + 256
        lo_c, hi_c = max(lo, 0), min(hi, S)
        xs = x[seq, lo_c:hi_c]
        xs = np.pad(xs, ((lo_c - lo, hi - hi_c), (0, 0)))
        xin_l = np.ascontiguousarray(xs.T.reshape(32, 128, HALO))
        pos = np.clip(np.arange(lo, hi), 0, S - 1)
        cos_l = np.ascontiguousarray(cosT[pos].T).astype(BF)
        ssv = sinT[pos].T.copy()
        ssv[0:64] *= -1.0
        ss_l = np.ascontiguousarray(ssv).astype(BF)
        maskv = np.zeros(HALO, dtype=np.float32)
        if lo < 0:
            maskv[: -lo] = MASKVAL
        if hi > S:
            maskv[-(hi - S):] = MASKVAL
        mask_l = np.ascontiguousarray(
            np.broadcast_to(maskv.reshape(20, 128).T, (128, 20)))
        first = (core % 4 == 0)
        in_maps.append(dict(
            xin=xin_l, wq=wq_s, wo=wo_s, wb=wb_s, wk=wk_l, wv=wv_l,
            wg9=wg9_l, acat=acat_l, bbase=bbase_l, bthr=bthr_l,
            ones_in=ones_c, e16_in=e16_c, cos_in=cos_l, ss_in=ss_l,
            alA_in=(al1 if first else alC_c), alB_in=(al0 if first else alD_c),
            alC_in=alC_c, alD_in=alD_c, mask_in=mask_l))
    return in_maps


def kernel(**inputs):
    nc = build_nc()
    in_maps = host_inputs(**inputs)
    res = run_bass_kernel_spmd(nc, in_maps, core_ids=list(range(NCORE)))
    out = np.empty((Bb, S, D), dtype=np.float32)
    for core in range(NCORE):
        y = np.asarray(res.results[core]["y2"])
        seq = core // 4
        c0 = (core % 4) * TOK
        out[seq, c0:c0 + TOK] = y.reshape(D, TOK).T
    return out
